# version 10
# baseline (speedup 1.0000x reference)
import numpy as np
from contextlib import ExitStack

import concourse.bass as bass
import concourse.mybir as mybir
from concourse.bass_utils import run_bass_kernel_spmd

F32 = mybir.dt.float32
BF16 = mybir.dt.bfloat16
I8 = mybir.dt.int8
AF = mybir.ActivationFunctionType
ALU = mybir.AluOpType
AX = mybir.AxisListType

NCORES = 8
D = 1024
NH = 16
DH = 64
NMAIN = 22
NPREF = 13
ROWS = NMAIN * 128
PREF_ROWS = NPREF * 128
NFLG = NPREF + NMAIN


class Plan:
    ENGINES = ("sync", "tensor", "vector", "scalar", "gpsimd")

    def __init__(self):
        self.ops = {e: [] for e in self.ENGINES}
        self.count = {}
        self.events = {}
        self.sem_owner = {}

    def emit(self, eng, fn, wait=(), inc=None, event=None, inc_by=1):
        wmax = {}
        for ev_name in wait:
            if ev_name is None:
                continue
            sem, cnt = self.events[ev_name]
            wmax[sem] = max(wmax.get(sem, 0), cnt)
        waits = sorted(wmax.items())
        self.ops[eng].append((fn, waits, inc, inc_by))
        if inc is not None:
            owner = self.sem_owner.setdefault(inc, eng)
            assert owner == eng, f"sem {inc} inc'd from {eng} and {owner}"
            self.count[inc] = self.count.get(inc, 0) + inc_by
            if event is not None:
                self.events[event] = (inc, self.count[inc])
        else:
            assert event is None

    def finish_dma(self, eng="sync"):
        waits = []
        for sem, total in sorted(self.count.items()):
            owner = self.sem_owner.get(sem)
            if owner in ("sync", "gpsimd", "scalar") and total % 16 == 0 and total > 0:
                ename = f"_fin_{sem}"
                self.events[ename] = (sem, total)
                waits.append(ename)
        self.emit(eng, lambda e: None, wait=tuple(waits))

    def emit_into(self, block, sems):
        def mk_body(oplist):
            def body(eng):
                last_wait = {}
                for fn, waits, inc, inc_by in oplist:
                    for sem_name, cnt in waits:
                        if last_wait.get(sem_name, -1) >= cnt:
                            continue
                        last_wait[sem_name] = cnt
                        eng.wait_ge(sems[sem_name], cnt)
                    inst = fn(eng)
                    if inc is not None:
                        inst.then_inc(sems[inc], inc_by)
            return body

        for eng_name in self.ENGINES:
            if self.ops[eng_name]:
                getattr(block, eng_name)(mk_body(self.ops[eng_name]))

    def barrier(self, tag, exclude=("mq",)):
        names = []
        for sem, total in sorted(self.count.items()):
            if sem in exclude:
                continue
            en = f"__bar_{tag}_{sem}"
            self.events[en] = (sem, total)
            names.append(en)
        engines_used = [e for e in self.ENGINES if self.ops[e]]
        for eng in self.ENGINES:
            self.emit(eng, lambda e: None, wait=tuple(names))


class PhasePlan:

    def __init__(self, plan, pfx):
        self.plan = plan
        self.pfx = pfx

    @property
    def count(self):
        return self.plan.count

    @property
    def events(self):
        return _PfxEvents(self.plan.events, self.pfx)

    def emit(self, eng, fn, wait=(), inc=None, event=None, inc_by=1):
        w = tuple(None if x is None else self.pfx + x for x in wait)
        e = None if event is None else self.pfx + event
        self.plan.emit(eng, fn, wait=w, inc=inc, event=e, inc_by=inc_by)

    def finish_dma(self, eng="sync"):
        self.plan.finish_dma(eng)


class _PfxEvents:
    def __init__(self, events, pfx):
        self._e = events
        self._p = pfx

    def __getitem__(self, k):
        return self._e[self._p + k]

    def __setitem__(self, k, v):
        self._e[self._p + k] = v

    def __contains__(self, k):
        return (self._p + k) in self._e


def emit_A(nc, P, dram, stage_kind, dbg_chunk=None, bg=None):
    X = dram("x_in", [ROWS, D], "ExternalInput")
    XP = dram("x_prefix", [PREF_ROWS, D], "ExternalInput")
    WT = {k: dram(f"w{k}T", [D, D], "ExternalInput") for k in "qkvo"}
    BR = {k: dram(f"b{k}", [1, D], "ExternalInput") for k in "qkvo"}
    ONES = dram("ones_row", [1, 128], "ExternalInput")
    EYE = dram("eye128", [128, 128], "ExternalInput")
    CMASK = dram("causal_mask", [128, 128], "ExternalInput")
    FLG = dram("state_flags", [128, NFLG], "ExternalInput")
    XLN = dram("xln", [ROWS, D], stage_kind)
    DBG = {}
    if dbg_chunk is not None:
        for nm in ("d_xf", "d_r", "d_S32", "d_ln"):
            DBG[nm] = dram(nm, [128, D], "ExternalOutput")
        DBG["d_stats"] = dram("d_stats", [128, 8], "ExternalOutput")

    ctx = ExitStack()
    sb = lambda name, shape, dt: ctx.enter_context(nc.sbuf_tensor(name + '_pA', shape, dt))
    ps = lambda name, shape: ctx.enter_context(nc.psum_tensor(name + '_pA', shape, F32))

    w16 = {k: sb(f"w16_{k}", [128, 8 * D], BF16) for k in "qkvo"}
    brow = {k: sb(f"brow_{k}", [1, D], F32) for k in "qkvo"}
    ones = sb("ones", [1, 128], F32)
    eye = sb("eye", [128, 128], BF16)
    cmask = sb("cmask", [128, 128], F32)
    flags = sb("flags", [128, NFLG], F32)
    xf = [sb(f"xf{i}", [128, D], F32) for i in range(2)]
    x16 = [sb(f"x16_{i}", [128, D], BF16) for i in range(2)]
    xfm = [sb(f"xfm{i}", [128, D], BF16) for i in range(2)]
    q_tm = [sb(f"q_tm{i}", [128, D], BF16) for i in range(2)]
    k_tm = [sb(f"k_tm{i}", [128, D], BF16) for i in range(2)]
    v_tm = [sb(f"v_tm{i}", [128, D], BF16) for i in range(2)]
    qfm = [sb(f"qfm{i}", [128, D], BF16) for i in range(2)]
    kfm = [sb(f"kfm{i}", [128, D], BF16) for i in range(2)]
    scm = [sb(f"scm{i}", [128, 128], BF16) for i in range(2)]
    att16 = [sb(f"att16{i}", [128, D], BF16) for i in range(2)]
    attfm = [sb(f"attfm{i}", [128, D], BF16) for i in range(2)]
    S32 = sb("S32", [128, 512], F32)
    S16 = sb("S16", [128, 512], BF16)
    xln_sb = [sb(f"xlnsb{i}", [128, D], F32) for i in range(2)]
    junk = sb("junk", [128, D], F32)
    sums = sb("sums", [128, 1], F32)
    mean = sb("mean", [128, 1], F32)
    sumsq = sb("sumsq", [128, 1], F32)
    var = sb("var", [128, 1], F32)
    rstd = sb("rstd", [128, 1], F32)
    mb = sb("mb", [128, 1], F32)

    p_tr = [ps(f"p_tr{i}", [128, 512]) for i in range(2)]
    p_proj = [ps(f"p_proj{i}", [128, 512]) for i in range(2)]
    p_sc = [ps(f"p_sc{i}", [128, 512]) for i in range(2)]
    p_att = ps("p_att", [128, 512])
    p_st = ps("p_st", [64, 512])

    ev = {}

    for k in "qkvo":
        P.emit("gpsimd",
               lambda g, k=k: g.dma_start(
                   out=w16[k][:].rearrange("p (j e) -> p j e", j=8),
                   in_=WT[k].rearrange("(j p) e -> p j e", p=128)),
               inc="wl", inc_by=16, event=f"w_{k}")
    P.emit("gpsimd", lambda g: g.dma_start(out=eye[:], in_=EYE[:, :]),
           inc="wl", inc_by=16, event="eye")
    P.emit("sync", lambda s: s.dma_start(out=cmask[:], in_=CMASK[:, :]),
           inc="dx", inc_by=16, event="cmask")
    for k in "qkvo":
        P.emit("sync", lambda s, k=k: s.dma_start(out=brow[k][:], in_=BR[k][:, :]),
               inc="dx", inc_by=16, event=f"b_{k}")
    P.emit("sync", lambda s: s.dma_start(out=ones[:], in_=ONES[:, :]),
           inc="dx", inc_by=16, event="ones")
    P.emit("sync", lambda s: s.dma_start(out=flags[:], in_=FLG[:, :]),
           inc="dx", inc_by=16, event="flags")
    P.events["wsetup"] = ("wl", P.count.get("wl", 0))
    P.events["dsetup"] = ("dx", P.count.get("dx", 0))
    P.emit("vector", lambda v: v.memset(S32[:], 0.0), inc="ve", event="S32_z")
    P.emit("vector", lambda v: v.memset(S16[:], 0.0), inc="ve", event="S16_ref")

    def transpose8(tag, src16, src_ev, dst, dst_free_ev=None, n=8):
        nb = (n + 3) // 4
        last_evac = None
        for b in range(nb):
            lo = 4 * b
            hi = min(lo + 4, n)
            for jj, j in enumerate(range(lo, hi)):
                P.emit("tensor",
                       lambda t, j=j, jj=jj, b=b, src16=src16:
                       t.matmul(out=p_tr[b % 2][:, 128 * jj:128 * jj + 128],
                                lhsT=src16[:, 128 * j:128 * j + 128],
                                rhs=eye[:], start=True, stop=True),
                       wait=(src_ev, "wsetup", ev.get(f"ptr_free{b % 2}")),
                       inc="pe" if j == hi - 1 else None,
                       event=f"{tag}_tr{b}" if j == hi - 1 else None)
            P.emit("vector",
                   lambda v, b=b, lo=lo, hi=hi, dst=dst:
                   v.tensor_copy(out=dst[:, 128 * lo:128 * hi],
                                 in_=p_tr[b % 2][:, 0:128 * (hi - lo)]),
                   wait=(f"{tag}_tr{b}", dst_free_ev),
                   inc="ve", event=f"{tag}_ev{b}")
            ev[f"ptr_free{b % 2}"] = f"{tag}_ev{b}"
            last_evac = f"{tag}_ev{b}"
        P.events[f"{tag}_done"] = P.events[last_evac]

    def proj(tag, w_key, src_fm, src_ev, dst_tm, dst_free_ev=None, bias=True,
             evac=True):
        for half in range(2):
            if bias:
                P.emit("tensor",
                       lambda t, half=half, w_key=w_key:
                       t.matmul(out=p_proj[half][:],
                                lhsT=ones[:],
                                rhs=brow[w_key][:, 512 * half:512 * half + 512],
                                start=True, stop=False, skip_group_check=True),
                       wait=(src_ev, "wsetup", "dsetup",
                             ev.get(f"pproj_free{half}")))
            for j in range(8):
                P.emit("tensor",
                       lambda t, j=j, half=half, src_fm=src_fm, w_key=w_key, bias=bias:
                       t.matmul(out=p_proj[half][:],
                                lhsT=src_fm[:, 128 * j:128 * j + 128],
                                rhs=w16[w_key][:, 1024 * j + 512 * half:
                                               1024 * j + 512 * half + 512],
                                start=(j == 0 and not bias), stop=(j == 7),
                                skip_group_check=bias),
                       wait=(src_ev, "wsetup", "dsetup",
                             ev.get(f"pproj_free{half}")),
                       inc=("pe" if j == 7 else None),
                       event=(f"{tag}_mm{half}" if j == 7 else None))
            if evac:
                P.emit("vector",
                       lambda v, half=half, dst_tm=dst_tm:
                       v.tensor_copy(out=dst_tm[:, 512 * half:512 * half + 512],
                                     in_=p_proj[half][:]),
                       wait=(f"{tag}_mm{half}", dst_free_ev),
                       inc="ve", event=f"{tag}_ev{half}")
                ev[f"pproj_free{half}"] = f"{tag}_ev{half}"
        if evac:
            P.events[f"{tag}_done"] = P.events[f"{tag}_ev1"]

    def state_half(idx, rnd, kv_ev, buf):
        h0 = 8 * rnd
        for i in range(8):
            h = h0 + i
            P.emit("tensor",
                   lambda t, h=h, i=i, buf=buf:
                   t.matmul(out=p_st[:, 64 * i:64 * i + 64],
                            lhsT=k_tm[buf][:, 64 * h:64 * h + 64],
                            rhs=v_tm[buf][:, 64 * h:64 * h + 64],
                            start=True, stop=True),
                   wait=(kv_ev, ev.get("pst_free")),
                   inc="pe" if i == 7 else None,
                   event=f"stmm_{idx}_{rnd}" if i == 7 else None)
        for i in range(8):
            h = h0 + i
            base = (h % 2) * 64
            col = (h // 2) * 64
            P.emit("vector",
                   lambda v, i=i, base=base, col=col, idx=idx:
                   v.scalar_tensor_tensor(out=S32[base:base + 64, col:col + 64],
                                          in0=p_st[:, 64 * i:64 * i + 64],
                                          scalar=flags[base:base + 64, idx:idx + 1],
                                          in1=S32[base:base + 64, col:col + 64],
                                          op0=ALU.mult, op1=ALU.add),
                   wait=(f"stmm_{idx}_{rnd}", "dsetup", "S32_z"),
                   inc="ve" if i == 7 else None,
                   event=f"stupd_{idx}_{rnd}" if i == 7 else None)
        ev["pst_free"] = f"stupd_{idx}_{rnd}"

    for p in range(NPREF):
        buf = p % 2
        P.emit("gpsimd",
               lambda g, p=p, buf=buf: g.dma_start(
                   out=x16[buf][:], in_=XP[128 * p:128 * p + 128, :]),
               wait=(ev.get(f"x16_free{buf}"),),
               inc=f"xg{buf}", inc_by=16, event=f"px16_{p}")
        transpose8(f"pxfm{p}", x16[buf], f"px16_{p}", xfm[buf],
                   dst_free_ev=ev.get(f"xfm_free{buf}"))
        ev[f"x16_free{buf}"] = f"pxfm{p}_tr1"
        proj(f"pk{p}", "k", xfm[buf], f"pxfm{p}_done", k_tm[buf],
             dst_free_ev=ev.get(f"ktm_free{buf}"))
        proj(f"pv{p}", "v", xfm[buf], f"pxfm{p}_done", v_tm[buf],
             dst_free_ev=ev.get(f"vtm_free{buf}"))
        ev[f"xfm_free{buf}"] = f"pv{p}_mm1"
        state_half(p, 0, f"pv{p}_done", buf)
        state_half(p, 1, f"pv{p}_done", buf)
        P.emit("vector", lambda v: v.tensor_copy(out=S16[:], in_=S32[:]),
               wait=(f"stupd_{p}_1",), inc="ve", event="S16_ref")
        ev[f"ktm_free{buf}"] = f"stmm_{p}_1"
        ev[f"vtm_free{buf}"] = f"stmm_{p}_1"

    for c in range(NMAIN):
        buf = c % 2
        fi = NPREF + c
        P.emit("sync",
               lambda s, c=c, buf=buf: s.dma_start(
                   out=xf[buf][:], in_=X[128 * c:128 * c + 128, :]),
               wait=(ev.get(f"xf_free{buf}"),),
               inc=f"xf{buf}", inc_by=16, event=f"xf_{c}")
        P.emit("gpsimd",
               lambda g, c=c, buf=buf: g.dma_start(
                   out=x16[buf][:], in_=X[128 * c:128 * c + 128, :]),
               wait=(ev.get(f"x16_free{buf}"),),
               inc=f"xg{buf}", inc_by=16, event=f"x16_{c}")
        if bg:
            for _ in range(2):
                if bg:
                    P.plan.emit("gpsimd", bg.pop(0), inc="mq", inc_by=16)
        transpose8(f"xfm{c}", x16[buf], f"x16_{c}", xfm[buf],
                   dst_free_ev=ev.get(f"xfm_free{buf}"))
        ev[f"x16_free{buf}"] = f"xfm{c}_tr1"
        proj(f"q{c}", "q", xfm[buf], f"xfm{c}_done", q_tm[buf],
             dst_free_ev=ev.get(f"qtm_free{buf}"))
        proj(f"k{c}", "k", xfm[buf], f"xfm{c}_done", k_tm[buf],
             dst_free_ev=ev.get(f"ktm_free{buf}"))
        proj(f"v{c}", "v", xfm[buf], f"xfm{c}_done", v_tm[buf],
             dst_free_ev=ev.get(f"vtm_free{buf}"))
        ev[f"xfm_free{buf}"] = f"v{c}_mm1"
        transpose8(f"qfm{c}", q_tm[buf], f"q{c}_done", qfm[buf],
                   dst_free_ev=ev.get(f"qfm_free{buf}"))
        transpose8(f"kfm{c}", k_tm[buf], f"k{c}_done", kfm[buf],
                   dst_free_ev=ev.get(f"kfm_free{buf}"))
        def emit_sc(h):
            base = (h % 2) * 64
            jb = (h // 2) * 128
            si = h % 2
            P.emit("tensor",
                   lambda t, base=base, jb=jb, si=si, buf=buf:
                   t.matmul(out=p_sc[si][:, 0:128],
                            lhsT=kfm[buf][base:base + 64, jb:jb + 128],
                            rhs=qfm[buf][base:base + 64, jb:jb + 128],
                            start=True, stop=True),
                   wait=(f"qfm{c}_done", f"kfm{c}_done",
                         ev.get(f"psc_free{si}")),
                   inc="pe", event=f"sc_{c}_{h}")
            P.emit("vector",
                   lambda v, si=si:
                   v.tensor_tensor(out=scm[si][:], in0=p_sc[si][:, 0:128],
                                   in1=cmask[:], op=ALU.mult),
                   wait=(f"sc_{c}_{h}", "dsetup", ev.get(f"scm_free{si}")),
                   inc="ve", event=f"scm_{c}_{h}")
            ev[f"psc_free{si}"] = f"scm_{c}_{h}"

        def emit_av(h):
            base = (h % 2) * 64
            si = h % 2
            i = h % 8
            P.emit("tensor",
                   lambda t, h=h, i=i, base=base, buf=buf:
                   t.matmul(out=p_att[:, 64 * i:64 * i + 64],
                            lhsT=qfm[buf][base:base + 64, (h // 2) * 128:(h // 2) * 128 + 128],
                            rhs=S16[base:base + 64, (h // 2) * 64:(h // 2) * 64 + 64],
                            start=True, stop=False),
                   wait=("S16_ref", f"qfm{c}_done", ev.get("patt_free")))
            P.emit("tensor",
                   lambda t, h=h, i=i, si=si, buf=buf:
                   t.matmul(out=p_att[:, 64 * i:64 * i + 64],
                            lhsT=scm[si][:],
                            rhs=v_tm[buf][:, 64 * h:64 * h + 64],
                            start=False, stop=True),
                   wait=(f"scm_{c}_{h}", f"v{c}_done"),
                   inc="pe", event=f"att_{c}_{h}")
            ev[f"scm_free{si}"] = f"att_{c}_{h}"

        for rnd in range(2):
            h0 = 8 * rnd
            emit_sc(h0)
            for i in range(8):
                h = h0 + i
                if i + 1 < 8:
                    emit_sc(h + 1)
                emit_av(h)
            P.emit("vector",
                   lambda v, rnd=rnd, buf=buf:
                   v.tensor_copy(out=att16[buf][:, 512 * rnd:512 * rnd + 512],
                                 in_=p_att[:, 0:512]),
                   wait=(f"att_{c}_{8 * rnd + 7}", ev.get(f"att16_free{buf}")),
                   inc="ve", event=f"att16_{c}_{rnd}")
            ev["patt_free"] = f"att16_{c}_{rnd}"
        state_half(fi, 0, f"v{c}_done", buf)
        state_half(fi, 1, f"v{c}_done", buf)
        P.emit("vector", lambda v: v.tensor_copy(out=S16[:], in_=S32[:]),
               wait=(f"stupd_{fi}_1", f"att_{c}_{NH - 1}",),
               inc="ve", event="S16_ref")
        ev[f"qtm_free{buf}"] = f"qfm{c}_tr1"
        ev[f"ktm_free{buf}"] = f"stmm_{fi}_1"
        ev[f"vtm_free{buf}"] = f"stmm_{fi}_1"
        ev[f"qfm_free{buf}"] = f"att_{c}_{NH - 1}"
        ev[f"kfm_free{buf}"] = f"sc_{c}_{NH - 1}"
        transpose8(f"afm{c}", att16[buf], f"att16_{c}_1", attfm[buf],
                   dst_free_ev=ev.get(f"attfm_free{buf}"))
        ev[f"att16_free{buf}"] = f"afm{c}_tr1"
        proj(f"o{c}", "o", attfm[buf], f"afm{c}_done", None, bias=True, evac=False)
        ev[f"attfm_free{buf}"] = f"o{c}_mm1"
        P.emit("vector",
               lambda v, buf=buf:
               v.scalar_tensor_tensor(out=xln_sb[buf][:, 0:512], in0=p_proj[0][:],
                                      scalar=1.0, in1=xf[buf][:, 0:512],
                                      op0=ALU.mult, op1=ALU.add,
                                      accum_out=sums[:]),
               wait=(f"o{c}_mm0", f"xf_{c}", ev.get(f"xlnsb_free{buf}")),
               inc="ve", event=f"r0_{c}")
        P.emit("vector",
               lambda v, buf=buf:
               v.scalar_tensor_tensor(out=xln_sb[buf][:, 512:1024], in0=p_proj[1][:],
                                      scalar=1.0, in1=xf[buf][:, 512:1024],
                                      op0=ALU.mult, op1=ALU.add,
                                      accum_out=mean[:]),
               wait=(f"o{c}_mm1",),
               inc="ve", event=f"r_{c}")
        ev[f"xf_free{buf}"] = f"r_{c}"
        ev["pproj_free0"] = f"r0_{c}"
        ev["pproj_free1"] = f"r_{c}"
        P.emit("vector",
               lambda v: v.tensor_add(out=sums[:], in0=sums[:], in1=mean[:]),
               wait=(f"r_{c}",), inc="ve", event=f"sums_{c}")
        P.emit("vector",
               lambda v: v.tensor_scalar_mul(out=mean[:], in0=sums[:], scalar1=1.0 / D),
               wait=(f"sums_{c}",), inc="ve", event=f"mean_{c}")
        P.emit("vector",
               lambda v: v.tensor_scalar_mul(out=mb[:], in0=mean[:], scalar1=-1.0),
               wait=(f"mean_{c}", ev.get("mb_free")), inc="ve", event=f"mb0_{c}")
        P.emit("scalar",
               lambda s, buf=buf: s.activation(out=junk[:], in_=xln_sb[buf][:],
                                               func=AF.Square, bias=mb[:], scale=1.0,
                                               accum_out=sumsq[:]),
               wait=(f"mb0_{c}", ev.get("sumsq_free")), inc="se", event=f"sq_{c}")
        P.emit("vector",
               lambda v: v.tensor_scalar(out=var[:], in0=sumsq[:],
                                         scalar1=1.0 / D, scalar2=1e-5,
                                         op0=ALU.mult, op1=ALU.add),
               wait=(f"sq_{c}",), inc="ve", event=f"var_{c}")
        ev["sumsq_free"] = f"var_{c}"
        P.emit("scalar",
               lambda s: s.activation(out=var[:], in_=var[:], func=AF.Sqrt),
               wait=(f"var_{c}",), inc="se", event=f"sd_{c}")
        P.emit("vector", lambda v: v.reciprocal(out=rstd[:], in_=var[:]),
               wait=(f"sd_{c}", ev.get("rstd_free")), inc="ve", event=f"rstd_{c}")
        P.emit("vector",
               lambda v: v.tensor_tensor(out=mb[:], in0=mb[:], in1=rstd[:], op=ALU.mult),
               wait=(f"rstd_{c}",), inc="ve", event=f"mb_{c}")
        P.emit("scalar",
               lambda s, buf=buf: s.activation(out=xln_sb[buf][:], in_=xln_sb[buf][:],
                                               func=AF.Identity, bias=mb[:],
                                               scale=rstd[:]),
               wait=(f"mb_{c}",), inc="se", event=f"xln_{c}")
        ev["mb_free"] = f"xln_{c}"
        ev["rstd_free"] = f"xln_{c}"
        P.emit("sync",
               lambda s, c=c, buf=buf: s.dma_start(out=XLN[128 * c:128 * c + 128, :],
                                                   in_=xln_sb[buf][:]),
               wait=(f"xln_{c}",), inc=f"st{buf}", inc_by=16, event=f"store_{c}")
        ev[f"xlnsb_free{buf}"] = f"store_{c}"

    ctx.close()


def _mk_dram(nc, stagekinds=()):
    cache = {}

    def dram(name, shape, kind):
        if name not in cache:
            dt = BF16 if (name == "w2q_scratch" or kind == "InternalB16") else F32
            k = "Internal" if kind == "InternalB16" else kind
            cache[name] = nc.dram_tensor(name, shape, dt, kind=k).ap()
        return cache[name]
    return dram


def _mk_sems(nc, ctx):
    cache = {}

    def get(names):
        out = {}
        for n in names:
            if n not in cache:
                cache[n] = ctx.enter_context(nc.semaphore(n))
            out[n] = cache[n]
        return cache
    return get


ALL_SEMS = ("dx", "pe", "ve", "se", "ge", "wl", "xg0", "xg1", "xf0", "xf1",
            "st0", "st1", "sdb", "ws0", "ws1", "xq0", "xq1", "xr0", "xr1",
            "so0", "so1", "w2s0", "w2s1", "w2l0", "w2l1", "xg", "ew0", "ew1",
            "mq")


def build_A(dbg_chunk=None):
    nc = bass.Bass("TRN2", target_bir_lowering=False, debug=False)
    dram = _mk_dram(nc)
    ctx = ExitStack()
    sems = _mk_sems(nc, ctx)(ALL_SEMS)
    P = Plan()
    emit_A(nc, PhasePlan(P, "A."), dram, "ExternalOutput", dbg_chunk=dbg_chunk)
    P.finish_dma()
    with nc.Block() as block:
        P.emit_into(block, sems)
    ctx.close()
    return nc


def host_prep_A(inputs):
    x = np.ascontiguousarray(np.asarray(inputs["L0_attn_x"], dtype=np.float32))
    wT = {k: np.ascontiguousarray(np.asarray(inputs[f"L0_attn_w{k}_w"], np.float32).T)
          for k in "qkvo"}
    b = {k: np.ascontiguousarray(np.asarray(inputs[f"L0_attn_w{k}_b"], np.float32)[None, :])
         for k in "qkvo"}
    ones = np.ones((1, 128), np.float32)
    eye = np.eye(128, dtype=np.float32)
    cm = np.triu(np.ones((128, 128), np.float32))
    in_maps = []
    for c in range(NCORES):
        seq, half = c // 2, c % 2
        xs = x[seq]
        if half == 0:
            pre = np.zeros((384, D), np.float32)
            own = xs[0:2048]
            post = xs[2048:2432]
            xp = np.zeros((PREF_ROWS, D), np.float32)
            sf = np.array([0.0] * NPREF + [0.0] * 3 + [1.0] * 19, np.float32)
        else:
            pre = xs[1664:2048]
            own = xs[2048:4096]
            post = np.zeros((384, D), np.float32)
            xp = np.ascontiguousarray(xs[0:PREF_ROWS])
            sf = np.array([1.0] * NPREF + [1.0] * 19 + [0.0] * 3, np.float32)
        x_in = np.ascontiguousarray(np.concatenate([pre, own, post], axis=0))
        flg = np.broadcast_to(sf[None, :], (128, NFLG)).copy()
        m = {"x_in": x_in, "x_prefix": xp,
             "wqT": wT["q"], "wkT": wT["k"], "wvT": wT["v"], "woT": wT["o"],
             "bq": b["q"], "bk": b["k"], "bv": b["v"], "bo": b["o"],
             "ones_row": ones, "eye128": eye, "causal_mask": cm,
             "state_flags": flg}
        in_maps.append(m)
    return in_maps



DH1 = 4096
NFC = 32
TT_B = 384
OUT_LO, OUT_HI = 128, 2688


def emit_B(nc, P, dram, in_kind, out_kind, dbg=False):
    XLN = dram("xln", [ROWS, D], in_kind)
    W1T = dram("w1T", [D, DH1], "ExternalInput")
    W2T = dram("w2T", [DH1, D], "ExternalInput")
    TAPS = dram("taps", [128, 96], "ExternalInput")
    CONVB = dram("convb", [128, NFC], "ExternalInput")
    CPAD = dram("cpad", [128, 2], "ExternalInput")
    ONES = dram("ones_row", [1, 128], "ExternalInput")
    ONESC = dram("ones_col", [128, 1], "ExternalInput")
    EYE = dram("eye128", [128, 128], "ExternalInput")
    X1 = dram("x1", [OUT_HI - OUT_LO, D], out_kind)
    W2Q = dram("w2q_scratch", [DH1, D], "Internal")
    DBG = {}

    ctx = ExitStack()
    sb = lambda name, shape, dt: ctx.enter_context(nc.sbuf_tensor(name + '_pB', shape, dt))
    ps = lambda name, shape: ctx.enter_context(nc.psum_tensor(name + '_pB', shape, F32))

    w1q = sb("w1q", [128, 8 * DH1], BF16)
    w2qb = [sb(f"w2qb{i}", [128, D], BF16) for i in range(2)]
    wstage = [sb(f"wstage{i}", [128, DH1], F32) for i in range(2)]
    wi8 = sb("wi8", [128, DH1], I8)
    xi8 = sb("xi8", [128, NMAIN * D], I8)
    dfac = sb("dfac", [128, NMAIN], F32)
    s127 = sb("s127", [128, NMAIN], F32)
    xlnb = [sb(f"xlnb{i}", [128, D], F32) for i in range(2)]
    xw = sb("xw", [128, D], BF16)
    xifm = sb("xifm", [128, 8 * (TT_B + 2)], BF16)
    slst = sb("slst", [128, NFC * TT_B], BF16)
    hsb = [sb(f"hsb{i}", [128, TT_B + 2], F32) for i in range(2)]
    cbuf = [sb(f"cbuf{i}", [128, TT_B], F32) for i in range(2)]
    absb = [sb(f"absb{i}", [128, TT_B], F32) for i in range(2)]
    runmx = sb("runmx", [128, TT_B], F32)
    tapst = sb("tapst", [128, 96], F32)
    convbt = sb("convbt", [128, NFC], F32)
    cpadt = sb("cpadt", [128, 2], F32)
    ones = sb("ones", [1, 128], F32)
    onesc = sb("onesc", [128, 1], F32)
    eye = sb("eye", [128, 128], BF16)
    eyef = sb("eyef", [128, 128], F32)
    macc = sb("macc", [128, 1], F32)
    mtmp = sb("mtmp", [128, 1], F32)
    stot = sb("stot", [1, 4], F32)
    wsc_bc = [sb(f"wscbc{i}", [128, 1], F32) for i in range(2)]
    dqc_bc = [sb(f"dqcbc{i}", [128, 1], F32) for i in range(2)]
    qtmp = sb("qtmp", [128, TT_B], F32)
    qi8 = sb("qi8", [128, TT_B], I8)
    xi2b = sb("xi2b", [128, TT_B], BF16)
    mcols = sb("mcols", [128, 3], F32)
    mrow1 = sb("mrow1", [1, TT_B], F32)
    srow1 = sb("srow1", [1, TT_B], F32)
    dqrow1 = sb("dqrow1", [1, TT_B], F32)
    sc_bc = sb("sc_bc", [128, TT_B], F32)
    dq_bc = sb("dq_bc", [128, TT_B], F32)
    xres = [sb(f"xres{i}", [128, D], F32) for i in range(2)]
    x1o = [sb(f"x1o{i}", [128, D], F32) for i in range(2)]
    junk = sb("junkB", [128, D], BF16)
    sums = sb("sumsB", [128, 1], F32)
    sums2 = sb("sums2B", [128, 1], F32)
    mean = sb("meanB", [128, 1], F32)
    sumsq = sb("sumsqB", [128, 1], F32)
    var = sb("varB", [128, 1], F32)
    rstd = sb("rstdB", [128, 1], F32)
    mb = sb("mbB", [128, 1], F32)
    xm = sb("xmB", [128, 1], F32)
    xr = sb("xrB", [128, 1], F32)

    p_h = [ps(f"p_h{i}", [128, 512]) for i in range(2)]
    p_w2 = [ps(f"p_w2_{i}", [128, D]) for i in range(3)]

    ev = {}

    for nm, dst, src in (("taps", tapst, TAPS), ("convb", convbt, CONVB),
                         ("cpad", cpadt, CPAD), ("ones", ones, ONES),
                         ("onesc", onesc, ONESC)):
        P.emit("sync", lambda s, dst=dst, src=src: s.dma_start(out=dst[:], in_=src[:, :]),
               inc="dx", inc_by=16, event=f"c_{nm}")
    P.events["csetup"] = ("dx", P.count["dx"])
    P.emit("gpsimd", lambda g: g.dma_start(out=eye[:], in_=EYE[:, :]),
           inc="wl", inc_by=16, event="eye")
    P.emit("gpsimd", lambda g: g.dma_start(out=eyef[:], in_=EYE[:, :]),
           inc="wl", inc_by=16, event="eyef")
    P.events["eyes"] = ("wl", P.count["wl"])

    def mean_pass(tag, W, ntiles, width, slot):
        P.emit("vector", lambda v: v.memset(macc[:], 0.0),
               wait=(ev.get("macc_user"),), inc="ve", event=f"{tag}_maccz")
        for j in range(ntiles):
            b = j % 2
            P.emit("sync", lambda s, j=j, b=b, W=W, width=width: s.dma_start(
                out=wstage[b][:, 0:width], in_=W[128 * j:128 * j + 128, :]),
                wait=(ev.get(f"wst_free{b}"),), inc=f"ws{b}", inc_by=16,
                event=f"{tag}l_{j}")
            P.emit("vector", lambda v, b=b, width=width: v.reduce_sum(
                out=mtmp[:], in_=wstage[b][:, 0:width], axis=AX.X,
                apply_absolute_value=True),
                wait=(f"{tag}l_{j}", f"{tag}_maccz", ev.get("mtmp_user2"),),
                inc="ve", event=f"{tag}r_{j}")
            P.emit("vector", lambda v: v.tensor_add(out=macc[:], in0=macc[:], in1=mtmp[:]),
                   wait=(f"{tag}r_{j}",), inc="ve", event=f"{tag}a_{j}")
            ev["mtmp_user2"] = f"{tag}a_{j}"
            ev[f"wst_free{b}"] = f"{tag}r_{j}"
        P.emit("tensor", lambda t: t.matmul(out=p_h[0][0:1, 0:1], lhsT=macc[:],
                                            rhs=onesc[:], start=True, stop=True),
               wait=(f"{tag}a_{ntiles-1}", "c_onesc", ev.get("ph0_free"),),
               inc="pe", event=f"{tag}tot_mm")
        P.emit("vector", lambda v, slot=slot: v.tensor_copy(
            out=stot[0:1, slot:slot + 1], in_=p_h[0][0:1, 0:1]),
            wait=(f"{tag}tot_mm",), inc="ve", event=f"{tag}tot")
        ev["ph0_free"] = f"{tag}tot"
        ev["macc_user"] = f"{tag}tot_mm"

    mean_pass("m1", W1T, 8, DH1, 0)
    mean_pass("m2", W2T, NFC, D, 1)

    for wi, nelem in ((0, D * DH1), (1, DH1 * D)):
        P.emit("vector",
               lambda v, wi=wi, nelem=nelem: v.tensor_scalar(
                   out=stot[0:1, 2:3], in0=stot[0:1, wi:wi + 1],
                   scalar1=1.0 / nelem, scalar2=1e-5, op0=ALU.mult, op1=ALU.max),
               wait=("m2tot", ev.get("stot_user"),), inc="ve", event=f"clipm_{wi}")
        P.emit("vector", lambda v: v.reciprocal(out=stot[0:1, 3:4], in_=stot[0:1, 2:3]),
               wait=(f"clipm_{wi}",), inc="ve", event=f"wscv_{wi}")
        P.emit("tensor", lambda t: t.matmul(out=p_h[0][:, 0:1], lhsT=ones[:],
                                            rhs=stot[0:1, 3:4], start=True, stop=True),
               wait=(f"wscv_{wi}", "c_ones", ev.get("ph0_free"),),
               inc="pe", event=f"wsc_mm{wi}")
        P.emit("vector", lambda v, wi=wi: v.tensor_copy(out=wsc_bc[wi][:],
                                                        in_=p_h[0][:, 0:1]),
               wait=(f"wsc_mm{wi}",), inc="ve", event=f"wscbc_{wi}")
        P.emit("vector", lambda v:
               v.tensor_scalar(out=stot[0:1, 3:4], in0=stot[0:1, 2:3],
                               scalar1=1.0 / 127.0, scalar2=None, op0=ALU.mult),
               wait=(f"wscbc_{wi}",), inc="ve", event=f"dqcv_{wi}")
        P.emit("tensor", lambda t: t.matmul(out=p_h[0][:, 0:1], lhsT=ones[:],
                                            rhs=stot[0:1, 3:4], start=True, stop=True),
               wait=(f"dqcv_{wi}",), inc="pe", event=f"dqc_mm{wi}")
        P.emit("vector", lambda v, wi=wi: v.tensor_copy(out=dqc_bc[wi][:],
                                                        in_=p_h[0][:, 0:1]),
               wait=(f"dqc_mm{wi}",), inc="ve", event=f"dqcbc_{wi}")
        ev["ph0_free"] = f"dqcbc_{wi}"
        ev["stot_user"] = f"dqc_mm{wi}"

    def quant_pass(tag, W, ntiles, width, wi, store_fn):
        for j in range(ntiles):
            b = j % 2
            P.emit("sync", lambda s, j=j, b=b, W=W, width=width: s.dma_start(
                out=wstage[b][:, 0:width], in_=W[128 * j:128 * j + 128, :]),
                wait=(ev.get(f"wst_free{b}"),), inc=f"ws{b}", inc_by=16,
                event=f"{tag}l_{j}")
            P.emit("vector", lambda v, b=b, wi=wi, width=width: v.tensor_scalar_mul(
                out=wstage[b][:, 0:width], in0=wstage[b][:, 0:width],
                scalar1=wsc_bc[wi][:]),
                wait=(f"{tag}l_{j}", f"wscbc_{wi}"), inc="ve", event=f"{tag}s_{j}")
            P.emit("vector", lambda v, b=b, width=width: v.tensor_scalar(
                out=wi8[:, 0:width], in0=wstage[b][:, 0:width],
                scalar1=12582912.0, scalar2=12582912.0,
                op0=ALU.add, op1=ALU.subtract),
                wait=(f"{tag}s_{j}", ev.get("wi8_user"),), inc="ve",
                event=f"{tag}i_{j}")
            store_fn(tag, j, b)
            ev[f"wst_free{b}"] = f"{tag}i_{j}"
            ev["wi8_user"] = f"{tag}q_{j}"

    def store_w1(tag, j, b):
        P.emit("vector", lambda v, j=j, tag=tag: v.tensor_scalar(
            out=w1q[:, DH1 * j:DH1 * j + DH1], in0=wi8[:, 0:DH1],
            scalar1=1.0, scalar2=-1.0, op0=ALU.min, op1=ALU.max),
            wait=(f"{tag}i_{j}",), inc="ve", event=f"{tag}q_{j}")

    def store_w2(tag, j, b):
        P.emit("vector", lambda v, b=b, tag=tag, j=j: v.tensor_scalar(
            out=w2qb[b][:], in0=wi8[:, 0:D],
            scalar1=1.0, scalar2=-1.0, op0=ALU.min, op1=ALU.max),
            wait=(f"{tag}i_{j}", ev.get(f"w2qb_user{b}"),), inc="ve",
            event=f"{tag}q_{j}")
        P.emit("sync", lambda s, j=j, b=b: s.dma_start(
            out=W2Q[128 * j:128 * j + 128, :], in_=w2qb[b][:]),
            wait=(f"{tag}q_{j}",), inc=f"w2s{b}", inc_by=16, event=f"{tag}st_{j}")
        ev[f"w2qb_user{b}"] = f"{tag}st_{j}"

    quant_pass("q1", W1T, 8, DH1, 0, store_w1)
    quant_pass("q2", W2T, NFC, D, 1, store_w2)

    for rt in range(NMAIN):
        b = rt % 2
        P.emit("sync", lambda s, rt=rt, b=b: s.dma_start(
            out=xlnb[b][:], in_=XLN[128 * rt:128 * rt + 128, :]),
            wait=(ev.get(f"xlnb_user{b}"),), inc=f"xq{b}", inc_by=16, event=f"xl_{rt}")
        P.emit("vector", lambda v, b=b: v.reduce_max(
            out=mtmp[:], in_=xlnb[b][:], axis=AX.X, apply_absolute_value=True),
            wait=(f"xl_{rt}", ev.get("mtmp_user"),), inc="ve", event=f"xm_{rt}")
        P.emit("vector", lambda v: v.tensor_scalar(
            out=xm[:], in0=mtmp[:], scalar1=1e-5, scalar2=None, op0=ALU.max),
            wait=(f"xm_{rt}", ev.get("xm_user"),), inc="ve", event=f"xc_{rt}")
        ev["mtmp_user"] = f"xc_{rt}"
        P.emit("vector", lambda v, rt=rt: v.tensor_scalar_mul(
            out=dfac[:, rt:rt + 1], in0=xm[:], scalar1=dqc_bc[0][:]),
            wait=(f"xc_{rt}", "dqcbc_0"), inc="ve", event=f"xd_{rt}")
        P.emit("vector", lambda v: v.reciprocal(out=xr[:], in_=xm[:]),
               wait=(f"xc_{rt}", ev.get("xr_user"),), inc="ve", event=f"xrr_{rt}")
        ev["xm_user"] = f"xrr_{rt}"
        P.emit("vector", lambda v, rt=rt: v.tensor_scalar_mul(
            out=s127[:, rt:rt + 1], in0=xr[:], scalar1=127.0),
            wait=(f"xrr_{rt}",), inc="ve", event=f"xs_{rt}")
        ev["xr_user"] = f"xs_{rt}"
        P.emit("vector", lambda v, rt=rt, b=b: v.tensor_scalar(
            out=xlnb[b][:], in0=xlnb[b][:], scalar1=s127[:, rt:rt + 1],
            scalar2=12582912.0, op0=ALU.mult, op1=ALU.add),
            wait=(f"xs_{rt}",), inc="ve", event=f"xp_{rt}")
        P.emit("vector", lambda v, rt=rt, b=b: v.tensor_single_scalar(
            out=xi8[:, D * rt:D * rt + D], in_=xlnb[b][:],
            scalar=12582912.0, op=ALU.subtract),
            wait=(f"xp_{rt}",), inc="ve", event=f"xi_{rt}")
        ev[f"xlnb_user{b}"] = f"xi_{rt}"

    P.emit("vector", lambda v: v.memset(xifm[:], 0.0), inc="ve", event="xifm_z")
    tts = []
    t0 = 0
    while t0 < ROWS:
        tts.append((t0, min(TT_B, ROWS - t0)))
        t0 += TT_B

    def b_stage1(ti, t0, W):
        nsub = (W + 127) // 128
        lo = max(t0 - 1, 0)
        hi = min(t0 + W + 1, ROWS)
        rt_lo, rt_hi = lo // 128, (hi - 1) // 128 + 1
        lo = max(t0 - 1, 0)
        hi = min(t0 + W + 1, ROWS)
        rt_lo, rt_hi = lo // 128, (hi - 1) // 128 + 1
        for rt in range(rt_lo, rt_hi):
            P.emit("vector", lambda v, rt=rt: v.tensor_scalar_mul(
                out=xw[:], in0=xi8[:, D * rt:D * rt + D], scalar1=dfac[:, rt:rt + 1]),
                wait=(f"xi_{rt}", f"xd_{rt}", ev.get("xw_user"),),
                inc="ve", event=f"dq_{ti}_{rt}")
            pad_ev = f"dq_{ti}_{rt}"
            for prow, pidx in ((383, 0), (2432, 1)):
                if rt * 128 <= prow < rt * 128 + 128:
                    pp = prow - rt * 128
                    pb = 64 if pp >= 64 else (32 if pp >= 32 else 0)
                    pb = (pp // 32) * 32
                    pb = pb if pb in (0, 32, 64) else 64
                    P.emit("vector", lambda v, pb=pb, pidx=pidx:
                           v.tensor_scalar_mul(
                               out=xw[pb:pb + 64, :], in0=xw[pb:pb + 64, :],
                               scalar1=cpadt[pb:pb + 64, pidx:pidx + 1]),
                           wait=(pad_ev, "csetup"), inc="ve",
                           event=f"pad_{ti}_{rt}_{pidx}")
                    pad_ev = f"pad_{ti}_{rt}_{pidx}"
            off = rt * 128 - (t0 - 1)
            cl = max(0, -off)
            ch = min(128, (W + 2) - off)
            for bb in range(2):
                for jj in range(4):
                    j = 4 * bb + jj
                    P.emit("tensor", lambda t, j=j, jj=jj, bb=bb:
                           t.matmul(out=p_h[bb][:, 128 * jj:128 * jj + 128],
                                    lhsT=xw[:, 128 * j:128 * j + 128],
                                    rhs=eye[:], start=True, stop=True),
                           wait=(pad_ev, "eyes", ev.get(f"ph_user{bb}"),),
                           inc="pe" if jj == 3 else None,
                           event=f"tr_{ti}_{rt}_{bb}" if jj == 3 else None)
                P.emit("vector", lambda v, bb=bb, off=off, cl=cl, ch=ch:
                       v.tensor_copy(
                           out=xifm[:].rearrange("p (j w) -> p j w", j=8)
                               [:, 4 * bb:4 * bb + 4, off + cl:off + ch],
                           in_=p_h[bb][:].rearrange("p (j w) -> p j w", j=4)
                               [:, :, cl:ch]),
                       wait=(f"tr_{ti}_{rt}_{bb}", "xifm_z", ev.get("xifm_user"),),
                       inc="ve", event=f"tre_{ti}_{rt}_{bb}")
                ev[f"ph_user{bb}"] = f"tre_{ti}_{rt}_{bb}"
            ev["xw_user"] = f"tr_{ti}_{rt}_1"
        P.events[f"xifm_{ti}"] = P.events[f"tre_{ti}_{rt_hi-1}_1"]

    def b_loop1_pre(ti, t0, W):
        nsub = (W + 127) // 128
        lo = max(t0 - 1, 0)
        hi = min(t0 + W + 1, ROWS)
        rt_lo, rt_hi = lo // 128, (hi - 1) // 128 + 1
        P.emit("vector", lambda v, W=W: v.memset(runmx[:, 0:W], 1e-5),
               wait=(ev.get("runmx_user"),), inc="ve", event=f"rmz_{ti}")
    def b_loop1_fc(ti, t0, W, fc):
        nsub = (W + 127) // 128
        lo = max(t0 - 1, 0)
        hi = min(t0 + W + 1, ROWS)
        rt_lo, rt_hi = lo // 128, (hi - 1) // 128 + 1
        b = fc % 2
        for j in range(8):
            P.emit("tensor", lambda t, j=j, fc=fc, b=b, W=W:
                   t.matmul(out=p_h[b][:, 0:W + 2],
                            lhsT=w1q[:, DH1 * j + 128 * fc:DH1 * j + 128 * fc + 128],
                            rhs=xifm[:, (TT_B + 2) * j:(TT_B + 2) * j + W + 2],
                            start=(j == 0), stop=(j == 7)),
                   wait=(f"xifm_{ti}", "q1q_7", "xifm_z", ev.get(f"ph_user{b}"),),
                   inc="pe" if j == 7 else None,
                   event=f"h_{ti}_{fc}" if j == 7 else None)
        P.emit("scalar", lambda s, b=b, W=W: s.activation(
            out=hsb[b][:, 0:W + 2], in_=p_h[b][:, 0:W + 2], func=AF.Identity),
            wait=(f"h_{ti}_{fc}", ev.get(f"hsb_user{b}"),),
            inc="se", event=f"he_{ti}_{fc}")
        ev[f"ph_user{b}"] = f"he_{ti}_{fc}"
        P.emit("vector", lambda v, b=b, fc=fc, W=W: v.tensor_scalar_mul(
            out=cbuf[b][:, 0:W], in0=hsb[b][:, 1:W + 1],
            scalar1=tapst[:, 3 * fc + 1:3 * fc + 2]),
            wait=(f"he_{ti}_{fc}", "csetup", ev.get(f"cbuf_user{b}"),),
            inc="ve", event=f"c1_{ti}_{fc}")
        P.emit("vector", lambda v, b=b, fc=fc, W=W: v.scalar_tensor_tensor(
            out=cbuf[b][:, 0:W], in0=hsb[b][:, 0:W],
            scalar=tapst[:, 3 * fc:3 * fc + 1], in1=cbuf[b][:, 0:W],
            op0=ALU.mult, op1=ALU.add),
            wait=(f"c1_{ti}_{fc}",), inc="ve", event=f"c2_{ti}_{fc}")
        P.emit("vector", lambda v, b=b, fc=fc, W=W: v.scalar_tensor_tensor(
            out=cbuf[b][:, 0:W], in0=hsb[b][:, 2:W + 2],
            scalar=tapst[:, 3 * fc + 2:3 * fc + 3], in1=cbuf[b][:, 0:W],
            op0=ALU.mult, op1=ALU.add),
            wait=(f"c2_{ti}_{fc}",), inc="ve", event=f"c3_{ti}_{fc}")
        ev[f"hsb_user{b}"] = f"c3_{ti}_{fc}"
        P.emit("scalar", lambda s, b=b, fc=fc, W=W: s.activation(
            out=slst[:, TT_B * fc:TT_B * fc + W], in_=cbuf[b][:, 0:W],
            func=AF.Silu, bias=convbt[:, fc:fc + 1], scale=1.0),
            wait=(f"c3_{ti}_{fc}", ev.get(f"slfree_{fc}"),),
            inc="se", event=f"sl_{ti}_{fc}")
        P.emit("scalar", lambda s, b=b, fc=fc, W=W: s.activation(
            out=absb[b][:, 0:W], in_=slst[:, TT_B * fc:TT_B * fc + W],
            func=AF.Abs),
            wait=(f"sl_{ti}_{fc}", ev.get(f"absb_user{b}"),),
            inc="se", event=f"ab_{ti}_{fc}")
        ev[f"cbuf_user{b}"] = f"ab_{ti}_{fc}"
        P.emit("vector", lambda v, b=b, W=W: v.tensor_tensor(
            out=runmx[:, 0:W], in0=runmx[:, 0:W], in1=absb[b][:, 0:W],
            op=ALU.max),
            wait=(f"ab_{ti}_{fc}", f"rmz_{ti}", ev.get("runmx_g"),),
            inc="ve", event=f"mx_{ti}_{fc}")
        ev["runmx_g"] = f"mx_{ti}_{fc}"
        ev[f"absb_user{b}"] = f"mx_{ti}_{fc}"
    def b_loop1_post(ti):
        ev["xifm_user"] = f"h_{ti}_{NFC-1}"
    def b_chain(ti, t0, W):
        nsub = (W + 127) // 128
        lo = max(t0 - 1, 0)
        hi = min(t0 + W + 1, ROWS)
        rt_lo, rt_hi = lo // 128, (hi - 1) // 128 + 1
        for s_ in range(nsub):
            sw = min(128, W - 128 * s_)
            P.emit("tensor", lambda t, s_=s_, sw=sw: t.transpose(
                out=p_w2[0][0:sw, 0:128], in_=runmx[:, 128 * s_:128 * s_ + sw],
                identity=eyef[:]),
                wait=(f"mx_{ti}_{NFC-1}", "eyes", ev.get("pw2s_user"),),
                inc="pe", event=f"mt_{ti}_{s_}")
            P.emit("vector", lambda v, s_=s_, sw=sw: v.reduce_max(
                out=mcols[:, s_:s_ + 1],
                in_=p_w2[0][:, 0:128], axis=AX.X),
                wait=(f"mt_{ti}_{s_}", ev.get("mcols_user"),),
                inc="ve", event=f"mc_{ti}_{s_}")
            ev["pw2s_user"] = f"mc_{ti}_{s_}"
        for s_ in range(nsub):
            P.emit("tensor", lambda t, s_=s_: t.transpose(
                out=p_w2[1][0:1, 128 * s_:128 * s_ + 128], in_=mcols[:, s_:s_ + 1],
                identity=eyef[:]),
                wait=(f"mc_{ti}_{nsub-1}", ev.get("pw2f_user"),),
                inc="pe" if s_ == nsub - 1 else None,
                event=f"mf_{ti}" if s_ == nsub - 1 else None)
        P.emit("vector", lambda v, W=W: v.tensor_copy(
            out=mrow1[:, 0:W] if W % 128 == 0 else mrow1[:, 0:((W + 127) // 128) * 128][:, 0:W],
            in_=p_w2[1][0:1, 0:W]),
            wait=(f"mf_{ti}", ev.get("mrow_user"),), inc="ve", event=f"mr_{ti}")
        ev["pw2f_user"] = f"mr_{ti}"
        P.emit("vector", lambda v, W=W: v.reciprocal(out=srow1[:, 0:W], in_=mrow1[:, 0:W]),
               wait=(f"mr_{ti}",), inc="ve", event=f"sr0_{ti}")
        P.emit("vector", lambda v, W=W: v.tensor_scalar_mul(
            out=srow1[:, 0:W], in0=srow1[:, 0:W], scalar1=127.0),
            wait=(f"sr0_{ti}",), inc="ve", event=f"sr_{ti}")
        P.emit("vector", lambda v, W=W: v.tensor_scalar_mul(
            out=dqrow1[:, 0:W], in0=mrow1[:, 0:W], scalar1=dqc_bc[1][0:1, :]),
            wait=(f"mr_{ti}", "dqcbc_1", ev.get("dqrow_user"),),
            inc="ve", event=f"dqr_{ti}")
        ev["mrow_user"] = f"dqr_{ti}"
        for s_ in range(nsub):
            sw = min(128, W - 128 * s_)
            P.emit("tensor", lambda t, s_=s_, sw=sw: t.matmul(
                out=p_w2[0][:, 128 * s_:128 * s_ + sw], lhsT=ones[:],
                rhs=srow1[0:1, 128 * s_:128 * s_ + sw], start=True, stop=True),
                wait=(f"sr_{ti}", ev.get("pw2s_user"),),
                inc="pe" if s_ == nsub - 1 else None,
                event=f"sbm_{ti}" if s_ == nsub - 1 else None)
        P.emit("vector", lambda v, W=W: v.tensor_copy(out=sc_bc[:, 0:W],
                                                      in_=p_w2[0][:, 0:W]),
               wait=(f"sbm_{ti}", ev.get("scbc_user"),), inc="ve", event=f"sbc_{ti}")
        ev["pw2s_user"] = f"sbc_{ti}"
        for s_ in range(nsub):
            sw = min(128, W - 128 * s_)
            P.emit("tensor", lambda t, s_=s_, sw=sw: t.matmul(
                out=p_w2[1][:, 128 * s_:128 * s_ + sw], lhsT=ones[:],
                rhs=dqrow1[0:1, 128 * s_:128 * s_ + sw], start=True, stop=True),
                wait=(f"dqr_{ti}", ev.get("pw2f_user"),),
                inc="pe" if s_ == nsub - 1 else None,
                event=f"dbm_{ti}" if s_ == nsub - 1 else None)
        P.emit("vector", lambda v, W=W: v.tensor_copy(out=dq_bc[:, 0:W],
                                                      in_=p_w2[1][:, 0:W]),
               wait=(f"dbm_{ti}", ev.get("dqbc_user"),), inc="ve", event=f"dbc_{ti}")
        ev["pw2f_user"] = f"dbc_{ti}"

    def b_loop2_fc(ti, t0, W, fc):
        nsub = (W + 127) // 128
        lo = max(t0 - 1, 0)
        hi = min(t0 + W + 1, ROWS)
        rt_lo, rt_hi = lo // 128, (hi - 1) // 128 + 1
        b = fc % 2
        P.emit("sync", lambda s, fc=fc, b=b: s.dma_start(
            out=w2qb[b][:], in_=W2Q[128 * fc:128 * fc + 128, :]),
            wait=(f"q2st_{NFC-1}", ev.get(f"w2qb_user{b}"),),
            inc=f"w2l{b}", inc_by=16, event=f"w2f_{ti}_{fc}")
        P.emit("vector", lambda v, fc=fc, W=W: v.tensor_tensor(
            out=qtmp[:, 0:W], in0=slst[:, TT_B * fc:TT_B * fc + W],
            in1=sc_bc[:, 0:W], op=ALU.mult),
            wait=(f"sbc_{ti}", f"sl_{ti}_{fc}", ev.get("qtmp_user"),),
            inc="ve", event=f"q0_{ti}_{fc}")
        ev[f"slfree_{fc}"] = f"q0_{ti}_{fc}"
        P.emit("vector", lambda v, W=W: v.tensor_scalar(
            out=qi8[:, 0:W], in0=qtmp[:, 0:W],
            scalar1=12582912.0, scalar2=12582912.0,
            op0=ALU.add, op1=ALU.subtract),
            wait=(f"q0_{ti}_{fc}", ev.get("qi8_user"),),
            inc="ve", event=f"qa_{ti}_{fc}")
        ev["qtmp_user"] = f"qa_{ti}_{fc}"
        P.emit("gpsimd", lambda g, fc=fc, W=W: g.tensor_tensor(
            out=xi2b[:, 0:W], in0=qi8[:, 0:W], in1=dq_bc[:, 0:W],
            op=ALU.mult),
            wait=(f"qa_{ti}_{fc}", f"dbc_{ti}", ev.get("xi2b_user"),),
            inc="ge", event=f"qb_{ti}_{fc}")
        ev["qi8_user"] = f"qb_{ti}_{fc}"
        for s_ in range(nsub):
            sw = min(128, W - 128 * s_)
            for half in range(2):
                last = (s_ == nsub - 1 and half == 1)
                P.emit("tensor", lambda t, fc=fc, b=b, s_=s_, half=half, sw=sw:
                       t.matmul(out=p_w2[s_][:, 512 * half:512 * half + 512],
                                lhsT=xi2b[:, 128 * s_:128 * s_ + sw],
                                rhs=w2qb[b][:, 512 * half:512 * half + 512],
                                start=(fc == 0), stop=(fc == NFC - 1)),
                       wait=(f"qb_{ti}_{fc}", f"w2f_{ti}_{fc}",
                             (f"sbc_{ti}" if s_ == 0 else
                              (f"dbc_{ti}" if s_ == 1 else None)) if fc == 0 else None,
                             ev.get(f"pw2_user{s_}") if fc == 0 else None,),
                       inc="pe" if last else None,
                       event=f"w2u_{ti}_{fc}" if last else None)
        ev["xi2b_user"] = f"w2u_{ti}_{fc}"
        ev[f"w2qb_user{b}"] = f"w2u_{ti}_{fc}"
    def b_loop2_post(ti):
        P.events[f"w2_{ti}"] = P.events[f"w2u_{ti}_{NFC-1}"]
    def b_epilogue(ti, t0, W):
        nsub = (W + 127) // 128
        lo = max(t0 - 1, 0)
        hi = min(t0 + W + 1, ROWS)
        rt_lo, rt_hi = lo // 128, (hi - 1) // 128 + 1
        for s_ in range(nsub):
            r0 = t0 + 128 * s_
            if not (OUT_LO <= r0 < OUT_HI):
                ev[f"pw2_user{s_}"] = f"w2_{ti}"
                continue
            ob = (r0 // 128) % 2
            P.emit("sync", lambda s, r0=r0, ob=ob: s.dma_start(
                out=xres[ob][:], in_=XLN[r0:r0 + 128, :]),
                wait=(ev.get(f"xres_user{ob}"),), inc=f"xr{ob}", inc_by=16,
                event=f"rl_{ti}_{s_}")
            P.emit("vector", lambda v, s_=s_, ob=ob: v.scalar_tensor_tensor(
                out=x1o[ob][:, 0:512], in0=p_w2[s_][:, 0:512], scalar=1.0,
                in1=xres[ob][:, 0:512], op0=ALU.mult, op1=ALU.add,
                accum_out=sums[:]),
                wait=(f"w2_{ti}", f"rl_{ti}_{s_}", ev.get(f"x1o_user{ob}"),),
                inc="ve", event=f"e0_{ti}_{s_}")
            P.emit("vector", lambda v, s_=s_, ob=ob: v.scalar_tensor_tensor(
                out=x1o[ob][:, 512:1024], in0=p_w2[s_][:, 512:1024], scalar=1.0,
                in1=xres[ob][:, 512:1024], op0=ALU.mult, op1=ALU.add,
                accum_out=sums2[:]),
                wait=(f"e0_{ti}_{s_}",), inc="ve", event=f"e1_{ti}_{s_}")
            ev[f"xres_user{ob}"] = f"e1_{ti}_{s_}"
            ev[f"pw2_user{s_}"] = f"e1_{ti}_{s_}"
            P.emit("vector", lambda v: v.tensor_add(out=sums[:], in0=sums[:], in1=sums2[:]),
                   wait=(f"e1_{ti}_{s_}",), inc="ve", event=f"es_{ti}_{s_}")
            P.emit("vector", lambda v: v.tensor_scalar_mul(out=mean[:], in0=sums[:],
                                                           scalar1=1.0 / D),
                   wait=(f"es_{ti}_{s_}",), inc="ve", event=f"em_{ti}_{s_}")
            P.emit("vector", lambda v: v.tensor_scalar_mul(out=mb[:], in0=mean[:],
                                                           scalar1=-1.0),
                   wait=(f"em_{ti}_{s_}", ev.get("mb_user"),), inc="ve",
                   event=f"eb_{ti}_{s_}")
            P.emit("scalar", lambda s, ob=ob: s.activation(
                out=junk[:], in_=x1o[ob][:], func=AF.Square, bias=mb[:], scale=1.0,
                accum_out=sumsq[:]),
                wait=(f"eb_{ti}_{s_}", ev.get("sumsq_user"),), inc="se",
                event=f"eq_{ti}_{s_}")
            P.emit("vector", lambda v: v.tensor_scalar(
                out=var[:], in0=sumsq[:], scalar1=1.0 / D, scalar2=1e-5,
                op0=ALU.mult, op1=ALU.add),
                wait=(f"eq_{ti}_{s_}",), inc="ve", event=f"ev_{ti}_{s_}")
            ev["sumsq_user"] = f"ev_{ti}_{s_}"
            P.emit("scalar", lambda s: s.activation(out=var[:], in_=var[:], func=AF.Sqrt),
                   wait=(f"ev_{ti}_{s_}",), inc="se", event=f"ed_{ti}_{s_}")
            P.emit("vector", lambda v: v.reciprocal(out=rstd[:], in_=var[:]),
                   wait=(f"ed_{ti}_{s_}", ev.get("rstd_user"),), inc="ve",
                   event=f"er_{ti}_{s_}")
            P.emit("vector", lambda v: v.tensor_tensor(out=mb[:], in0=mb[:],
                                                       in1=rstd[:], op=ALU.mult),
                   wait=(f"er_{ti}_{s_}",), inc="ve", event=f"emb_{ti}_{s_}")
            ev["mb_user"] = f"emb_{ti}_{s_}"
            P.emit("scalar", lambda s, ob=ob: s.activation(
                out=x1o[ob][:], in_=x1o[ob][:], func=AF.Identity, bias=mb[:],
                scale=rstd[:]),
                wait=(f"emb_{ti}_{s_}",), inc="se", event=f"eln_{ti}_{s_}")
            ev["mb_user"] = f"eln_{ti}_{s_}"
            ev["rstd_user"] = f"eln_{ti}_{s_}"
            P.emit("sync", lambda s, r0=r0, ob=ob: s.dma_start(
                out=X1[r0 - OUT_LO:r0 - OUT_LO + 128, :], in_=x1o[ob][:]),
                wait=(f"eln_{ti}_{s_}",), inc=f"so{ob}", inc_by=16,
                event=f"st_{ti}_{s_}")
            ev[f"x1o_user{ob}"] = f"st_{ti}_{s_}"
        ev["runmx_user"] = f"mt_{ti}_{nsub-1}"
        ev["scbc_user"] = f"qa_{ti}_{NFC-1}"
        ev["dqbc_user"] = f"qb_{ti}_{NFC-1}"
        ev["pw2s_user"] = ev.get("pw2_user0")
        ev["pw2f_user"] = ev.get("pw2_user1")

    for ti in range(len(tts)):
        t0, W = tts[ti]
        b_stage1(ti, t0, W)
        b_loop1_pre(ti, t0, W)
        for fc in range(NFC):
            b_loop1_fc(ti, t0, W, fc)
        b_loop1_post(ti)
        b_chain(ti, t0, W)
        for fc in range(NFC):
            b_loop2_fc(ti, t0, W, fc)
        b_loop2_post(ti)
        b_epilogue(ti, t0, W)


    ctx.close()


def build_B():
    nc = bass.Bass("TRN2", target_bir_lowering=False, debug=False)
    dram = _mk_dram(nc)
    ctx = ExitStack()
    sems = _mk_sems(nc, ctx)(ALL_SEMS)
    P = Plan()
    emit_B(nc, PhasePlan(P, "B."), dram, "ExternalInput", "ExternalOutput")
    P.finish_dma()
    with nc.Block() as block:
        P.emit_into(block, sems)
    ctx.close()
    return nc


def host_prep_B(inputs):
    w1 = np.asarray(inputs["L0_ffn_w1"], np.float32)
    w2 = np.asarray(inputs["L0_ffn_w2"], np.float32)
    conv_w = np.asarray(inputs["L0_ffn_conv_w"], np.float32)
    conv_b = np.asarray(inputs["L0_ffn_conv_b"], np.float32)
    w1T = np.ascontiguousarray(w1.T)
    w2T = np.ascontiguousarray(w2.T)
    taps = np.ascontiguousarray(conv_w[:, 0, :].reshape(NFC, 128, 3)
                                .transpose(1, 0, 2).reshape(128, NFC * 3))
    convb = np.ascontiguousarray(conv_b.reshape(NFC, 128).T)
    ones = np.ones((1, 128), np.float32)
    onesc = np.ones((128, 1), np.float32)
    eye = np.eye(128, dtype=np.float32)
    in_maps = []
    for c in range(NCORES):
        half = c % 2
        fL = 0.0 if half == 0 else 1.0
        fR = 1.0 if half == 0 else 0.0
        cp = np.ones((128, 2), np.float32)
        cp[127, 0] = fL
        cp[0, 1] = fR
        m = {"xln": None,
             "w1T": w1T, "w2T": w2T, "taps": taps, "convb": convb,
             "cpad": cp, "ones_row": ones, "ones_col": onesc, "eye128": eye}
        in_maps.append(m)
    return in_maps

NKT = 20
NQC = 8


def _c_mask_plan(qc):
    base = {(0, 0): "tria", (0, 1): None, (0, 2): None, (0, 3): None,
            (0, 4): "trib",
            (1, 1): "tria", (1, 2): None, (1, 3): None, (1, 4): None,
            (1, 5): "trib"}
    out = dict(base)
    if qc == 0:
        out[(0, 0)] = ("d", 0)
        out[(0, 1)] = ("d", 1)
        out[(1, 1)] = ("d", 2)
    if qc == NQC - 1:
        out[(0, 4)] = ("d", 3)
        out[(1, 4)] = ("d", 4)
        out[(1, 5)] = ("d", 5)
    return [(qs, ks, out[(qs, ks)]) for (qs, ks) in sorted(out)]


def emit_C(nc, P, dram, in_kind, out_kind):
    X1 = dram("x1", [NKT * 128, D], in_kind)
    WT = {k: dram(f"l1w{k}", [D, D], "ExternalInput") for k in "qkvo"}
    BR = {k: dram(f"l1b{k}", [1, D], "ExternalInput") for k in "qkvo"}
    ONES = dram("ones_row", [1, 128], "ExternalInput")
    EYE = dram("eye128", [128, 128], "ExternalInput")
    TRIA = dram("tria", [128, 128], "ExternalInput")
    TRIB = dram("trib", [128, 128], "ExternalInput")
    DMASK = dram("dmask", [6 * 128, 128], "ExternalInput")
    X2 = dram("x2", [2048, D], out_kind)

    ctx = ExitStack()
    sb = lambda name, shape, dt: ctx.enter_context(nc.sbuf_tensor(name + '_pC', shape, dt))
    ps = lambda name, shape: ctx.enter_context(nc.psum_tensor(name + '_pC', shape, F32))

    w16 = {k: sb(f"w16_{k}", [128, 8 * D], BF16) for k in "qkvo"}
    brow = {k: sb(f"brow_{k}", [1, D], F32) for k in "qkvo"}
    ones = sb("ones", [1, 128], F32)
    eye = sb("eye", [128, 128], BF16)
    tria = sb("triaT", [128, 128], BF16)
    trib = sb("tribT", [128, 128], BF16)
    dmask = sb("dmaskT", [128, 6 * 128], BF16)
    x16 = sb("x16C", [128, D], BF16)
    xfm = sb("xfmC", [128, D], BF16)
    q_tm = sb("q_tmC", [128, D], BF16)
    k_tm = sb("k_tmC", [128, D], BF16)
    qfm = [sb(f"qfmC{i}", [128, D], BF16) for i in range(8)]
    kfm = [sb(f"kfmC{i}", [128, D], BF16) for i in range(8)]
    v65 = [sb(f"v65C{i}", [128, 1040], BF16) for i in range(8)]
    psc = [sb(f"pscC{i}", [128, 768], BF16) for i in range(2)]
    pT = sb("pTC", [128, 768], BF16)
    att16 = sb("att16C", [128, D], BF16)
    attfm = sb("attfmC", [128, D], BF16)
    recips = sb("recipsC", [128, 1], F32)
    xres = [sb(f"xresC{i}", [128, D], F32) for i in range(2)]
    x2o = [sb(f"x2oC{i}", [128, D], F32) for i in range(2)]
    junk = sb("junkC", [128, D], F32)
    sums = sb("sumsC", [128, 1], F32)
    sums2 = sb("sums2C", [128, 1], F32)
    mean = sb("meanC", [128, 1], F32)
    sumsq = sb("sumsqC", [128, 1], F32)
    var = sb("varC", [128, 1], F32)
    rstd = sb("rstdC", [128, 1], F32)
    mb = sb("mbC", [128, 1], F32)

    p_s = [ps("p_sA", [128, 768]), ps("p_sB", [128, 768])]
    p_tr = ps("p_trC", [128, 768])
    p_at = ps("p_atC", [128, 260])
    p_pr1 = ps("p_prC", [128, 512])
    p_pr = [p_pr1, p_pr1]

    ev = {}

    for k in "qkvo":
        P.emit("gpsimd",
               lambda g, k=k: g.dma_start(
                   out=w16[k][:].rearrange("p (j e) -> p j e", j=8),
                   in_=WT[k].rearrange("(j p) e -> p j e", p=128)),
               inc="wl", inc_by=16, event=f"w_{k}")
    P.emit("gpsimd", lambda g: g.dma_start(out=eye[:], in_=EYE[:, :]),
           inc="wl", inc_by=16, event="eye")
    P.emit("gpsimd", lambda g: g.dma_start(out=tria[:], in_=TRIA[:, :]),
           inc="wl", inc_by=16, event="tria")
    P.emit("gpsimd", lambda g: g.dma_start(out=trib[:], in_=TRIB[:, :]),
           inc="wl", inc_by=16, event="trib")
    P.emit("gpsimd", lambda g: g.dma_start(
        out=dmask[:].rearrange("p (i c) -> p i c", i=6),
        in_=DMASK.rearrange("(i p) c -> p i c", p=128)),
        inc="wl", inc_by=16, event="dmaskl")
    for k in "qkvo":
        P.emit("sync", lambda s, k=k: s.dma_start(out=brow[k][:], in_=BR[k][:, :]),
               inc="dx", inc_by=16, event=f"b_{k}")
    P.emit("sync", lambda s: s.dma_start(out=ones[:], in_=ONES[:, :]),
           inc="dx", inc_by=16, event="ones")
    P.events["wsetup"] = ("wl", P.count["wl"])
    P.events["dsetup"] = ("dx", P.count["dx"])

    def proj_C_half(tag, w_key, src_ev, half):
        P.emit("tensor",
               lambda t, half=half, w_key=w_key:
               t.matmul(out=p_pr[half][:], lhsT=ones[:],
                        rhs=brow[w_key][:, 512 * half:512 * half + 512],
                        start=True, stop=False, skip_group_check=True),
               wait=(src_ev, "wsetup", "dsetup", ev.get("ppr_user"),))
        for j in range(8):
            P.emit("tensor",
                   lambda t, j=j, half=half, w_key=w_key:
                   t.matmul(out=p_pr[half][:],
                            lhsT=xfm[:, 128 * j:128 * j + 128],
                            rhs=w16[w_key][:, 1024 * j + 512 * half:
                                           1024 * j + 512 * half + 512],
                            start=False, stop=(j == 7), skip_group_check=True),
                   inc="pe" if j == 7 else None,
                   event=f"{tag}_mm{half}" if j == 7 else None)

    def transpose8_C(tag, src16, src_ev, dst, dst_free_ev=None):
        for b in range(2):
            for jj in range(4):
                j = 4 * b + jj
                P.emit("tensor",
                       lambda t, j=j, jj=jj, src16=src16:
                       t.matmul(out=p_tr[:, 128 * jj:128 * jj + 128],
                                lhsT=src16[:, 128 * j:128 * j + 128],
                                rhs=eye[:], start=True, stop=True),
                       wait=(src_ev, "wsetup", ev.get("ptrC_user"),),
                       inc="pe" if jj == 3 else None,
                       event=f"{tag}_tr{b}" if jj == 3 else None)
            P.emit("vector",
                   lambda v, b=b, dst=dst:
                   v.tensor_copy(out=dst[:, 512 * b:512 * b + 512],
                                 in_=p_tr[:, 0:512]),
                   wait=(f"{tag}_tr{b}", dst_free_ev),
                   inc="ve", event=f"{tag}_ev{b}")
            ev["ptrC_user"] = f"{tag}_ev{b}"
        P.events[f"{tag}_done"] = P.events[f"{tag}_ev1"]

    def stage1(rt):
        slot = rt % 8
        P.emit("gpsimd", lambda g, rt=rt: g.dma_start(
            out=x16[:], in_=X1[128 * rt:128 * rt + 128, :]),
            wait=(ev.get("x16C_user"),), inc="xg", inc_by=16, event=f"x16_{rt}")
        transpose8_C(f"xf{rt}", x16, f"x16_{rt}", xfm,
                     dst_free_ev=ev.get("xfmC_user"))
        ev["x16C_user"] = f"xf{rt}_tr1"
        if 2 <= rt < 18:
            for half in range(2):
                proj_C_half(f"q{rt}", "q", f"xf{rt}_done", half)
                P.emit("vector", lambda v, half=half: v.tensor_copy(
                    out=q_tm[:, 512 * half:512 * half + 512], in_=p_pr[half][:]),
                    wait=(f"q{rt}_mm{half}", ev.get("qtmC_user"),),
                    inc="ve", event=f"qtm_{rt}_{half}")
                ev["ppr_user"] = f"qtm_{rt}_{half}"
            transpose8_C(f"qf{rt}", q_tm, f"qtm_{rt}_1", qfm[slot],
                         dst_free_ev=ev.get(f"qfm_user{slot}"))
            ev["qtmC_user"] = f"qf{rt}_tr1"
        for half in range(2):
            proj_C_half(f"k{rt}", "k", f"xf{rt}_done", half)
            P.emit("vector", lambda v, half=half: v.tensor_copy(
                out=k_tm[:, 512 * half:512 * half + 512], in_=p_pr[half][:]),
                wait=(f"k{rt}_mm{half}", ev.get("ktmC_user"),),
                inc="ve", event=f"ktm_{rt}_{half}")
            ev["ppr_user"] = f"ktm_{rt}_{half}"
        transpose8_C(f"kf{rt}", k_tm, f"ktm_{rt}_1", kfm[slot],
                     dst_free_ev=ev.get(f"kfm_user{slot}"))
        ev["ktmC_user"] = f"kf{rt}_tr1"
        P.emit("vector", lambda v, slot=slot: v.memset(
            v65[slot][:, 64:1040:65], 1.0),
            wait=(ev.get(f"v65_user{slot}"),), inc="ve", event=f"vm_{rt}")
        for half in range(2):
            proj_C_half(f"v{rt}", "v", f"xf{rt}_done", half)
            P.emit("vector", lambda v, half=half, slot=slot: v.tensor_copy(
                out=v65[slot][:, 520 * half:520 * half + 520]
                    .rearrange("p (h d) -> p h d", h=8)[:, :, 0:64],
                in_=p_pr[half][:].rearrange("p (h d) -> p h d", h=8)),
                wait=(f"v{rt}_mm{half}", f"vm_{rt}"),
                inc="ve", event=f"v65_{rt}_{half}")
            ev["ppr_user"] = f"v65_{rt}_{half}"
        P.events[f"s1_{rt}"] = P.events[f"v65_{rt}_1"]
        ev["xfmC_user"] = f"v{rt}_mm1"

    def chunk(qc):
        plan_list = _c_mask_plan(qc)
        for qs in range(2):
            qrt = 2 + 2 * qc + qs
            qslot = qrt % 8
            sub = [(ks, mk) for (q2, ks, mk) in plan_list if q2 == qs]
            pcol0 = 0 if qs == 0 else 128
            nsk = len(sub)

            def emit_scores(h):
                si = h % 2
                base = (h % 2) * 64
                jb = (h // 2) * 128
                for idx, (ks, mk) in enumerate(sub):
                    kslot = (2 * qc + ks) % 8
                    P.emit("tensor",
                           lambda t, base=base, jb=jb, qslot=qslot,
                                  kslot=kslot, si=si, ks=ks:
                           t.matmul(out=p_s[si][:, 128 * ks:128 * ks + 128],
                                    lhsT=qfm[qslot][base:base + 64, jb:jb + 128],
                                    rhs=kfm[kslot][base:base + 64, jb:jb + 128],
                                    start=True, stop=True),
                           wait=(f"win_{qc}", ev.get(f"ps_user{si}"),),
                           inc="pe" if idx == nsk - 1 else None,
                           event=f"sc_{qc}_{qs}_{h}" if idx == nsk - 1 else None)
                P.emit("scalar",
                       lambda s, si=si, pcol0=pcol0:
                       s.activation(out=psc[si][:, pcol0:pcol0 + 640],
                                    in_=p_s[si][:, pcol0:pcol0 + 640],
                                    func=AF.Exp, scale=0.125),
                       wait=(f"sc_{qc}_{qs}_{h}", ev.get(f"psc_user{si}"),),
                       inc="se", event=f"ex_{qc}_{qs}_{h}")
                ev[f"ps_user{si}"] = f"ex_{qc}_{qs}_{h}"
                mask_ev = f"ex_{qc}_{qs}_{h}"
                nm_i = 0
                for (ks, mk) in sub:
                    if mk is None:
                        continue
                    mt = {"tria": tria, "trib": trib}.get(mk)
                    moff = None if mt is not None else 128 * mk[1]
                    pcol = 128 * ks
                    P.emit("vector",
                           lambda v, pcol=pcol, mt=mt, moff=moff, si=si:
                           v.tensor_tensor(out=psc[si][:, pcol:pcol + 128],
                                           in0=psc[si][:, pcol:pcol + 128],
                                           in1=mt[:] if mt is not None
                                           else dmask[:, moff:moff + 128],
                                           op=ALU.mult),
                           wait=(mask_ev, "wsetup"),
                           inc="ve", event=f"mk_{qc}_{qs}_{h}_{nm_i}")
                    mask_ev = f"mk_{qc}_{qs}_{h}_{nm_i}"
                    nm_i += 1
                P.events[f"msk_{qc}_{qs}_{h}"] = P.events[mask_ev]

            def emit_avpart(h, rnd, i4):
                hsi = h % 2
                for si_, (ks, mk) in enumerate(sub):
                    pcol = 128 * ks
                    P.emit("tensor",
                           lambda t, pcol=pcol, si_=si_, hsi=hsi:
                           t.matmul(out=p_tr[:, 128 * si_:128 * si_ + 128],
                                    lhsT=psc[hsi][:, pcol:pcol + 128],
                                    rhs=eye[:], start=True, stop=True),
                           wait=(f"msk_{qc}_{qs}_{h}", ev.get("ptrC_user"),),
                           inc="pe" if si_ == nsk - 1 else None,
                           event=f"pt_{qc}_{qs}_{h}" if si_ == nsk - 1 else None)
                P.emit("vector", lambda v, nsk=nsk: v.tensor_copy(
                    out=pT[:, 0:128 * nsk], in_=p_tr[:, 0:128 * nsk]),
                    wait=(f"pt_{qc}_{qs}_{h}", ev.get("pT_user"),),
                    inc="ve", event=f"pte_{qc}_{qs}_{h}")
                ev["ptrC_user"] = f"pte_{qc}_{qs}_{h}"
                ev[f"psc_user{hsi}"] = f"pt_{qc}_{qs}_{h}"
                for si_, (ks, mk) in enumerate(sub):
                    kslot = (2 * qc + ks) % 8
                    P.emit("tensor",
                           lambda t, si_=si_, i4=i4, h=h, kslot=kslot:
                           t.matmul(out=p_at[:, 65 * i4:65 * i4 + 65],
                                    lhsT=pT[:, 128 * si_:128 * si_ + 128],
                                    rhs=v65[kslot][:, 65 * h:65 * h + 65],
                                    start=(si_ == 0), stop=(si_ == nsk - 1)),
                           wait=(f"pte_{qc}_{qs}_{h}",
                                 ev.get("pat_user") if si_ == 0 else None,),
                           inc="pe" if si_ == nsk - 1 else None,
                           event=f"av_{qc}_{qs}_{h}" if si_ == nsk - 1 else None)
                ev["pT_user"] = f"av_{qc}_{qs}_{h}"

            emit_scores(0)
            for rnd in range(4):
                for i4 in range(4):
                    h = 4 * rnd + i4
                    if h + 1 < NH:
                        emit_scores(h + 1)
                    emit_avpart(h, rnd, i4)
                for i4 in range(4):
                    h = 4 * rnd + i4
                    P.emit("vector", lambda v, i4=i4: v.reciprocal(
                        out=recips[:], in_=p_at[:, 65 * i4 + 64:65 * i4 + 65]),
                        wait=(f"av_{qc}_{qs}_{4*rnd+3}", ev.get("recips_user"),),
                        inc="ve", event=f"rc_{qc}_{qs}_{h}")
                    P.emit("scalar", lambda s, i4=i4, h=h: s.activation(
                        out=att16[:, 64 * h:64 * h + 64],
                        in_=p_at[:, 65 * i4:65 * i4 + 64],
                        func=AF.Identity, scale=recips[:]),
                        wait=(f"rc_{qc}_{qs}_{h}", ev.get("att16C_user"),),
                        inc="se", event=f"ae_{qc}_{qs}_{h}")
                    ev["recips_user"] = f"ae_{qc}_{qs}_{h}"
                ev["pat_user"] = f"ae_{qc}_{qs}_{4*rnd+3}"

            transpose8_C(f"af{qc}_{qs}", att16, f"ae_{qc}_{qs}_{NH-1}", attfm,
                         dst_free_ev=ev.get("attfmC_user"))
            ev["att16C_user"] = f"af{qc}_{qs}_tr1"
            proj_o_ev = f"af{qc}_{qs}_done"
            r0 = 128 * qrt
            ob = qrt % 2
            P.emit("sync", lambda s, r0=r0, ob=ob: s.dma_start(
                out=xres[ob][:], in_=X1[r0:r0 + 128, :]),
                wait=(ev.get(f"xresC_user{ob}"),), inc=f"xr{ob}", inc_by=16,
                event=f"rl_{qc}_{qs}")
            for half in range(2):
                P.emit("tensor",
                       lambda t, half=half:
                       t.matmul(out=p_pr[half][:], lhsT=ones[:],
                                rhs=brow["o"][:, 512 * half:512 * half + 512],
                                start=True, stop=False, skip_group_check=True),
                       wait=(proj_o_ev, ev.get("ppr_user"),))
                for j in range(8):
                    P.emit("tensor",
                           lambda t, j=j, half=half:
                           t.matmul(out=p_pr[half][:],
                                    lhsT=attfm[:, 128 * j:128 * j + 128],
                                    rhs=w16["o"][:, 1024 * j + 512 * half:
                                                 1024 * j + 512 * half + 512],
                                    start=False, stop=(j == 7), skip_group_check=True),
                           inc="pe" if j == 7 else None,
                           event=f"o{qc}_{qs}_mm{half}" if j == 7 else None)
                P.emit("vector", lambda v, ob=ob, half=half: v.scalar_tensor_tensor(
                    out=x2o[ob][:, 512 * half:512 * half + 512], in0=p_pr[half][:],
                    scalar=1.0, in1=xres[ob][:, 512 * half:512 * half + 512],
                    op0=ALU.mult, op1=ALU.add,
                    accum_out=sums[:] if half == 0 else sums2[:]),
                    wait=(f"o{qc}_{qs}_mm{half}", f"rl_{qc}_{qs}",
                          ev.get(f"x2o_user{ob}") if half == 0 else None,),
                    inc="ve", event=f"e{half}_{qc}_{qs}")
                ev["ppr_user"] = f"e{half}_{qc}_{qs}"
            ev["attfmC_user"] = f"o{qc}_{qs}_mm1"
            ev[f"xresC_user{ob}"] = f"e1_{qc}_{qs}"
            P.emit("vector", lambda v: v.tensor_add(out=sums[:], in0=sums[:],
                                                    in1=sums2[:]),
                   wait=(f"e1_{qc}_{qs}",), inc="ve", event=f"es_{qc}_{qs}")
            P.emit("vector", lambda v: v.tensor_scalar_mul(out=mean[:], in0=sums[:],
                                                           scalar1=1.0 / D),
                   wait=(f"es_{qc}_{qs}",), inc="ve", event=f"em_{qc}_{qs}")
            P.emit("vector", lambda v: v.tensor_scalar_mul(out=mb[:], in0=mean[:],
                                                           scalar1=-1.0),
                   wait=(f"em_{qc}_{qs}", ev.get("mbC_user"),), inc="ve",
                   event=f"eb_{qc}_{qs}")
            P.emit("scalar", lambda s, ob=ob: s.activation(
                out=junk[:], in_=x2o[ob][:], func=AF.Square, bias=mb[:], scale=1.0,
                accum_out=sumsq[:]),
                wait=(f"eb_{qc}_{qs}", ev.get("sumsqC_user"),), inc="se",
                event=f"eq_{qc}_{qs}")
            P.emit("vector", lambda v: v.tensor_scalar(
                out=var[:], in0=sumsq[:], scalar1=1.0 / D, scalar2=1e-5,
                op0=ALU.mult, op1=ALU.add),
                wait=(f"eq_{qc}_{qs}",), inc="ve", event=f"ev_{qc}_{qs}")
            ev["sumsqC_user"] = f"ev_{qc}_{qs}"
            P.emit("scalar", lambda s: s.activation(out=var[:], in_=var[:],
                                                    func=AF.Sqrt),
                   wait=(f"ev_{qc}_{qs}",), inc="se", event=f"ed_{qc}_{qs}")
            P.emit("vector", lambda v: v.reciprocal(out=rstd[:], in_=var[:]),
                   wait=(f"ed_{qc}_{qs}", ev.get("rstdC_user"),), inc="ve",
                   event=f"er_{qc}_{qs}")
            P.emit("vector", lambda v: v.tensor_tensor(out=mb[:], in0=mb[:],
                                                       in1=rstd[:], op=ALU.mult),
                   wait=(f"er_{qc}_{qs}",), inc="ve", event=f"emb_{qc}_{qs}")
            ev["mbC_user"] = f"emb_{qc}_{qs}"
            P.emit("scalar", lambda s, ob=ob: s.activation(
                out=x2o[ob][:], in_=x2o[ob][:], func=AF.Identity, bias=mb[:],
                scale=rstd[:]),
                wait=(f"emb_{qc}_{qs}",), inc="se", event=f"eln_{qc}_{qs}")
            ev["mbC_user"] = f"eln_{qc}_{qs}"
            ev["rstdC_user"] = f"eln_{qc}_{qs}"
            P.emit("sync", lambda s, r0=r0, ob=ob: s.dma_start(
                out=X2[r0 - 256:r0 - 256 + 128, :], in_=x2o[ob][:]),
                wait=(f"eln_{qc}_{qs}",), inc=f"so{ob}", inc_by=16,
                event=f"sx_{qc}_{qs}")
            ev[f"x2o_user{ob}"] = f"sx_{qc}_{qs}"
        for ks in range(6):
            kslot = (2 * qc + ks) % 8
            ev[f"kfm_user{kslot}"] = f"av_{qc}_{1}_{NH-1}"
            ev[f"v65_user{kslot}"] = f"av_{qc}_{1}_{NH-1}"
        for qs in range(2):
            ev[f"qfm_user{(2 + 2*qc + qs) % 8}"] = f"av_{qc}_{qs}_{NH-1}"

    def _c_exp_ranges(sub):
        out = []
        ks_list = [ks for ks, _ in sub]
        lo0 = min(k for k in ks_list if k < 4) * 128
        hi0 = (max(k for k in ks_list if k < 4) + 1) * 128
        out.append((0, lo0, hi0))
        if any(k >= 4 for k in ks_list):
            lo1 = (min(k for k in ks_list if k >= 4) - 4) * 128
            hi1 = (max(k for k in ks_list if k >= 4) - 4 + 1) * 128
            out.append((1, lo1, hi1))
        return out

    fired = set()
    for rt in range(NKT):
        stage1(rt)
        for qc in range(NQC):
            if qc not in fired and 2 * qc + 5 <= rt:
                if 2 * qc + 3 <= rt:
                    fired.add(qc)
                    P.events[f"win_{qc}"] = P.events[f"s1_{2*qc+5}"]
                    chunk(qc)
    for qc in range(NQC):
        assert qc in fired

    ctx.close()


def build_C():
    nc = bass.Bass("TRN2", target_bir_lowering=False, debug=False)
    dram = _mk_dram(nc)
    ctx = ExitStack()
    sems = _mk_sems(nc, ctx)(ALL_SEMS)
    P = Plan()
    emit_C(nc, PhasePlan(P, "C."), dram, "ExternalInput", "ExternalOutput")
    P.finish_dma()
    with nc.Block() as block:
        P.emit_into(block, sems)
    ctx.close()
    return nc


def host_prep_C(inputs):
    wq = np.ascontiguousarray(np.asarray(inputs["L1_attn_wq"], np.float32))
    wk = np.ascontiguousarray(np.asarray(inputs["L1_attn_wk"], np.float32))
    wv = np.ascontiguousarray(np.asarray(inputs["L1_attn_wv"], np.float32))
    wo = np.ascontiguousarray(np.asarray(inputs["L1_attn_wo"], np.float32))
    b = {k: np.ascontiguousarray(np.asarray(inputs[f"L1_attn_b{k}"], np.float32)[None, :])
         for k in "qkvo"}
    ones = np.ones((1, 128), np.float32)
    eye = np.eye(128, dtype=np.float32)
    ii, jj = np.meshgrid(np.arange(128), np.arange(128), indexing="ij")
    tria = (jj >= ii).astype(np.float32)
    trib = (jj <= ii).astype(np.float32)
    full = np.ones((128, 128), np.float32)
    zero = np.zeros((128, 128), np.float32)
    in_maps = []
    for c in range(NCORES):
        half = c % 2
        if half == 0:
            dm = [zero, zero, zero, trib, full, trib]
        else:
            dm = [tria, full, tria, zero, zero, zero]
        dmask = np.ascontiguousarray(np.concatenate(dm, axis=0))
        m = {"x1": None,
             "l1wq": wq, "l1wk": wk, "l1wv": wv, "l1wo": wo,
             "l1bq": b["q"], "l1bk": b["k"], "l1bv": b["v"], "l1bo": b["o"],
             "ones_row": ones, "eye128": eye, "tria": tria, "trib": trib,
             "dmask": dmask}
        in_maps.append(m)
    return in_maps

NE = 18
IM = 512
NTD = 16



def precastD_jobs(nc, dram):
    MGW = dram("mgw", [NE * D, IM], "ExternalInput")
    MUW = dram("muw", [NE * D, IM], "ExternalInput")
    MDW = dram("mdw", [NE * IM, D], "ExternalInput")
    MGWQ = dram("mgwq_scratch", [NE * D, IM], "InternalB16")
    MUWQ = dram("muwq_scratch", [NE * D, IM], "InternalB16")
    MDWQ = dram("mdwq_scratch", [NE * IM, D], "InternalB16")
    jobs = []
    for e in range(NE):
        for srcT, dstT, rows in ((MGW, MGWQ, D), (MUW, MUWQ, D), (MDW, MDWQ, IM)):
            jobs.append(lambda g, srcT=srcT, dstT=dstT, e=e, rows=rows:
                        g.dma_start(out=dstT[rows * e:rows * (e + 1), :],
                                    in_=srcT[rows * e:rows * (e + 1), :]))
    return jobs


def emit_precastD(nc, P, dram):
    for jb in precastD_jobs(nc, dram):
        P.emit("gpsimd", jb, inc="mq", inc_by=16)
    P.events["mq_all"] = ("mq", P.count["mq"])


def emit_D(nc, P, dram, in_kind):
    X2 = dram("x2", [2048, D], in_kind)
    RT = dram("routerT", [D, 16], "ExternalInput")
    EB = dram("ebias", [1, 16], "ExternalInput")
    TB = dram("tbreak", [1, 16], "ExternalInput")
    MGW = dram("mgw", [NE * D, IM], "ExternalInput")
    MUW = dram("muw", [NE * D, IM], "ExternalInput")
    MDW = dram("mdw", [NE * IM, D], "ExternalInput")
    ONES = dram("ones_row", [1, 128], "ExternalInput")
    EYE = dram("eye128", [128, 128], "ExternalInput")
    OUT = dram("out", [2048, D], "ExternalOutput")
    MGWQ = dram("mgwq_scratch", [NE * D, IM], "InternalB16")
    MUWQ = dram("muwq_scratch", [NE * D, IM], "InternalB16")
    MDWQ = dram("mdwq_scratch", [NE * IM, D], "InternalB16")

    ctx = ExitStack()
    sb = lambda name, shape, dt: ctx.enter_context(nc.sbuf_tensor(name + '_pD', shape, dt))
    ps = lambda name, shape: ctx.enter_context(nc.psum_tensor(name + '_pD', shape, F32))

    x2fm = [sb(f"x2fmD{i}", [128, D], BF16) for i in range(NTD)]
    racc = [sb(f"raccD{i}", [128, D], F32) for i in range(8)]
    gw16 = [sb(f"gw16D{i}", [128, 8 * IM], BF16) for i in range(2)]
    uw16 = [sb(f"uw16D{i}", [128, 8 * IM], BF16) for i in range(2)]
    dw16 = [sb(f"dw16D{i}", [128, 4 * D], BF16) for i in range(2)]
    rT = sb("rTD", [128, 8 * 16], F32)
    comb = sb("combD", [128, NTD * NE], F32)
    ones = sb("onesD", [1, 128], F32)
    eye = sb("eyeD", [128, 128], BF16)
    eyef = sb("eyefD", [128, 128], F32)
    ebias_bc = sb("ebiasD", [128, 16], F32)
    tb_bc = sb("tbD", [128, 16], F32)
    x16 = sb("x16D", [128, D], BF16)
    xf32 = sb("xf32D", [128, D], F32)
    i8mark = sb("i8markD", [128, 64], I8)
    fmark = sb("fmarkD", [128, 16], F32)
    xffm = sb("xffmD", [128, D], F32)
    scr = sb("scrD", [128, 16], F32)
    sc = sb("scD", [128, 16], F32)
    msk = sb("mskD", [128, 16], F32)
    msk2 = sb("msk2D", [128, 16], F32)
    gsel = sb("gselD", [128, 16], F32)
    u1 = sb("u1D", [128, 4], F32)
    w1_ = sb("w1D", [128, 4], F32)
    u2 = sb("u2D", [128, 4], F32)
    w2_ = sb("w2D", [128, 4], F32)
    min1 = sb("min1D", [128, 1], F32)
    min2 = sb("min2D", [128, 1], F32)
    ta = sb("taD", [128, 4], F32)
    tbv = sb("tbvD", [128, 4], F32)
    gs = sb("gsD", [128, 4], F32)
    gp = sb("gpD", [128, 2], F32)
    gq = sb("gqD", [128, 2], F32)
    sa = sb("saD", [128, 1], F32)
    sbv = sb("sbD", [128, 1], F32)
    mx2 = sb("mx2D", [128, 1], F32)
    gmask = sb("gmaskD", [128, 4], F32)
    mr = sb("mrD", [128, 1], F32)
    gr = sb("grD", [128, 16], F32)
    tw = sb("twD", [128, 16], F32)
    dn = sb("dnD", [128, 1], F32)
    cs = sb("csD", [128, 1], F32)
    gsl = [sb(f"gslD{i}", [128, 512], F32) for i in range(2)]
    he = [sb(f"heD{i}", [128, 512], F32) for i in range(2)]
    hes = [sb(f"hesD{i}", [128, 512], BF16) for i in range(2)]
    hesf = [sb(f"hesfD{i}", [128, 512], BF16) for i in range(2)]
    xres = [sb(f"xresD{i}", [128, D], F32) for i in range(2)]
    oo = [sb(f"ooD{i}", [128, D], F32) for i in range(2)]
    junk = sb("junkD", [128, D], F32)
    sums = sb("sumsD", [128, 1], F32)
    sums2 = sb("sums2D", [128, 1], F32)
    mean = sb("meanD", [128, 1], F32)
    sumsq = sb("sumsqD", [128, 1], F32)
    var = sb("varD", [128, 1], F32)
    rstd = sb("rstdD", [128, 1], F32)
    mb = sb("mbD", [128, 1], F32)

    p_g = [ps(f"p_gD{i}", [128, 512]) for i in range(2)]
    p_u = [ps(f"p_uD{i}", [128, 512]) for i in range(2)]
    p_d = ps("p_dD", [128, D])
    p_tr = [ps(f"p_trD{i}", [128, 512]) for i in range(2)]
    p_r = p_tr[0]

    ev = {}

    P.emit("gpsimd", lambda g: g.dma_start(out=eye[:], in_=EYE[:, :]),
           inc="wl", inc_by=16, event="eye")
    P.emit("sync", lambda s: s.dma_start(out=eyef[:], in_=EYE[:, :]),
           inc="dx", inc_by=16, event="eyef")
    P.emit("sync", lambda s: s.dma_start(out=ones[:], in_=ONES[:, :]),
           inc="dx", inc_by=16, event="ones")
    P.emit("sync", lambda s: s.dma_start(
        out=rT[:].rearrange("p (j e) -> p j e", j=8),
        in_=RT.rearrange("(j p) e -> p j e", p=128)),
        inc="dx", inc_by=16, event="rT")
    P.emit("sync", lambda s: s.dma_start(out=scr[0:1, :], in_=EB[:, :]),
           inc="dx", inc_by=16, event="ebrow")
    P.emit("sync", lambda s: s.dma_start(out=sc[0:1, :], in_=TB[:, :]),
           inc="dx", inc_by=16, event="tbrow")
    P.events["dsetup"] = ("dx", P.count["dx"])
    P.emit("vector", lambda v: v.memset(i8mark[:], 1),
           inc="ve", event="i8m")
    P.emit("vector", lambda v: v.tensor_copy(out=fmark[:].bitcast(I8), in_=i8mark[:]),
           wait=("i8m",), inc="ve", event="fmark")
    P.emit("tensor", lambda t: t.matmul(out=p_r[:, 0:16], lhsT=ones[:],
                                        rhs=scr[0:1, :], start=True, stop=True),
           wait=("dsetup",), inc="pe", event="ebmm")
    P.emit("vector", lambda v: v.tensor_copy(out=ebias_bc[:], in_=p_r[:, 0:16]),
           wait=("ebmm",), inc="ve", event="ebbc")
    P.emit("tensor", lambda t: t.matmul(out=p_r[:, 0:16], lhsT=ones[:],
                                        rhs=sc[0:1, :], start=True, stop=True),
           wait=("ebbc",), inc="pe", event="tbmm")
    P.emit("vector", lambda v: v.tensor_copy(out=tb_bc[:], in_=p_r[:, 0:16]),
           wait=("tbmm",), inc="ve", event="tbbc")
    ev0_init = True

    ev["ptrD_user0"] = "tbbc"
    for t in range(NTD):
        P.emit("gpsimd", lambda g, t=t: g.dma_start(
            out=x16[:], in_=X2[128 * t:128 * t + 128, :]),
            wait=(ev.get("x16D_user"),), inc="xg", inc_by=16, event=f"x16_{t}")
        P.emit("sync", lambda s, t=t: s.dma_start(
            out=xf32[:], in_=X2[128 * t:128 * t + 128, :]),
            wait=(ev.get("xf32D_user"),), inc="xr0", inc_by=16, event=f"xf_{t}")
        for b in range(2):
            for jj in range(4):
                j = 4 * b + jj
                P.emit("tensor", lambda tt, j=j, jj=jj, b=b:
                       tt.matmul(out=p_tr[b][:, 128 * jj:128 * jj + 128],
                                 lhsT=x16[:, 128 * j:128 * j + 128],
                                 rhs=eye[:], start=True, stop=True),
                       wait=(f"x16_{t}", "eye", ev.get(f"ptrD_user{b}"),),
                       inc="pe" if jj == 3 else None,
                       event=f"tr_{t}_{b}" if jj == 3 else None)
            P.emit("vector", lambda v, b=b, t=t: v.tensor_copy(
                out=x2fm[t][:, 512 * b:512 * b + 512], in_=p_tr[b][:]),
                wait=(f"tr_{t}_{b}",), inc="ve", event=f"tre_{t}_{b}")
            ev[f"ptrD_user{b}"] = f"tre_{t}_{b}"
        ev["x16D_user"] = f"tr_{t}_1"
        P.events[f"x2fm_{t}"] = P.events[f"tre_{t}_1"]
        for j in range(8):
            P.emit("tensor", lambda tt, j=j:
                   tt.transpose(out=p_tr[0][:, 0:128], in_=xf32[:, 128 * j:128 * j + 128],
                                identity=eyef[:]),
                   wait=(f"xf_{t}", "eyef", ev.get("ptrD_user0"),),
                   inc="pe", event=f"ftr_{t}_{j}")
            P.emit("vector", lambda v, j=j: v.tensor_copy(
                out=xffm[:, 128 * j:128 * j + 128], in_=p_tr[0][:, 0:128]),
                wait=(f"ftr_{t}_{j}", ev.get("xffm_user"),),
                inc="ve", event=f"fte_{t}_{j}")
            ev["ptrD_user0"] = f"fte_{t}_{j}"
        ev["xf32D_user"] = f"ftr_{t}_7"
        for j in range(8):
            P.emit("tensor", lambda tt, j=j:
                   tt.matmul(out=p_r[:, 0:16], lhsT=xffm[:, 128 * j:128 * j + 128],
                             rhs=rT[:, 16 * j:16 * j + 16],
                             start=(j == 0), stop=(j == 7)),
                   wait=(f"fte_{t}_7", "rT",
                         ev.get("ptrD_user0") if j == 0 else None,),
                   inc="pe" if j == 7 else None,
                   event=f"rlg_{t}" if j == 7 else None)
        ev["xffm_user"] = f"rlg_{t}"
        def VE(fn, wait, evn):
            P.emit("vector", fn, wait=wait, inc="ve", event=evn)
        P.emit("scalar", lambda s: s.activation(out=scr[:], in_=p_r[:, 0:16],
                                                func=AF.Sigmoid),
               wait=(f"rlg_{t}", ev.get("scr_user"),), inc="se", event=f"sg_{t}")
        ev["ptrD_user0"] = f"sg_{t}"
        VE(lambda v: v.tensor_add(out=sc[:], in0=scr[:], in1=ebias_bc[:]),
           (f"sg_{t}", "ebbc", "tbbc", ev.get("sc_user"),), f"sc_{t}")
        VE(lambda v: v.tensor_tensor(out=u1[:], in0=sc[:, 0:16:4], in1=sc[:, 1:16:4], op=ALU.min),
           (f"sc_{t}",), f"u1_{t}")
        VE(lambda v: v.tensor_tensor(out=w1_[:], in0=sc[:, 0:16:4], in1=sc[:, 1:16:4], op=ALU.max),
           (f"u1_{t}",), f"w1_{t}")
        VE(lambda v: v.tensor_tensor(out=u2[:], in0=sc[:, 2:16:4], in1=sc[:, 3:16:4], op=ALU.min),
           (f"w1_{t}",), f"u2_{t}")
        VE(lambda v: v.tensor_tensor(out=w2_[:], in0=sc[:, 2:16:4], in1=sc[:, 3:16:4], op=ALU.max),
           (f"u2_{t}",), f"w2_{t}")
        VE(lambda v: v.tensor_tensor(out=ta[:], in0=u1[:], in1=u2[:], op=ALU.max),
           (f"w2_{t}",), f"ta_{t}")
        VE(lambda v: v.tensor_tensor(out=tbv[:], in0=w1_[:], in1=w2_[:], op=ALU.min),
           (f"ta_{t}",), f"tb_{t}")
        VE(lambda v: v.tensor_tensor(out=gs[:], in0=sc[:, 0:16:4], in1=sc[:, 1:16:4], op=ALU.add),
           (f"tb_{t}",), f"g1_{t}")
        VE(lambda v: v.tensor_tensor(out=gs[:], in0=gs[:], in1=sc[:, 2:16:4], op=ALU.add),
           (f"g1_{t}",), f"g2_{t}")
        VE(lambda v: v.tensor_tensor(out=gs[:], in0=gs[:], in1=sc[:, 3:16:4], op=ALU.add),
           (f"g2_{t}",), f"g3_{t}")
        VE(lambda v: v.tensor_tensor(out=u1[:], in0=u1[:], in1=u2[:], op=ALU.min),
           (f"g3_{t}",), f"mn1_{t}")
        VE(lambda v: v.tensor_tensor(out=ta[:], in0=ta[:], in1=tbv[:], op=ALU.min),
           (f"mn1_{t}",), f"mn2_{t}")
        VE(lambda v: v.tensor_sub(out=gs[:], in0=gs[:], in1=u1[:]),
           (f"mn2_{t}",), f"g4_{t}")
        VE(lambda v: v.tensor_sub(out=gs[:], in0=gs[:], in1=ta[:]),
           (f"g4_{t}",), f"g5_{t}")
        VE(lambda v: v.tensor_tensor(out=gp[:], in0=gs[:, 0:4:2], in1=gs[:, 1:4:2], op=ALU.max),
           (f"g5_{t}",), f"gp_{t}")
        VE(lambda v: v.tensor_tensor(out=gq[:], in0=gs[:, 0:4:2], in1=gs[:, 1:4:2], op=ALU.min),
           (f"gp_{t}",), f"gq_{t}")
        VE(lambda v: v.tensor_tensor(out=sa[:], in0=gp[:, 0:1], in1=gp[:, 1:2], op=ALU.min),
           (f"gq_{t}",), f"sa_{t}")
        VE(lambda v: v.tensor_tensor(out=sbv[:], in0=gq[:, 0:1], in1=gq[:, 1:2], op=ALU.max),
           (f"sa_{t}",), f"sb_{t}")
        VE(lambda v: v.tensor_tensor(out=mx2[:], in0=sa[:], in1=sbv[:], op=ALU.max),
           (f"sb_{t}",), f"mx2_{t}")
        VE(lambda v: v.tensor_single_scalar(out=gmask[:], in_=gs[:], scalar=mx2[:], op=ALU.is_ge),
           (f"mx2_{t}",), f"gm_{t}")
        for j4 in range(4):
            VE(lambda v, j4=j4: v.tensor_tensor(out=msk[:, j4:16:4], in0=sc[:, j4:16:4],
                                                in1=gmask[:], op=ALU.mult),
               (f"gm_{t}" if j4 == 0 else f"mk{j4-1}_{t}",), f"mk{j4}_{t}")
        VE(lambda v: v.tensor_add(out=msk[:], in0=msk[:], in1=tb_bc[:]),
           (f"mk3_{t}",), f"mtb_{t}")
        VE(lambda v: v.tensor_copy(out=msk2[:], in_=msk[:]),
           (f"mtb_{t}",), f"ms2_{t}")
        prev = f"ms2_{t}"
        for r_ in range(4):
            VE(lambda v: v.reduce_max(out=mr[:], in_=msk2[:], axis=AX.X),
               (prev,), f"xm{r_}_{t}")
            if r_ < 3:
                VE(lambda v: v.tensor_single_scalar(out=gr[:], in_=msk2[:], scalar=mr[:],
                                                    op=ALU.is_ge),
                   (f"xm{r_}_{t}",), f"xg{r_}_{t}")
                VE(lambda v: v.scalar_tensor_tensor(out=msk2[:], in0=gr[:], scalar=-1e9,
                                                    in1=msk2[:], op0=ALU.mult, op1=ALU.add),
                   (f"xg{r_}_{t}",), f"xs{r_}_{t}")
                prev = f"xs{r_}_{t}"
        VE(lambda v: v.tensor_single_scalar(out=gr[:], in_=msk[:], scalar=mr[:], op=ALU.is_ge),
           (f"xm3_{t}",), f"sel_{t}")
        VE(lambda v: v.tensor_tensor(out=tw[:], in0=gr[:], in1=scr[:], op=ALU.mult),
           (f"sel_{t}",), f"tw_{t}")
        VE(lambda v: v.reduce_sum(out=dn[:], in_=tw[:], axis=AX.X),
           (f"tw_{t}",), f"dn_{t}")
        VE(lambda v: v.tensor_single_scalar(out=dn[:], in_=dn[:], scalar=1e-20, op=ALU.add),
           (f"dn_{t}",), f"dn2_{t}")
        VE(lambda v: v.reciprocal(out=cs[:], in_=dn[:]),
           (f"dn2_{t}",), f"rc_{t}")
        VE(lambda v: v.tensor_scalar_mul(out=cs[:], in0=cs[:], scalar1=2.5),
           (f"rc_{t}",), f"cs_{t}")
        VE(lambda v, t=t: v.tensor_scalar_mul(out=comb[:, NE * t:NE * t + 16],
                                              in0=tw[:], scalar1=cs[:]),
           (f"cs_{t}",), f"cb_{t}")
        VE(lambda v, t=t: v.memset(comb[:, NE * t + 16:NE * t + 18], 1.0),
           (f"cb_{t}",), f"cb2_{t}")
        ev["scr_user"] = f"tw_{t}"
        ev["sc_user"] = f"mk3_{t}"

    for hp in range(2):
        tiles = list(range(8 * hp, 8 * hp + 8))
        for i8_ in range(8):
            P.emit("vector", lambda v, i8_=i8_: v.memset(racc[i8_][:], 0.0),
                   wait=(ev.get(f"racc_user{i8_}"),), inc="ve", event=f"rz_{hp}_{i8_}")
        for e in range(NE):
            b = e % 2
            for nm, buf, src, ntile in (("g", gw16, MGWQ, 8), ("u", uw16, MUWQ, 8),
                                        ("d", dw16, MDWQ, 4)):
                P.emit("sync", lambda g, nm=nm, buf=buf, src=src, b=b, e=e, ntile=ntile:
                       g.dma_start(
                           out=buf[b][:].rearrange("p (j i) -> p j i", j=ntile),
                           in_=src[(D if ntile == 8 else IM) * e:
                                   (D if ntile == 8 else IM) * (e + 1), :]
                               .rearrange("(j p) i -> p j i", p=128)),
                       wait=("mq_all", ev.get(f"ew_user_{nm}{b}"),),
                       inc=f"ew{b}", inc_by=16, event=f"ld{nm}_{hp}_{e}")
            P.events[f"ld_{hp}_{e}"] = P.events[f"ldd_{hp}_{e}"]
            def emit_gu(t):
                pp_ = t % 2
                for nm, buf, pp in (("g", gw16, p_g[pp_]), ("u", uw16, p_u[pp_])):
                    for j in range(8):
                        P.emit("tensor", lambda tt, nm=nm, buf=buf, pp=pp, j=j, b=b, t=t:
                               tt.matmul(out=pp[:],
                                         lhsT=x2fm[t][:, 128 * j:128 * j + 128],
                                         rhs=buf[b][:, 512 * j:512 * j + 512],
                                         start=(j == 0), stop=(j == 7)),
                               wait=(f"ld_{hp}_{e}", f"x2fm_{t}",
                                     ev.get(f"p{nm}_user{pp_}"),),
                               inc="pe" if j == 7 else None,
                               event=f"{nm}_{hp}_{e}_{t}" if j == 7 else None)
                P.emit("scalar", lambda s, pp_=pp_: s.activation(
                    out=gsl[pp_][:], in_=p_g[pp_][:], func=AF.Silu),
                    wait=(f"g_{hp}_{e}_{t}", ev.get(f"gsl_user{pp_}"),),
                    inc="se", event=f"sl_{hp}_{e}_{t}")
                ev[f"pg_user{pp_}"] = f"sl_{hp}_{e}_{t}"
                P.emit("vector", lambda v, pp_=pp_: v.tensor_tensor(
                    out=he[pp_][:], in0=gsl[pp_][:], in1=p_u[pp_][:], op=ALU.mult),
                    wait=(f"sl_{hp}_{e}_{t}", f"u_{hp}_{e}_{t}",
                          ev.get(f"he_user{pp_}"),),
                    inc="ve", event=f"he_{hp}_{e}_{t}")
                ev[f"pu_user{pp_}"] = f"he_{hp}_{e}_{t}"
                ev[f"gsl_user{pp_}"] = f"he_{hp}_{e}_{t}"
                P.emit("vector", lambda v, e=e, t=t, pp_=pp_: v.tensor_scalar_mul(
                    out=hes[pp_][:], in0=he[pp_][:],
                    scalar1=comb[:, NE * t + e:NE * t + e + 1]),
                    wait=(f"he_{hp}_{e}_{t}", f"cb2_{t}", ev.get(f"hes_user{pp_}"),),
                    inc="ve", event=f"hs_{hp}_{e}_{t}")
                ev[f"he_user{pp_}"] = f"hs_{hp}_{e}_{t}"

            def emit_tail(t):
                pp_ = t % 2
                i8_ = t % 8
                for jj in range(4):
                    P.emit("tensor", lambda tt, jj=jj, pp_=pp_:
                           tt.matmul(out=p_tr[1][:, 128 * jj:128 * jj + 128],
                                     lhsT=hes[pp_][:, 128 * jj:128 * jj + 128],
                                     rhs=eye[:], start=True, stop=True),
                           wait=(f"hs_{hp}_{e}_{t}", ev.get("ptrD_user1"),),
                           inc="pe" if jj == 3 else None,
                           event=f"ht_{hp}_{e}_{t}" if jj == 3 else None)
                P.emit("vector", lambda v, pp_=pp_: v.tensor_copy(
                    out=hesf[pp_][:], in_=p_tr[1][:]),
                    wait=(f"ht_{hp}_{e}_{t}", ev.get(f"hesf_user{pp_}"),),
                    inc="ve", event=f"hf_{hp}_{e}_{t}")
                ev["ptrD_user1"] = f"hf_{hp}_{e}_{t}"
                ev[f"hes_user{pp_}"] = f"ht_{hp}_{e}_{t}"
                for cc in range(4):
                    for half in range(2):
                        last = (cc == 3 and half == 1)
                        P.emit("tensor", lambda tt, cc=cc, half=half, b=b, pp_=pp_:
                               tt.matmul(out=p_d[:, 512 * half:512 * half + 512],
                                         lhsT=hesf[pp_][:, 128 * cc:128 * cc + 128],
                                         rhs=dw16[b][:, 1024 * cc + 512 * half:
                                                     1024 * cc + 512 * half + 512],
                                         start=(cc == 0), stop=(cc == 3)),
                               wait=(f"hf_{hp}_{e}_{t}",
                                     ev.get("pd_user") if cc == 0 else None,),
                               inc="pe" if last else None,
                               event=f"dn_{hp}_{e}_{t}" if last else None)
                P.emit("vector", lambda v, i8_=i8_: v.tensor_add(
                    out=racc[i8_][:], in0=racc[i8_][:], in1=p_d[:]),
                    wait=(f"dn_{hp}_{e}_{t}", f"rz_{hp}_{i8_}",
                          ev.get("racc_add"),),
                    inc="ve", event=f"ra_{hp}_{e}_{t}")
                ev["pd_user"] = f"ra_{hp}_{e}_{t}"
                ev["racc_add"] = f"ra_{hp}_{e}_{t}"
                ev[f"hesf_user{pp_}"] = f"dn_{hp}_{e}_{t}"

            for idx, t in enumerate(tiles):
                emit_gu(t)
                if idx > 0:
                    emit_tail(tiles[idx - 1])
            emit_tail(tiles[-1])
            for nm in "gud":
                ev[f"ew_user_{nm}{b}"] = f"dn_{hp}_{e}_{NE}" if False else f"dn_{hp}_{e}_{tiles[-1]}"
        for t in tiles:
            i8_ = t % 8
            ob = t % 2
            P.emit("sync", lambda s, t=t, ob=ob: s.dma_start(
                out=xres[ob][:], in_=X2[128 * t:128 * t + 128, :]),
                wait=(ev.get(f"xresD_user{ob}"),), inc="xr1", inc_by=16,
                event=f"rl_{t}")
            P.emit("vector", lambda v, i8_=i8_, ob=ob: v.scalar_tensor_tensor(
                out=oo[ob][:, 0:512], in0=racc[i8_][:, 0:512], scalar=1.0,
                in1=xres[ob][:, 0:512], op0=ALU.mult, op1=ALU.add,
                accum_out=sums[:]),
                wait=(f"ra_{hp}_{NE-1}_{t}", f"rl_{t}", ev.get(f"oo_user{ob}"),),
                inc="ve", event=f"e0_{t}")
            P.emit("vector", lambda v, i8_=i8_, ob=ob: v.scalar_tensor_tensor(
                out=oo[ob][:, 512:1024], in0=racc[i8_][:, 512:1024], scalar=1.0,
                in1=xres[ob][:, 512:1024], op0=ALU.mult, op1=ALU.add,
                accum_out=sums2[:]),
                wait=(f"e0_{t}",), inc="ve", event=f"e1_{t}")
            ev[f"xresD_user{ob}"] = f"e1_{t}"
            ev[f"racc_user{i8_}"] = f"e1_{t}"
            P.emit("vector", lambda v: v.tensor_add(out=sums[:], in0=sums[:], in1=sums2[:]),
                   wait=(f"e1_{t}",), inc="ve", event=f"es_{t}")
            P.emit("vector", lambda v: v.tensor_scalar_mul(out=mean[:], in0=sums[:],
                                                           scalar1=1.0 / D),
                   wait=(f"es_{t}",), inc="ve", event=f"em_{t}")
            P.emit("vector", lambda v: v.tensor_scalar_mul(out=mb[:], in0=mean[:],
                                                           scalar1=-1.0),
                   wait=(f"em_{t}", ev.get("mbD_user"),), inc="ve", event=f"eb_{t}")
            P.emit("scalar", lambda s, ob=ob: s.activation(
                out=junk[:], in_=oo[ob][:], func=AF.Square, bias=mb[:], scale=1.0,
                accum_out=sumsq[:]),
                wait=(f"eb_{t}", ev.get("sumsqD_user"),), inc="se", event=f"eq_{t}")
            P.emit("vector", lambda v: v.tensor_scalar(
                out=var[:], in0=sumsq[:], scalar1=1.0 / D, scalar2=1e-5,
                op0=ALU.mult, op1=ALU.add),
                wait=(f"eq_{t}",), inc="ve", event=f"evr_{t}")
            ev["sumsqD_user"] = f"evr_{t}"
            P.emit("scalar", lambda s: s.activation(out=var[:], in_=var[:], func=AF.Sqrt),
                   wait=(f"evr_{t}",), inc="se", event=f"ed_{t}")
            P.emit("vector", lambda v: v.reciprocal(out=rstd[:], in_=var[:]),
                   wait=(f"ed_{t}", ev.get("rstdD_user"),), inc="ve", event=f"er_{t}")
            P.emit("vector", lambda v: v.tensor_tensor(out=mb[:], in0=mb[:],
                                                       in1=rstd[:], op=ALU.mult),
                   wait=(f"er_{t}",), inc="ve", event=f"embD_{t}")
            ev["mbD_user"] = f"embD_{t}"
            P.emit("scalar", lambda s, ob=ob: s.activation(
                out=oo[ob][:], in_=oo[ob][:], func=AF.Identity, bias=mb[:],
                scale=rstd[:]),
                wait=(f"embD_{t}",), inc="se", event=f"elnD_{t}")
            ev["mbD_user"] = f"elnD_{t}"
            ev["rstdD_user"] = f"elnD_{t}"
            P.emit("sync", lambda s, t=t, ob=ob: s.dma_start(
                out=OUT[128 * t:128 * t + 128, :], in_=oo[ob][:]),
                wait=(f"elnD_{t}",), inc=f"so{ob}", inc_by=16, event=f"sx_{t}")
            ev[f"oo_user{ob}"] = f"sx_{t}"

    ctx.close()


def build_D():
    nc = bass.Bass("TRN2", target_bir_lowering=False, debug=False)
    dram = _mk_dram(nc)
    ctx = ExitStack()
    sems = _mk_sems(nc, ctx)(ALL_SEMS)
    P = Plan()
    emit_precastD(nc, PhasePlan(P, "D."), dram)
    emit_D(nc, PhasePlan(P, "D."), dram, "ExternalInput")
    P.finish_dma()
    with nc.Block() as block:
        P.emit_into(block, sems)
    ctx.close()
    return nc


def build_fused():
    nc = bass.Bass("TRN2", target_bir_lowering=False, debug=False)
    dram = _mk_dram(nc)
    ctx = ExitStack()
    sems = _mk_sems(nc, ctx)(ALL_SEMS)
    P = Plan()
    bg = precastD_jobs(nc, dram)
    emit_A(nc, PhasePlan(P, "A."), dram, "Internal", bg=bg)
    for jb in bg:
        P.emit("gpsimd", jb, inc="mq", inc_by=16)
    P.events["D.mq_all"] = ("mq", P.count["mq"])
    P.barrier("AB")
    emit_B(nc, PhasePlan(P, "B."), dram, "Internal", "Internal")
    P.barrier("BC")
    emit_C(nc, PhasePlan(P, "C."), dram, "Internal", "Internal")
    P.barrier("CD")
    emit_D(nc, PhasePlan(P, "D."), dram, "Internal")
    P.finish_dma()
    with nc.Block() as block:
        P.emit_into(block, sems)
    ctx.close()
    return nc


def host_prep_D(inputs):
    gw = np.asarray(inputs["L1_ffn_gate_w"], np.float32)
    uw = np.asarray(inputs["L1_ffn_up_w"], np.float32)
    dw = np.asarray(inputs["L1_ffn_down_w"], np.float32)
    shg = np.asarray(inputs["L1_ffn_sh_gate_w"], np.float32)
    shu = np.asarray(inputs["L1_ffn_sh_up_w"], np.float32)
    shd = np.asarray(inputs["L1_ffn_sh_down_w"], np.float32)
    mgw = np.concatenate([gw[e].T for e in range(16)]
                         + [shg[0:512].T, shg[512:1024].T], axis=0)
    muw = np.concatenate([uw[e].T for e in range(16)]
                         + [shu[0:512].T, shu[512:1024].T], axis=0)
    mdw = np.concatenate([dw[e].T for e in range(16)]
                         + [shd[:, 0:512].T, shd[:, 512:1024].T], axis=0)
    rT = np.ascontiguousarray(np.asarray(inputs["L1_ffn_router_weight"], np.float32).T)
    eb = np.ascontiguousarray(np.asarray(inputs["L1_ffn_e_bias"], np.float32)[None, :])
    tb = ((15 - np.arange(16)) * 1e-9).astype(np.float32)[None, :]
    ones = np.ones((1, 128), np.float32)
    eye = np.eye(128, dtype=np.float32)
    m = {"x2": None, "routerT": rT, "ebias": eb, "tbreak": np.ascontiguousarray(tb),
         "mgw": np.ascontiguousarray(mgw), "muw": np.ascontiguousarray(muw),
         "mdw": np.ascontiguousarray(mdw), "ones_row": ones, "eye128": eye}
    return [dict(m) for _ in range(NCORES)]

_NC_CACHE = {}


def _built(name, fn):
    if name not in _NC_CACHE:
        _NC_CACHE[name] = fn()
    return _NC_CACHE[name]


def kernel(**inputs):
    nc = _built("F", build_fused)
    core_ids = list(range(NCORES))
    maps = []
    for c in range(NCORES):
        m = {}
        for prep in (host_prep_A, host_prep_B, host_prep_C, host_prep_D):
            mm = prep(inputs)[c]
            m.update({k: v for k, v in mm.items() if v is not None})
        maps.append(m)
    res = run_bass_kernel_spmd(nc, maps, core_ids=core_ids)
    out = np.empty((4, 4096, D), np.float32)
    for c in range(NCORES):
        seq, half = c // 2, c % 2
        out[seq, 2048 * half:2048 * (half + 1)] = res.results[c]["out"]
    return out


# version 11
# speedup vs baseline: 1.0853x; 1.0853x over previous
import numpy as np
from contextlib import ExitStack

import concourse.bass as bass
import concourse.mybir as mybir
from concourse.bass_utils import run_bass_kernel_spmd

F32 = mybir.dt.float32
BF16 = mybir.dt.bfloat16
I8 = mybir.dt.int8
AF = mybir.ActivationFunctionType
ALU = mybir.AluOpType
AX = mybir.AxisListType

NCORES = 8
D = 1024
NH = 16
DH = 64
NMAIN = 22
NPREF = 13
ROWS = NMAIN * 128
PREF_ROWS = NPREF * 128
NFLG = NPREF + NMAIN


class Plan:
    ENGINES = ("sync", "tensor", "vector", "scalar", "gpsimd")

    def __init__(self):
        self.ops = {e: [] for e in self.ENGINES}
        self.count = {}
        self.events = {}
        self.sem_owner = {}

    def emit(self, eng, fn, wait=(), inc=None, event=None, inc_by=1):
        wmax = {}
        for ev_name in wait:
            if ev_name is None:
                continue
            sem, cnt = self.events[ev_name]
            wmax[sem] = max(wmax.get(sem, 0), cnt)
        waits = sorted(wmax.items())
        self.ops[eng].append((fn, waits, inc, inc_by))
        if inc is not None:
            owner = self.sem_owner.setdefault(inc, eng)
            assert owner == eng, f"sem {inc} inc'd from {eng} and {owner}"
            self.count[inc] = self.count.get(inc, 0) + inc_by
            if event is not None:
                self.events[event] = (inc, self.count[inc])
        else:
            assert event is None

    def finish_dma(self, eng="sync"):
        waits = []
        for sem, total in sorted(self.count.items()):
            owner = self.sem_owner.get(sem)
            if owner in ("sync", "gpsimd", "scalar") and total % 16 == 0 and total > 0:
                ename = f"_fin_{sem}"
                self.events[ename] = (sem, total)
                waits.append(ename)
        self.emit(eng, lambda e: None, wait=tuple(waits))

    def emit_into(self, block, sems):
        def mk_body(oplist):
            def body(eng):
                last_wait = {}
                for fn, waits, inc, inc_by in oplist:
                    for sem_name, cnt in waits:
                        if last_wait.get(sem_name, -1) >= cnt:
                            continue
                        last_wait[sem_name] = cnt
                        eng.wait_ge(sems[sem_name], cnt)
                    inst = fn(eng)
                    if inc is not None:
                        inst.then_inc(sems[inc], inc_by)
            return body

        for eng_name in self.ENGINES:
            if self.ops[eng_name]:
                getattr(block, eng_name)(mk_body(self.ops[eng_name]))

    def barrier(self, tag, exclude=("mq",)):
        names = []
        for sem, total in sorted(self.count.items()):
            if sem in exclude:
                continue
            en = f"__bar_{tag}_{sem}"
            self.events[en] = (sem, total)
            names.append(en)
        engines_used = [e for e in self.ENGINES if self.ops[e]]
        for eng in self.ENGINES:
            self.emit(eng, lambda e: None, wait=tuple(names))


class PhasePlan:

    def __init__(self, plan, pfx):
        self.plan = plan
        self.pfx = pfx

    @property
    def count(self):
        return self.plan.count

    @property
    def events(self):
        return _PfxEvents(self.plan.events, self.pfx)

    def emit(self, eng, fn, wait=(), inc=None, event=None, inc_by=1):
        w = tuple(None if x is None else self.pfx + x for x in wait)
        e = None if event is None else self.pfx + event
        self.plan.emit(eng, fn, wait=w, inc=inc, event=e, inc_by=inc_by)

    def finish_dma(self, eng="sync"):
        self.plan.finish_dma(eng)


class _PfxEvents:
    def __init__(self, events, pfx):
        self._e = events
        self._p = pfx

    def __getitem__(self, k):
        return self._e[self._p + k]

    def __setitem__(self, k, v):
        self._e[self._p + k] = v

    def __contains__(self, k):
        return (self._p + k) in self._e


def emit_A(nc, P, dram, stage_kind, dbg_chunk=None, bg=None):
    X = dram("x_in", [ROWS, D], "ExternalInput")
    XP = dram("x_prefix", [PREF_ROWS, D], "ExternalInput")
    WT = {k: dram(f"w{k}T", [D, D], "ExternalInput") for k in "qkvo"}
    BR = {k: dram(f"b{k}", [1, D], "ExternalInput") for k in "qkvo"}
    ONES = dram("ones_row", [1, 128], "ExternalInput")
    EYE = dram("eye128", [128, 128], "ExternalInput")
    CMASK = dram("causal_mask", [128, 128], "ExternalInput")
    FLG = dram("state_flags", [128, NFLG], "ExternalInput")
    XLN = dram("xln", [ROWS, D], stage_kind)
    DBG = {}
    if dbg_chunk is not None:
        for nm in ("d_xf", "d_r", "d_S32", "d_ln"):
            DBG[nm] = dram(nm, [128, D], "ExternalOutput")
        DBG["d_stats"] = dram("d_stats", [128, 8], "ExternalOutput")

    ctx = ExitStack()
    sb = lambda name, shape, dt: ctx.enter_context(nc.sbuf_tensor(name + '_pA', shape, dt))
    ps = lambda name, shape: ctx.enter_context(nc.psum_tensor(name + '_pA', shape, F32))

    w16 = {k: sb(f"w16_{k}", [128, 8 * D], BF16) for k in "qkvo"}
    brow = {k: sb(f"brow_{k}", [1, D], F32) for k in "qkvo"}
    ones = sb("ones", [1, 128], F32)
    eye = sb("eye", [128, 128], BF16)
    cmask = sb("cmask", [128, 128], F32)
    flags = sb("flags", [128, NFLG], F32)
    xf = [sb(f"xf{i}", [128, D], F32) for i in range(2)]
    x16 = [sb(f"x16_{i}", [128, D], BF16) for i in range(2)]
    xfm = [sb(f"xfm{i}", [128, D], BF16) for i in range(2)]
    q_tm = [sb(f"q_tm{i}", [128, D], BF16) for i in range(2)]
    k_tm = [sb(f"k_tm{i}", [128, D], BF16) for i in range(2)]
    v_tm = [sb(f"v_tm{i}", [128, D], BF16) for i in range(2)]
    qfm = [sb(f"qfm{i}", [128, D], BF16) for i in range(2)]
    kfm = [sb(f"kfm{i}", [128, D], BF16) for i in range(2)]
    scm = [sb(f"scm{i}", [128, 128], BF16) for i in range(2)]
    att16 = [sb(f"att16{i}", [128, D], BF16) for i in range(2)]
    attfm = [sb(f"attfm{i}", [128, D], BF16) for i in range(2)]
    S32 = sb("S32", [128, 512], F32)
    S16 = sb("S16", [128, 512], BF16)
    xln_sb = [sb(f"xlnsb{i}", [128, D], F32) for i in range(2)]
    junk = sb("junk", [128, D], F32)
    sums = sb("sums", [128, 1], F32)
    mean = sb("mean", [128, 1], F32)
    sumsq = sb("sumsq", [128, 1], F32)
    var = sb("var", [128, 1], F32)
    rstd = sb("rstd", [128, 1], F32)
    mb = sb("mb", [128, 1], F32)

    p_tr = [ps(f"p_tr{i}", [128, 512]) for i in range(2)]
    p_proj = [ps(f"p_proj{i}", [128, 512]) for i in range(2)]
    p_sc = [ps(f"p_sc{i}", [128, 512]) for i in range(2)]
    p_att = ps("p_att", [128, 512])
    p_st = ps("p_st", [64, 512])

    ev = {}

    for k in "qkvo":
        P.emit("gpsimd",
               lambda g, k=k: g.dma_start(
                   out=w16[k][:].rearrange("p (j e) -> p j e", j=8),
                   in_=WT[k].rearrange("(j p) e -> p j e", p=128)),
               inc="wl", inc_by=16, event=f"w_{k}")
    P.emit("gpsimd", lambda g: g.dma_start(out=eye[:], in_=EYE[:, :]),
           inc="wl", inc_by=16, event="eye")
    P.emit("sync", lambda s: s.dma_start(out=cmask[:], in_=CMASK[:, :]),
           inc="dx", inc_by=16, event="cmask")
    for k in "qkvo":
        P.emit("sync", lambda s, k=k: s.dma_start(out=brow[k][:], in_=BR[k][:, :]),
               inc="dx", inc_by=16, event=f"b_{k}")
    P.emit("sync", lambda s: s.dma_start(out=ones[:], in_=ONES[:, :]),
           inc="dx", inc_by=16, event="ones")
    P.emit("sync", lambda s: s.dma_start(out=flags[:], in_=FLG[:, :]),
           inc="dx", inc_by=16, event="flags")
    P.events["wsetup"] = ("wl", P.count.get("wl", 0))
    P.events["dsetup"] = ("dx", P.count.get("dx", 0))
    P.emit("vector", lambda v: v.memset(S32[:], 0.0), inc="ve", event="S32_z")
    P.emit("vector", lambda v: v.memset(S16[:], 0.0), inc="ve", event="S16_ref")

    def transpose8(tag, src16, src_ev, dst, dst_free_ev=None, n=8):
        nb = (n + 3) // 4
        last_evac = None
        for b in range(nb):
            lo = 4 * b
            hi = min(lo + 4, n)
            for jj, j in enumerate(range(lo, hi)):
                P.emit("tensor",
                       lambda t, j=j, jj=jj, b=b, src16=src16:
                       t.matmul(out=p_tr[b % 2][:, 128 * jj:128 * jj + 128],
                                lhsT=src16[:, 128 * j:128 * j + 128],
                                rhs=eye[:], start=True, stop=True),
                       wait=(src_ev, "wsetup", ev.get(f"ptr_free{b % 2}")),
                       inc="pe" if j == hi - 1 else None,
                       event=f"{tag}_tr{b}" if j == hi - 1 else None)
            P.emit("vector",
                   lambda v, b=b, lo=lo, hi=hi, dst=dst:
                   v.tensor_copy(out=dst[:, 128 * lo:128 * hi],
                                 in_=p_tr[b % 2][:, 0:128 * (hi - lo)]),
                   wait=(f"{tag}_tr{b}", dst_free_ev),
                   inc="ve", event=f"{tag}_ev{b}")
            ev[f"ptr_free{b % 2}"] = f"{tag}_ev{b}"
            last_evac = f"{tag}_ev{b}"
        P.events[f"{tag}_done"] = P.events[last_evac]

    def proj(tag, w_key, src_fm, src_ev, dst_tm, dst_free_ev=None, bias=True,
             evac=True):
        for half in range(2):
            if bias:
                P.emit("tensor",
                       lambda t, half=half, w_key=w_key:
                       t.matmul(out=p_proj[half][:],
                                lhsT=ones[:],
                                rhs=brow[w_key][:, 512 * half:512 * half + 512],
                                start=True, stop=False, skip_group_check=True),
                       wait=(src_ev, "wsetup", "dsetup",
                             ev.get(f"pproj_free{half}")))
            for j in range(8):
                P.emit("tensor",
                       lambda t, j=j, half=half, src_fm=src_fm, w_key=w_key, bias=bias:
                       t.matmul(out=p_proj[half][:],
                                lhsT=src_fm[:, 128 * j:128 * j + 128],
                                rhs=w16[w_key][:, 1024 * j + 512 * half:
                                               1024 * j + 512 * half + 512],
                                start=(j == 0 and not bias), stop=(j == 7),
                                skip_group_check=bias),
                       wait=(src_ev, "wsetup", "dsetup",
                             ev.get(f"pproj_free{half}")),
                       inc=("pe" if j == 7 else None),
                       event=(f"{tag}_mm{half}" if j == 7 else None))
            if evac:
                P.emit("vector",
                       lambda v, half=half, dst_tm=dst_tm:
                       v.tensor_copy(out=dst_tm[:, 512 * half:512 * half + 512],
                                     in_=p_proj[half][:]),
                       wait=(f"{tag}_mm{half}", dst_free_ev),
                       inc="ve", event=f"{tag}_ev{half}")
                ev[f"pproj_free{half}"] = f"{tag}_ev{half}"
        if evac:
            P.events[f"{tag}_done"] = P.events[f"{tag}_ev1"]

    def state_half(idx, rnd, kv_ev, buf):
        h0 = 8 * rnd
        for i in range(8):
            h = h0 + i
            P.emit("tensor",
                   lambda t, h=h, i=i, buf=buf:
                   t.matmul(out=p_st[:, 64 * i:64 * i + 64],
                            lhsT=k_tm[buf][:, 64 * h:64 * h + 64],
                            rhs=v_tm[buf][:, 64 * h:64 * h + 64],
                            start=True, stop=True),
                   wait=(kv_ev, ev.get("pst_free")),
                   inc="pe" if i == 7 else None,
                   event=f"stmm_{idx}_{rnd}" if i == 7 else None)
        for i in range(8):
            h = h0 + i
            base = (h % 2) * 64
            col = (h // 2) * 64
            P.emit("vector",
                   lambda v, i=i, base=base, col=col, idx=idx:
                   v.scalar_tensor_tensor(out=S32[base:base + 64, col:col + 64],
                                          in0=p_st[:, 64 * i:64 * i + 64],
                                          scalar=flags[base:base + 64, idx:idx + 1],
                                          in1=S32[base:base + 64, col:col + 64],
                                          op0=ALU.mult, op1=ALU.add),
                   wait=(f"stmm_{idx}_{rnd}", "dsetup", "S32_z"),
                   inc="ve" if i == 7 else None,
                   event=f"stupd_{idx}_{rnd}" if i == 7 else None)
        ev["pst_free"] = f"stupd_{idx}_{rnd}"

    for p in range(NPREF):
        buf = p % 2
        P.emit("gpsimd",
               lambda g, p=p, buf=buf: g.dma_start(
                   out=x16[buf][:], in_=XP[128 * p:128 * p + 128, :]),
               wait=(ev.get(f"x16_free{buf}"),),
               inc=f"xg{buf}", inc_by=16, event=f"px16_{p}")
        transpose8(f"pxfm{p}", x16[buf], f"px16_{p}", xfm[buf],
                   dst_free_ev=ev.get(f"xfm_free{buf}"))
        ev[f"x16_free{buf}"] = f"pxfm{p}_tr1"
        proj(f"pk{p}", "k", xfm[buf], f"pxfm{p}_done", k_tm[buf],
             dst_free_ev=ev.get(f"ktm_free{buf}"))
        proj(f"pv{p}", "v", xfm[buf], f"pxfm{p}_done", v_tm[buf],
             dst_free_ev=ev.get(f"vtm_free{buf}"))
        ev[f"xfm_free{buf}"] = f"pv{p}_mm1"
        state_half(p, 0, f"pv{p}_done", buf)
        state_half(p, 1, f"pv{p}_done", buf)
        P.emit("vector", lambda v: v.tensor_copy(out=S16[:], in_=S32[:]),
               wait=(f"stupd_{p}_1",), inc="ve", event="S16_ref")
        ev[f"ktm_free{buf}"] = f"stmm_{p}_1"
        ev[f"vtm_free{buf}"] = f"stmm_{p}_1"

    for c in range(NMAIN):
        buf = c % 2
        fi = NPREF + c
        P.emit("sync",
               lambda s, c=c, buf=buf: s.dma_start(
                   out=xf[buf][:], in_=X[128 * c:128 * c + 128, :]),
               wait=(ev.get(f"xf_free{buf}"),),
               inc=f"xf{buf}", inc_by=16, event=f"xf_{c}")
        P.emit("gpsimd",
               lambda g, c=c, buf=buf: g.dma_start(
                   out=x16[buf][:], in_=X[128 * c:128 * c + 128, :]),
               wait=(ev.get(f"x16_free{buf}"),),
               inc=f"xg{buf}", inc_by=16, event=f"x16_{c}")
        if bg:
            for _ in range(2):
                if bg:
                    P.plan.emit("gpsimd", bg.pop(0), inc="mq", inc_by=16)
        transpose8(f"xfm{c}", x16[buf], f"x16_{c}", xfm[buf],
                   dst_free_ev=ev.get(f"xfm_free{buf}"))
        ev[f"x16_free{buf}"] = f"xfm{c}_tr1"
        proj(f"q{c}", "q", xfm[buf], f"xfm{c}_done", q_tm[buf],
             dst_free_ev=ev.get(f"qtm_free{buf}"))
        proj(f"k{c}", "k", xfm[buf], f"xfm{c}_done", k_tm[buf],
             dst_free_ev=ev.get(f"ktm_free{buf}"))
        proj(f"v{c}", "v", xfm[buf], f"xfm{c}_done", v_tm[buf],
             dst_free_ev=ev.get(f"vtm_free{buf}"))
        ev[f"xfm_free{buf}"] = f"v{c}_mm1"
        transpose8(f"qfm{c}", q_tm[buf], f"q{c}_done", qfm[buf],
                   dst_free_ev=ev.get(f"qfm_free{buf}"))
        transpose8(f"kfm{c}", k_tm[buf], f"k{c}_done", kfm[buf],
                   dst_free_ev=ev.get(f"kfm_free{buf}"))
        def emit_sc(h):
            base = (h % 2) * 64
            jb = (h // 2) * 128
            si = h % 2
            P.emit("tensor",
                   lambda t, base=base, jb=jb, si=si, buf=buf:
                   t.matmul(out=p_sc[si][:, 0:128],
                            lhsT=kfm[buf][base:base + 64, jb:jb + 128],
                            rhs=qfm[buf][base:base + 64, jb:jb + 128],
                            start=True, stop=True),
                   wait=(f"qfm{c}_done", f"kfm{c}_done",
                         ev.get(f"psc_free{si}")),
                   inc="pe", event=f"sc_{c}_{h}")
            P.emit("vector",
                   lambda v, si=si:
                   v.tensor_tensor(out=scm[si][:], in0=p_sc[si][:, 0:128],
                                   in1=cmask[:], op=ALU.mult),
                   wait=(f"sc_{c}_{h}", "dsetup", ev.get(f"scm_free{si}")),
                   inc="ve", event=f"scm_{c}_{h}")
            ev[f"psc_free{si}"] = f"scm_{c}_{h}"

        def emit_av(h):
            base = (h % 2) * 64
            si = h % 2
            i = h % 8
            P.emit("tensor",
                   lambda t, h=h, i=i, base=base, buf=buf:
                   t.matmul(out=p_att[:, 64 * i:64 * i + 64],
                            lhsT=qfm[buf][base:base + 64, (h // 2) * 128:(h // 2) * 128 + 128],
                            rhs=S16[base:base + 64, (h // 2) * 64:(h // 2) * 64 + 64],
                            start=True, stop=False),
                   wait=("S16_ref", f"qfm{c}_done", ev.get("patt_free")))
            P.emit("tensor",
                   lambda t, h=h, i=i, si=si, buf=buf:
                   t.matmul(out=p_att[:, 64 * i:64 * i + 64],
                            lhsT=scm[si][:],
                            rhs=v_tm[buf][:, 64 * h:64 * h + 64],
                            start=False, stop=True),
                   wait=(f"scm_{c}_{h}", f"v{c}_done"),
                   inc="pe", event=f"att_{c}_{h}")
            ev[f"scm_free{si}"] = f"att_{c}_{h}"

        for rnd in range(2):
            h0 = 8 * rnd
            emit_sc(h0)
            for i in range(8):
                h = h0 + i
                if i + 1 < 8:
                    emit_sc(h + 1)
                emit_av(h)
            P.emit("vector",
                   lambda v, rnd=rnd, buf=buf:
                   v.tensor_copy(out=att16[buf][:, 512 * rnd:512 * rnd + 512],
                                 in_=p_att[:, 0:512]),
                   wait=(f"att_{c}_{8 * rnd + 7}", ev.get(f"att16_free{buf}")),
                   inc="ve", event=f"att16_{c}_{rnd}")
            ev["patt_free"] = f"att16_{c}_{rnd}"
        state_half(fi, 0, f"v{c}_done", buf)
        state_half(fi, 1, f"v{c}_done", buf)
        P.emit("vector", lambda v: v.tensor_copy(out=S16[:], in_=S32[:]),
               wait=(f"stupd_{fi}_1", f"att_{c}_{NH - 1}",),
               inc="ve", event="S16_ref")
        ev[f"qtm_free{buf}"] = f"qfm{c}_tr1"
        ev[f"ktm_free{buf}"] = f"stmm_{fi}_1"
        ev[f"vtm_free{buf}"] = f"stmm_{fi}_1"
        ev[f"qfm_free{buf}"] = f"att_{c}_{NH - 1}"
        ev[f"kfm_free{buf}"] = f"sc_{c}_{NH - 1}"
        transpose8(f"afm{c}", att16[buf], f"att16_{c}_1", attfm[buf],
                   dst_free_ev=ev.get(f"attfm_free{buf}"))
        ev[f"att16_free{buf}"] = f"afm{c}_tr1"
        proj(f"o{c}", "o", attfm[buf], f"afm{c}_done", None, bias=True, evac=False)
        ev[f"attfm_free{buf}"] = f"o{c}_mm1"
        P.emit("vector",
               lambda v, buf=buf:
               v.scalar_tensor_tensor(out=xln_sb[buf][:, 0:512], in0=p_proj[0][:],
                                      scalar=1.0, in1=xf[buf][:, 0:512],
                                      op0=ALU.mult, op1=ALU.add,
                                      accum_out=sums[:]),
               wait=(f"o{c}_mm0", f"xf_{c}", ev.get(f"xlnsb_free{buf}")),
               inc="ve", event=f"r0_{c}")
        P.emit("vector",
               lambda v, buf=buf:
               v.scalar_tensor_tensor(out=xln_sb[buf][:, 512:1024], in0=p_proj[1][:],
                                      scalar=1.0, in1=xf[buf][:, 512:1024],
                                      op0=ALU.mult, op1=ALU.add,
                                      accum_out=mean[:]),
               wait=(f"o{c}_mm1",),
               inc="ve", event=f"r_{c}")
        ev[f"xf_free{buf}"] = f"r_{c}"
        ev["pproj_free0"] = f"r0_{c}"
        ev["pproj_free1"] = f"r_{c}"
        P.emit("vector",
               lambda v: v.tensor_add(out=sums[:], in0=sums[:], in1=mean[:]),
               wait=(f"r_{c}",), inc="ve", event=f"sums_{c}")
        P.emit("vector",
               lambda v: v.tensor_scalar_mul(out=mean[:], in0=sums[:], scalar1=1.0 / D),
               wait=(f"sums_{c}",), inc="ve", event=f"mean_{c}")
        P.emit("vector",
               lambda v: v.tensor_scalar_mul(out=mb[:], in0=mean[:], scalar1=-1.0),
               wait=(f"mean_{c}", ev.get("mb_free")), inc="ve", event=f"mb0_{c}")
        P.emit("scalar",
               lambda s, buf=buf: s.activation(out=junk[:], in_=xln_sb[buf][:],
                                               func=AF.Square, bias=mb[:], scale=1.0,
                                               accum_out=sumsq[:]),
               wait=(f"mb0_{c}", ev.get("sumsq_free")), inc="se", event=f"sq_{c}")
        P.emit("vector",
               lambda v: v.tensor_scalar(out=var[:], in0=sumsq[:],
                                         scalar1=1.0 / D, scalar2=1e-5,
                                         op0=ALU.mult, op1=ALU.add),
               wait=(f"sq_{c}",), inc="ve", event=f"var_{c}")
        ev["sumsq_free"] = f"var_{c}"
        P.emit("scalar",
               lambda s: s.activation(out=var[:], in_=var[:], func=AF.Sqrt),
               wait=(f"var_{c}",), inc="se", event=f"sd_{c}")
        P.emit("vector", lambda v: v.reciprocal(out=rstd[:], in_=var[:]),
               wait=(f"sd_{c}", ev.get("rstd_free")), inc="ve", event=f"rstd_{c}")
        P.emit("vector",
               lambda v: v.tensor_tensor(out=mb[:], in0=mb[:], in1=rstd[:], op=ALU.mult),
               wait=(f"rstd_{c}",), inc="ve", event=f"mb_{c}")
        P.emit("scalar",
               lambda s, buf=buf: s.activation(out=xln_sb[buf][:], in_=xln_sb[buf][:],
                                               func=AF.Identity, bias=mb[:],
                                               scale=rstd[:]),
               wait=(f"mb_{c}",), inc="se", event=f"xln_{c}")
        ev["mb_free"] = f"xln_{c}"
        ev["rstd_free"] = f"xln_{c}"
        P.emit("sync",
               lambda s, c=c, buf=buf: s.dma_start(out=XLN[128 * c:128 * c + 128, :],
                                                   in_=xln_sb[buf][:]),
               wait=(f"xln_{c}",), inc=f"st{buf}", inc_by=16, event=f"store_{c}")
        ev[f"xlnsb_free{buf}"] = f"store_{c}"

    ctx.close()


def _mk_dram(nc, stagekinds=()):
    cache = {}

    def dram(name, shape, kind):
        if name not in cache:
            dt = BF16 if (name == "w2q_scratch" or kind == "InternalB16") else F32
            k = "Internal" if kind == "InternalB16" else kind
            cache[name] = nc.dram_tensor(name, shape, dt, kind=k).ap()
        return cache[name]
    return dram


def _mk_sems(nc, ctx):
    cache = {}

    def get(names):
        out = {}
        for n in names:
            if n not in cache:
                cache[n] = ctx.enter_context(nc.semaphore(n))
            out[n] = cache[n]
        return cache
    return get


ALL_SEMS = ("dx", "pe", "ve", "se", "ge", "wl", "xg0", "xg1", "xf0", "xf1",
            "st0", "st1", "sdb", "ws0", "ws1", "xq0", "xq1", "xr0", "xr1",
            "so0", "so1", "w2s0", "w2s1", "w2l0", "w2l1", "xg", "ew0", "ew1",
            "mq")


def build_A(dbg_chunk=None):
    nc = bass.Bass("TRN2", target_bir_lowering=False, debug=False)
    dram = _mk_dram(nc)
    ctx = ExitStack()
    sems = _mk_sems(nc, ctx)(ALL_SEMS)
    P = Plan()
    emit_A(nc, PhasePlan(P, "A."), dram, "ExternalOutput", dbg_chunk=dbg_chunk)
    P.finish_dma()
    with nc.Block() as block:
        P.emit_into(block, sems)
    ctx.close()
    return nc


def host_prep_A(inputs):
    x = np.ascontiguousarray(np.asarray(inputs["L0_attn_x"], dtype=np.float32))
    wT = {k: np.ascontiguousarray(np.asarray(inputs[f"L0_attn_w{k}_w"], np.float32).T)
          for k in "qkvo"}
    b = {k: np.ascontiguousarray(np.asarray(inputs[f"L0_attn_w{k}_b"], np.float32)[None, :])
         for k in "qkvo"}
    ones = np.ones((1, 128), np.float32)
    eye = np.eye(128, dtype=np.float32)
    cm = np.triu(np.ones((128, 128), np.float32))
    in_maps = []
    for c in range(NCORES):
        seq, half = c // 2, c % 2
        xs = x[seq]
        if half == 0:
            pre = np.zeros((384, D), np.float32)
            own = xs[0:2048]
            post = xs[2048:2432]
            xp = np.zeros((PREF_ROWS, D), np.float32)
            sf = np.array([0.0] * NPREF + [0.0] * 3 + [1.0] * 19, np.float32)
        else:
            pre = xs[1664:2048]
            own = xs[2048:4096]
            post = np.zeros((384, D), np.float32)
            xp = np.ascontiguousarray(xs[0:PREF_ROWS])
            sf = np.array([1.0] * NPREF + [1.0] * 19 + [0.0] * 3, np.float32)
        x_in = np.ascontiguousarray(np.concatenate([pre, own, post], axis=0))
        flg = np.broadcast_to(sf[None, :], (128, NFLG)).copy()
        m = {"x_in": x_in, "x_prefix": xp,
             "wqT": wT["q"], "wkT": wT["k"], "wvT": wT["v"], "woT": wT["o"],
             "bq": b["q"], "bk": b["k"], "bv": b["v"], "bo": b["o"],
             "ones_row": ones, "eye128": eye, "causal_mask": cm,
             "state_flags": flg}
        in_maps.append(m)
    return in_maps



DH1 = 4096
NFC = 32
TT_B = 384
OUT_LO, OUT_HI = 128, 2688


def emit_B(nc, P, dram, in_kind, out_kind, dbg=False):
    XLN = dram("xln", [ROWS, D], in_kind)
    W1T = dram("w1T", [D, DH1], "ExternalInput")
    W2T = dram("w2T", [DH1, D], "ExternalInput")
    TAPS = dram("taps", [128, 96], "ExternalInput")
    CONVB = dram("convb", [128, NFC], "ExternalInput")
    CPAD = dram("cpad", [128, 2], "ExternalInput")
    ONES = dram("ones_row", [1, 128], "ExternalInput")
    ONESC = dram("ones_col", [128, 1], "ExternalInput")
    EYE = dram("eye128", [128, 128], "ExternalInput")
    X1 = dram("x1", [OUT_HI - OUT_LO, D], out_kind)
    W2Q = dram("w2q_scratch", [DH1, D], "Internal")
    DBG = {}

    ctx = ExitStack()
    sb = lambda name, shape, dt: ctx.enter_context(nc.sbuf_tensor(name + '_pB', shape, dt))
    ps = lambda name, shape: ctx.enter_context(nc.psum_tensor(name + '_pB', shape, F32))

    w1q = sb("w1q", [128, 8 * DH1], BF16)
    w2qb = [sb(f"w2qb{i}", [128, D], BF16) for i in range(2)]
    wstage = [sb(f"wstage{i}", [128, DH1], F32) for i in range(2)]
    wi8 = sb("wi8", [128, DH1], I8)
    xi8 = sb("xi8", [128, NMAIN * D], I8)
    dfac = sb("dfac", [128, NMAIN], F32)
    s127 = sb("s127", [128, NMAIN], F32)
    xlnb = [sb(f"xlnb{i}", [128, D], F32) for i in range(2)]
    xw = sb("xw", [128, D], BF16)
    xifm = sb("xifm", [128, 8 * (TT_B + 2)], BF16)
    slst = sb("slst", [128, NFC * TT_B], BF16)
    hsb = [sb(f"hsb{i}", [128, TT_B + 2], F32) for i in range(2)]
    cbuf = [sb(f"cbuf{i}", [128, TT_B], F32) for i in range(2)]
    absb = [sb(f"absb{i}", [128, TT_B], F32) for i in range(2)]
    runmx = sb("runmx", [128, TT_B], F32)
    tapst = sb("tapst", [128, 96], F32)
    convbt = sb("convbt", [128, NFC], F32)
    cpadt = sb("cpadt", [128, 2], F32)
    ones = sb("ones", [1, 128], F32)
    onesc = sb("onesc", [128, 1], F32)
    eye = sb("eye", [128, 128], BF16)
    eyef = sb("eyef", [128, 128], F32)
    macc = sb("macc", [128, 1], F32)
    mtmp = sb("mtmp", [128, 1], F32)
    stot = sb("stot", [1, 4], F32)
    wsc_bc = [sb(f"wscbc{i}", [128, 1], F32) for i in range(2)]
    dqc_bc = [sb(f"dqcbc{i}", [128, 1], F32) for i in range(2)]
    qtmp = sb("qtmp", [128, TT_B], F32)
    qi8 = sb("qi8", [128, TT_B], I8)
    xi2b = sb("xi2b", [128, TT_B], BF16)
    mcols = sb("mcols", [128, 3], F32)
    mrow1 = sb("mrow1", [1, TT_B], F32)
    srow1 = sb("srow1", [1, TT_B], F32)
    dqrow1 = sb("dqrow1", [1, TT_B], F32)
    sc_bc = sb("sc_bc", [128, TT_B], F32)
    dq_bc = sb("dq_bc", [128, TT_B], F32)
    xres = [sb(f"xres{i}", [128, D], F32) for i in range(2)]
    x1o = [sb(f"x1o{i}", [128, D], F32) for i in range(2)]
    junk = sb("junkB", [128, D], BF16)
    sums = sb("sumsB", [128, 1], F32)
    sums2 = sb("sums2B", [128, 1], F32)
    mean = sb("meanB", [128, 1], F32)
    sumsq = sb("sumsqB", [128, 1], F32)
    var = sb("varB", [128, 1], F32)
    rstd = sb("rstdB", [128, 1], F32)
    mb = sb("mbB", [128, 1], F32)
    xm = sb("xmB", [128, 1], F32)
    xr = sb("xrB", [128, 1], F32)

    p_h = [ps(f"p_h{i}", [128, 512]) for i in range(2)]
    p_w2 = [ps(f"p_w2_{i}", [128, D]) for i in range(3)]

    ev = {}

    for nm, dst, src in (("taps", tapst, TAPS), ("convb", convbt, CONVB),
                         ("cpad", cpadt, CPAD), ("ones", ones, ONES),
                         ("onesc", onesc, ONESC)):
        P.emit("sync", lambda s, dst=dst, src=src: s.dma_start(out=dst[:], in_=src[:, :]),
               inc="dx", inc_by=16, event=f"c_{nm}")
    P.events["csetup"] = ("dx", P.count["dx"])
    P.emit("gpsimd", lambda g: g.dma_start(out=eye[:], in_=EYE[:, :]),
           inc="wl", inc_by=16, event="eye")
    P.emit("gpsimd", lambda g: g.dma_start(out=eyef[:], in_=EYE[:, :]),
           inc="wl", inc_by=16, event="eyef")
    P.events["eyes"] = ("wl", P.count["wl"])

    def mean_pass(tag, W, ntiles, width, slot):
        P.emit("vector", lambda v: v.memset(macc[:], 0.0),
               wait=(ev.get("macc_user"),), inc="ve", event=f"{tag}_maccz")
        for j in range(ntiles):
            b = j % 2
            P.emit("sync", lambda s, j=j, b=b, W=W, width=width: s.dma_start(
                out=wstage[b][:, 0:width], in_=W[128 * j:128 * j + 128, :]),
                wait=(ev.get(f"wst_free{b}"),), inc=f"ws{b}", inc_by=16,
                event=f"{tag}l_{j}")
            P.emit("vector", lambda v, b=b, width=width: v.reduce_sum(
                out=mtmp[:], in_=wstage[b][:, 0:width], axis=AX.X,
                apply_absolute_value=True),
                wait=(f"{tag}l_{j}", f"{tag}_maccz", ev.get("mtmp_user2"),),
                inc="ve", event=f"{tag}r_{j}")
            P.emit("vector", lambda v: v.tensor_add(out=macc[:], in0=macc[:], in1=mtmp[:]),
                   wait=(f"{tag}r_{j}",), inc="ve", event=f"{tag}a_{j}")
            ev["mtmp_user2"] = f"{tag}a_{j}"
            ev[f"wst_free{b}"] = f"{tag}r_{j}"
        P.emit("tensor", lambda t: t.matmul(out=p_h[0][0:1, 0:1], lhsT=macc[:],
                                            rhs=onesc[:], start=True, stop=True),
               wait=(f"{tag}a_{ntiles-1}", "c_onesc", ev.get("ph0_free"),),
               inc="pe", event=f"{tag}tot_mm")
        P.emit("vector", lambda v, slot=slot: v.tensor_copy(
            out=stot[0:1, slot:slot + 1], in_=p_h[0][0:1, 0:1]),
            wait=(f"{tag}tot_mm",), inc="ve", event=f"{tag}tot")
        ev["ph0_free"] = f"{tag}tot"
        ev["macc_user"] = f"{tag}tot_mm"

    mean_pass("m1", W1T, 8, DH1, 0)
    mean_pass("m2", W2T, NFC, D, 1)

    for wi, nelem in ((0, D * DH1), (1, DH1 * D)):
        P.emit("vector",
               lambda v, wi=wi, nelem=nelem: v.tensor_scalar(
                   out=stot[0:1, 2:3], in0=stot[0:1, wi:wi + 1],
                   scalar1=1.0 / nelem, scalar2=1e-5, op0=ALU.mult, op1=ALU.max),
               wait=("m2tot", ev.get("stot_user"),), inc="ve", event=f"clipm_{wi}")
        P.emit("vector", lambda v: v.reciprocal(out=stot[0:1, 3:4], in_=stot[0:1, 2:3]),
               wait=(f"clipm_{wi}",), inc="ve", event=f"wscv_{wi}")
        P.emit("tensor", lambda t: t.matmul(out=p_h[0][:, 0:1], lhsT=ones[:],
                                            rhs=stot[0:1, 3:4], start=True, stop=True),
               wait=(f"wscv_{wi}", "c_ones", ev.get("ph0_free"),),
               inc="pe", event=f"wsc_mm{wi}")
        P.emit("vector", lambda v, wi=wi: v.tensor_copy(out=wsc_bc[wi][:],
                                                        in_=p_h[0][:, 0:1]),
               wait=(f"wsc_mm{wi}",), inc="ve", event=f"wscbc_{wi}")
        P.emit("vector", lambda v:
               v.tensor_scalar(out=stot[0:1, 3:4], in0=stot[0:1, 2:3],
                               scalar1=1.0 / 127.0, scalar2=None, op0=ALU.mult),
               wait=(f"wscbc_{wi}",), inc="ve", event=f"dqcv_{wi}")
        P.emit("tensor", lambda t: t.matmul(out=p_h[0][:, 0:1], lhsT=ones[:],
                                            rhs=stot[0:1, 3:4], start=True, stop=True),
               wait=(f"dqcv_{wi}",), inc="pe", event=f"dqc_mm{wi}")
        P.emit("vector", lambda v, wi=wi: v.tensor_copy(out=dqc_bc[wi][:],
                                                        in_=p_h[0][:, 0:1]),
               wait=(f"dqc_mm{wi}",), inc="ve", event=f"dqcbc_{wi}")
        ev["ph0_free"] = f"dqcbc_{wi}"
        ev["stot_user"] = f"dqc_mm{wi}"

    def quant_pass(tag, W, ntiles, width, wi, store_fn):
        for j in range(ntiles):
            b = j % 2
            P.emit("sync", lambda s, j=j, b=b, W=W, width=width: s.dma_start(
                out=wstage[b][:, 0:width], in_=W[128 * j:128 * j + 128, :]),
                wait=(ev.get(f"wst_free{b}"),), inc=f"ws{b}", inc_by=16,
                event=f"{tag}l_{j}")
            P.emit("vector", lambda v, b=b, wi=wi, width=width: v.tensor_scalar_mul(
                out=wstage[b][:, 0:width], in0=wstage[b][:, 0:width],
                scalar1=wsc_bc[wi][:]),
                wait=(f"{tag}l_{j}", f"wscbc_{wi}"), inc="ve", event=f"{tag}s_{j}")
            P.emit("vector", lambda v, b=b, width=width: v.tensor_scalar(
                out=wi8[:, 0:width], in0=wstage[b][:, 0:width],
                scalar1=12582912.0, scalar2=12582912.0,
                op0=ALU.add, op1=ALU.subtract),
                wait=(f"{tag}s_{j}", ev.get("wi8_user"),), inc="ve",
                event=f"{tag}i_{j}")
            store_fn(tag, j, b)
            ev[f"wst_free{b}"] = f"{tag}i_{j}"
            ev["wi8_user"] = f"{tag}q_{j}"

    def store_w1(tag, j, b):
        P.emit("vector", lambda v, j=j, tag=tag: v.tensor_scalar(
            out=w1q[:, DH1 * j:DH1 * j + DH1], in0=wi8[:, 0:DH1],
            scalar1=1.0, scalar2=-1.0, op0=ALU.min, op1=ALU.max),
            wait=(f"{tag}i_{j}",), inc="ve", event=f"{tag}q_{j}")

    def store_w2(tag, j, b):
        P.emit("vector", lambda v, b=b, tag=tag, j=j: v.tensor_scalar(
            out=w2qb[b][:], in0=wi8[:, 0:D],
            scalar1=1.0, scalar2=-1.0, op0=ALU.min, op1=ALU.max),
            wait=(f"{tag}i_{j}", ev.get(f"w2qb_user{b}"),), inc="ve",
            event=f"{tag}q_{j}")
        P.emit("sync", lambda s, j=j, b=b: s.dma_start(
            out=W2Q[128 * j:128 * j + 128, :], in_=w2qb[b][:]),
            wait=(f"{tag}q_{j}",), inc=f"w2s{b}", inc_by=16, event=f"{tag}st_{j}")
        ev[f"w2qb_user{b}"] = f"{tag}st_{j}"

    quant_pass("q1", W1T, 8, DH1, 0, store_w1)
    quant_pass("q2", W2T, NFC, D, 1, store_w2)

    for rt in range(NMAIN):
        b = rt % 2
        P.emit("sync", lambda s, rt=rt, b=b: s.dma_start(
            out=xlnb[b][:], in_=XLN[128 * rt:128 * rt + 128, :]),
            wait=(ev.get(f"xlnb_user{b}"),), inc=f"xq{b}", inc_by=16, event=f"xl_{rt}")
        P.emit("vector", lambda v, b=b: v.reduce_max(
            out=mtmp[:], in_=xlnb[b][:], axis=AX.X, apply_absolute_value=True),
            wait=(f"xl_{rt}", ev.get("mtmp_user"),), inc="ve", event=f"xm_{rt}")
        P.emit("vector", lambda v: v.tensor_scalar(
            out=xm[:], in0=mtmp[:], scalar1=1e-5, scalar2=None, op0=ALU.max),
            wait=(f"xm_{rt}", ev.get("xm_user"),), inc="ve", event=f"xc_{rt}")
        ev["mtmp_user"] = f"xc_{rt}"
        P.emit("vector", lambda v, rt=rt: v.tensor_scalar_mul(
            out=dfac[:, rt:rt + 1], in0=xm[:], scalar1=dqc_bc[0][:]),
            wait=(f"xc_{rt}", "dqcbc_0"), inc="ve", event=f"xd_{rt}")
        P.emit("vector", lambda v: v.reciprocal(out=xr[:], in_=xm[:]),
               wait=(f"xc_{rt}", ev.get("xr_user"),), inc="ve", event=f"xrr_{rt}")
        ev["xm_user"] = f"xrr_{rt}"
        P.emit("vector", lambda v, rt=rt: v.tensor_scalar_mul(
            out=s127[:, rt:rt + 1], in0=xr[:], scalar1=127.0),
            wait=(f"xrr_{rt}",), inc="ve", event=f"xs_{rt}")
        ev["xr_user"] = f"xs_{rt}"
        P.emit("vector", lambda v, rt=rt, b=b: v.tensor_scalar(
            out=xlnb[b][:], in0=xlnb[b][:], scalar1=s127[:, rt:rt + 1],
            scalar2=12582912.0, op0=ALU.mult, op1=ALU.add),
            wait=(f"xs_{rt}",), inc="ve", event=f"xp_{rt}")
        P.emit("vector", lambda v, rt=rt, b=b: v.tensor_single_scalar(
            out=xi8[:, D * rt:D * rt + D], in_=xlnb[b][:],
            scalar=12582912.0, op=ALU.subtract),
            wait=(f"xp_{rt}",), inc="ve", event=f"xi_{rt}")
        ev[f"xlnb_user{b}"] = f"xi_{rt}"

    P.emit("vector", lambda v: v.memset(xifm[:], 0.0), inc="ve", event="xifm_z")
    tts = []
    t0 = 0
    while t0 < ROWS:
        tts.append((t0, min(TT_B, ROWS - t0)))
        t0 += TT_B

    def b_stage1(ti, t0, W):
        nsub = (W + 127) // 128
        lo = max(t0 - 1, 0)
        hi = min(t0 + W + 1, ROWS)
        rt_lo, rt_hi = lo // 128, (hi - 1) // 128 + 1
        lo = max(t0 - 1, 0)
        hi = min(t0 + W + 1, ROWS)
        rt_lo, rt_hi = lo // 128, (hi - 1) // 128 + 1
        for rt in range(rt_lo, rt_hi):
            P.emit("vector", lambda v, rt=rt: v.tensor_scalar_mul(
                out=xw[:], in0=xi8[:, D * rt:D * rt + D], scalar1=dfac[:, rt:rt + 1]),
                wait=(f"xi_{rt}", f"xd_{rt}", ev.get("xw_user"),),
                inc="ve", event=f"dq_{ti}_{rt}")
            pad_ev = f"dq_{ti}_{rt}"
            for prow, pidx in ((383, 0), (2432, 1)):
                if rt * 128 <= prow < rt * 128 + 128:
                    pp = prow - rt * 128
                    pb = 64 if pp >= 64 else (32 if pp >= 32 else 0)
                    pb = (pp // 32) * 32
                    pb = pb if pb in (0, 32, 64) else 64
                    P.emit("vector", lambda v, pb=pb, pidx=pidx:
                           v.tensor_scalar_mul(
                               out=xw[pb:pb + 64, :], in0=xw[pb:pb + 64, :],
                               scalar1=cpadt[pb:pb + 64, pidx:pidx + 1]),
                           wait=(pad_ev, "csetup"), inc="ve",
                           event=f"pad_{ti}_{rt}_{pidx}")
                    pad_ev = f"pad_{ti}_{rt}_{pidx}"
            off = rt * 128 - (t0 - 1)
            cl = max(0, -off)
            ch = min(128, (W + 2) - off)
            for bb in range(2):
                for jj in range(4):
                    j = 4 * bb + jj
                    P.emit("tensor", lambda t, j=j, jj=jj, bb=bb:
                           t.matmul(out=p_h[bb][:, 128 * jj:128 * jj + 128],
                                    lhsT=xw[:, 128 * j:128 * j + 128],
                                    rhs=eye[:], start=True, stop=True),
                           wait=(pad_ev, "eyes", ev.get(f"ph_user{bb}"),),
                           inc="pe" if jj == 3 else None,
                           event=f"tr_{ti}_{rt}_{bb}" if jj == 3 else None)
                P.emit("vector", lambda v, bb=bb, off=off, cl=cl, ch=ch:
                       v.tensor_copy(
                           out=xifm[:].rearrange("p (j w) -> p j w", j=8)
                               [:, 4 * bb:4 * bb + 4, off + cl:off + ch],
                           in_=p_h[bb][:].rearrange("p (j w) -> p j w", j=4)
                               [:, :, cl:ch]),
                       wait=(f"tr_{ti}_{rt}_{bb}", "xifm_z", ev.get("xifm_user"),),
                       inc="ve", event=f"tre_{ti}_{rt}_{bb}")
                ev[f"ph_user{bb}"] = f"tre_{ti}_{rt}_{bb}"
            ev["xw_user"] = f"tr_{ti}_{rt}_1"
        P.events[f"xifm_{ti}"] = P.events[f"tre_{ti}_{rt_hi-1}_1"]

    def b_loop1_pre(ti, t0, W):
        nsub = (W + 127) // 128
        lo = max(t0 - 1, 0)
        hi = min(t0 + W + 1, ROWS)
        rt_lo, rt_hi = lo // 128, (hi - 1) // 128 + 1
        P.emit("vector", lambda v, W=W: v.memset(runmx[:, 0:W], 1e-5),
               wait=(ev.get("runmx_user"),), inc="ve", event=f"rmz_{ti}")
    def b_loop1_fc(ti, t0, W, fc):
        nsub = (W + 127) // 128
        lo = max(t0 - 1, 0)
        hi = min(t0 + W + 1, ROWS)
        rt_lo, rt_hi = lo // 128, (hi - 1) // 128 + 1
        b = fc % 2
        for j in range(8):
            P.emit("tensor", lambda t, j=j, fc=fc, b=b, W=W:
                   t.matmul(out=p_h[b][:, 0:W + 2],
                            lhsT=w1q[:, DH1 * j + 128 * fc:DH1 * j + 128 * fc + 128],
                            rhs=xifm[:, (TT_B + 2) * j:(TT_B + 2) * j + W + 2],
                            start=(j == 0), stop=(j == 7)),
                   wait=(f"xifm_{ti}", "q1q_7", "xifm_z", ev.get(f"ph_user{b}"),),
                   inc="pe" if j == 7 else None,
                   event=f"h_{ti}_{fc}" if j == 7 else None)
        P.emit("scalar", lambda s, b=b, W=W: s.activation(
            out=hsb[b][:, 0:W + 2], in_=p_h[b][:, 0:W + 2], func=AF.Identity),
            wait=(f"h_{ti}_{fc}", ev.get(f"hsb_user{b}"),),
            inc="se", event=f"he_{ti}_{fc}")
        ev[f"ph_user{b}"] = f"he_{ti}_{fc}"
        P.emit("vector", lambda v, b=b, fc=fc, W=W: v.tensor_scalar_mul(
            out=cbuf[b][:, 0:W], in0=hsb[b][:, 1:W + 1],
            scalar1=tapst[:, 3 * fc + 1:3 * fc + 2]),
            wait=(f"he_{ti}_{fc}", "csetup", ev.get(f"cbuf_user{b}"),),
            inc="ve", event=f"c1_{ti}_{fc}")
        P.emit("vector", lambda v, b=b, fc=fc, W=W: v.scalar_tensor_tensor(
            out=cbuf[b][:, 0:W], in0=hsb[b][:, 0:W],
            scalar=tapst[:, 3 * fc:3 * fc + 1], in1=cbuf[b][:, 0:W],
            op0=ALU.mult, op1=ALU.add),
            wait=(f"c1_{ti}_{fc}",), inc="ve", event=f"c2_{ti}_{fc}")
        P.emit("vector", lambda v, b=b, fc=fc, W=W: v.scalar_tensor_tensor(
            out=cbuf[b][:, 0:W], in0=hsb[b][:, 2:W + 2],
            scalar=tapst[:, 3 * fc + 2:3 * fc + 3], in1=cbuf[b][:, 0:W],
            op0=ALU.mult, op1=ALU.add),
            wait=(f"c2_{ti}_{fc}",), inc="ve", event=f"c3_{ti}_{fc}")
        ev[f"hsb_user{b}"] = f"c3_{ti}_{fc}"
        P.emit("scalar", lambda s, b=b, fc=fc, W=W: s.activation(
            out=slst[:, TT_B * fc:TT_B * fc + W], in_=cbuf[b][:, 0:W],
            func=AF.Silu, bias=convbt[:, fc:fc + 1], scale=1.0),
            wait=(f"c3_{ti}_{fc}", ev.get(f"slfree_{fc}"),),
            inc="se", event=f"sl_{ti}_{fc}")
        P.emit("scalar", lambda s, b=b, fc=fc, W=W: s.activation(
            out=absb[b][:, 0:W], in_=slst[:, TT_B * fc:TT_B * fc + W],
            func=AF.Abs),
            wait=(f"sl_{ti}_{fc}", ev.get(f"absb_user{b}"),),
            inc="se", event=f"ab_{ti}_{fc}")
        ev[f"cbuf_user{b}"] = f"ab_{ti}_{fc}"
        P.emit("vector", lambda v, b=b, W=W: v.tensor_tensor(
            out=runmx[:, 0:W], in0=runmx[:, 0:W], in1=absb[b][:, 0:W],
            op=ALU.max),
            wait=(f"ab_{ti}_{fc}", f"rmz_{ti}", ev.get("runmx_g"),),
            inc="ve", event=f"mx_{ti}_{fc}")
        ev["runmx_g"] = f"mx_{ti}_{fc}"
        ev[f"absb_user{b}"] = f"mx_{ti}_{fc}"
    def b_loop1_post(ti):
        ev["xifm_user"] = f"h_{ti}_{NFC-1}"
    def b_chain(ti, t0, W):
        nsub = (W + 127) // 128
        lo = max(t0 - 1, 0)
        hi = min(t0 + W + 1, ROWS)
        rt_lo, rt_hi = lo // 128, (hi - 1) // 128 + 1
        for s_ in range(nsub):
            sw = min(128, W - 128 * s_)
            P.emit("tensor", lambda t, s_=s_, sw=sw: t.transpose(
                out=p_w2[0][0:sw, 0:128], in_=runmx[:, 128 * s_:128 * s_ + sw],
                identity=eyef[:]),
                wait=(f"mx_{ti}_{NFC-1}", "eyes", ev.get("pw2s_user"),),
                inc="pe", event=f"mt_{ti}_{s_}")
            P.emit("vector", lambda v, s_=s_, sw=sw: v.reduce_max(
                out=mcols[:, s_:s_ + 1],
                in_=p_w2[0][:, 0:128], axis=AX.X),
                wait=(f"mt_{ti}_{s_}", ev.get("mcols_user"),),
                inc="ve", event=f"mc_{ti}_{s_}")
            ev["pw2s_user"] = f"mc_{ti}_{s_}"
        for s_ in range(nsub):
            P.emit("tensor", lambda t, s_=s_: t.transpose(
                out=p_w2[1][0:1, 128 * s_:128 * s_ + 128], in_=mcols[:, s_:s_ + 1],
                identity=eyef[:]),
                wait=(f"mc_{ti}_{nsub-1}", ev.get("pw2f_user"),),
                inc="pe" if s_ == nsub - 1 else None,
                event=f"mf_{ti}" if s_ == nsub - 1 else None)
        P.emit("vector", lambda v, W=W: v.tensor_copy(
            out=mrow1[:, 0:W] if W % 128 == 0 else mrow1[:, 0:((W + 127) // 128) * 128][:, 0:W],
            in_=p_w2[1][0:1, 0:W]),
            wait=(f"mf_{ti}", ev.get("mrow_user"),), inc="ve", event=f"mr_{ti}")
        ev["pw2f_user"] = f"mr_{ti}"
        P.emit("vector", lambda v, W=W: v.reciprocal(out=srow1[:, 0:W], in_=mrow1[:, 0:W]),
               wait=(f"mr_{ti}",), inc="ve", event=f"sr0_{ti}")
        P.emit("vector", lambda v, W=W: v.tensor_scalar_mul(
            out=srow1[:, 0:W], in0=srow1[:, 0:W], scalar1=127.0),
            wait=(f"sr0_{ti}",), inc="ve", event=f"sr_{ti}")
        P.emit("vector", lambda v, W=W: v.tensor_scalar_mul(
            out=dqrow1[:, 0:W], in0=mrow1[:, 0:W], scalar1=dqc_bc[1][0:1, :]),
            wait=(f"mr_{ti}", "dqcbc_1", ev.get("dqrow_user"),),
            inc="ve", event=f"dqr_{ti}")
        ev["mrow_user"] = f"dqr_{ti}"
        for s_ in range(nsub):
            sw = min(128, W - 128 * s_)
            P.emit("tensor", lambda t, s_=s_, sw=sw: t.matmul(
                out=p_w2[0][:, 128 * s_:128 * s_ + sw], lhsT=ones[:],
                rhs=srow1[0:1, 128 * s_:128 * s_ + sw], start=True, stop=True),
                wait=(f"sr_{ti}", ev.get("pw2s_user"),),
                inc="pe" if s_ == nsub - 1 else None,
                event=f"sbm_{ti}" if s_ == nsub - 1 else None)
        P.emit("vector", lambda v, W=W: v.tensor_copy(out=sc_bc[:, 0:W],
                                                      in_=p_w2[0][:, 0:W]),
               wait=(f"sbm_{ti}", ev.get("scbc_user"),), inc="ve", event=f"sbc_{ti}")
        ev["pw2s_user"] = f"sbc_{ti}"
        for s_ in range(nsub):
            sw = min(128, W - 128 * s_)
            P.emit("tensor", lambda t, s_=s_, sw=sw: t.matmul(
                out=p_w2[1][:, 128 * s_:128 * s_ + sw], lhsT=ones[:],
                rhs=dqrow1[0:1, 128 * s_:128 * s_ + sw], start=True, stop=True),
                wait=(f"dqr_{ti}", ev.get("pw2f_user"),),
                inc="pe" if s_ == nsub - 1 else None,
                event=f"dbm_{ti}" if s_ == nsub - 1 else None)
        P.emit("vector", lambda v, W=W: v.tensor_copy(out=dq_bc[:, 0:W],
                                                      in_=p_w2[1][:, 0:W]),
               wait=(f"dbm_{ti}", ev.get("dqbc_user"),), inc="ve", event=f"dbc_{ti}")
        ev["pw2f_user"] = f"dbc_{ti}"

    def b_loop2_fc(ti, t0, W, fc):
        nsub = (W + 127) // 128
        lo = max(t0 - 1, 0)
        hi = min(t0 + W + 1, ROWS)
        rt_lo, rt_hi = lo // 128, (hi - 1) // 128 + 1
        b = fc % 2
        P.emit("sync", lambda s, fc=fc, b=b: s.dma_start(
            out=w2qb[b][:], in_=W2Q[128 * fc:128 * fc + 128, :]),
            wait=(f"q2st_{NFC-1}", ev.get(f"w2qb_user{b}"),),
            inc=f"w2l{b}", inc_by=16, event=f"w2f_{ti}_{fc}")
        P.emit("vector", lambda v, fc=fc, W=W: v.tensor_tensor(
            out=qtmp[:, 0:W], in0=slst[:, TT_B * fc:TT_B * fc + W],
            in1=sc_bc[:, 0:W], op=ALU.mult),
            wait=(f"sbc_{ti}", f"sl_{ti}_{fc}", ev.get("qtmp_user"),),
            inc="ve", event=f"q0_{ti}_{fc}")
        ev[f"slfree_{fc}"] = f"q0_{ti}_{fc}"
        P.emit("vector", lambda v, W=W: v.tensor_scalar(
            out=qi8[:, 0:W], in0=qtmp[:, 0:W],
            scalar1=12582912.0, scalar2=12582912.0,
            op0=ALU.add, op1=ALU.subtract),
            wait=(f"q0_{ti}_{fc}", ev.get("qi8_user"),),
            inc="ve", event=f"qa_{ti}_{fc}")
        ev["qtmp_user"] = f"qa_{ti}_{fc}"
        P.emit("gpsimd", lambda g, fc=fc, W=W: g.tensor_tensor(
            out=xi2b[:, 0:W], in0=qi8[:, 0:W], in1=dq_bc[:, 0:W],
            op=ALU.mult),
            wait=(f"qa_{ti}_{fc}", f"dbc_{ti}", ev.get("xi2b_user"),),
            inc="ge", event=f"qb_{ti}_{fc}")
        ev["qi8_user"] = f"qb_{ti}_{fc}"
        for s_ in range(nsub):
            sw = min(128, W - 128 * s_)
            for half in range(2):
                last = (s_ == nsub - 1 and half == 1)
                P.emit("tensor", lambda t, fc=fc, b=b, s_=s_, half=half, sw=sw:
                       t.matmul(out=p_w2[s_][:, 512 * half:512 * half + 512],
                                lhsT=xi2b[:, 128 * s_:128 * s_ + sw],
                                rhs=w2qb[b][:, 512 * half:512 * half + 512],
                                start=(fc == 0), stop=(fc == NFC - 1)),
                       wait=(f"qb_{ti}_{fc}", f"w2f_{ti}_{fc}",
                             (f"sbc_{ti}" if s_ == 0 else
                              (f"dbc_{ti}" if s_ == 1 else None)) if fc == 0 else None,
                             ev.get(f"pw2_user{s_}") if fc == 0 else None,),
                       inc="pe" if last else None,
                       event=f"w2u_{ti}_{fc}" if last else None)
        ev["xi2b_user"] = f"w2u_{ti}_{fc}"
        ev[f"w2qb_user{b}"] = f"w2u_{ti}_{fc}"
    def b_loop2_post(ti):
        P.events[f"w2_{ti}"] = P.events[f"w2u_{ti}_{NFC-1}"]
    def b_epilogue(ti, t0, W):
        nsub = (W + 127) // 128
        lo = max(t0 - 1, 0)
        hi = min(t0 + W + 1, ROWS)
        rt_lo, rt_hi = lo // 128, (hi - 1) // 128 + 1
        for s_ in range(nsub):
            r0 = t0 + 128 * s_
            if not (OUT_LO <= r0 < OUT_HI):
                ev[f"pw2_user{s_}"] = f"w2_{ti}"
                continue
            ob = (r0 // 128) % 2
            P.emit("sync", lambda s, r0=r0, ob=ob: s.dma_start(
                out=xres[ob][:], in_=XLN[r0:r0 + 128, :]),
                wait=(ev.get(f"xres_user{ob}"),), inc=f"xr{ob}", inc_by=16,
                event=f"rl_{ti}_{s_}")
            P.emit("vector", lambda v, s_=s_, ob=ob: v.scalar_tensor_tensor(
                out=x1o[ob][:, 0:512], in0=p_w2[s_][:, 0:512], scalar=1.0,
                in1=xres[ob][:, 0:512], op0=ALU.mult, op1=ALU.add,
                accum_out=sums[:]),
                wait=(f"w2_{ti}", f"rl_{ti}_{s_}", ev.get(f"x1o_user{ob}"),),
                inc="ve", event=f"e0_{ti}_{s_}")
            P.emit("vector", lambda v, s_=s_, ob=ob: v.scalar_tensor_tensor(
                out=x1o[ob][:, 512:1024], in0=p_w2[s_][:, 512:1024], scalar=1.0,
                in1=xres[ob][:, 512:1024], op0=ALU.mult, op1=ALU.add,
                accum_out=sums2[:]),
                wait=(f"e0_{ti}_{s_}",), inc="ve", event=f"e1_{ti}_{s_}")
            ev[f"xres_user{ob}"] = f"e1_{ti}_{s_}"
            ev[f"pw2_user{s_}"] = f"e1_{ti}_{s_}"
            P.emit("vector", lambda v: v.tensor_add(out=sums[:], in0=sums[:], in1=sums2[:]),
                   wait=(f"e1_{ti}_{s_}",), inc="ve", event=f"es_{ti}_{s_}")
            P.emit("vector", lambda v: v.tensor_scalar_mul(out=mean[:], in0=sums[:],
                                                           scalar1=1.0 / D),
                   wait=(f"es_{ti}_{s_}",), inc="ve", event=f"em_{ti}_{s_}")
            P.emit("vector", lambda v: v.tensor_scalar_mul(out=mb[:], in0=mean[:],
                                                           scalar1=-1.0),
                   wait=(f"em_{ti}_{s_}", ev.get("mb_user"),), inc="ve",
                   event=f"eb_{ti}_{s_}")
            P.emit("scalar", lambda s, ob=ob: s.activation(
                out=junk[:], in_=x1o[ob][:], func=AF.Square, bias=mb[:], scale=1.0,
                accum_out=sumsq[:]),
                wait=(f"eb_{ti}_{s_}", ev.get("sumsq_user"),), inc="se",
                event=f"eq_{ti}_{s_}")
            P.emit("vector", lambda v: v.tensor_scalar(
                out=var[:], in0=sumsq[:], scalar1=1.0 / D, scalar2=1e-5,
                op0=ALU.mult, op1=ALU.add),
                wait=(f"eq_{ti}_{s_}",), inc="ve", event=f"ev_{ti}_{s_}")
            ev["sumsq_user"] = f"ev_{ti}_{s_}"
            P.emit("scalar", lambda s: s.activation(out=var[:], in_=var[:], func=AF.Sqrt),
                   wait=(f"ev_{ti}_{s_}",), inc="se", event=f"ed_{ti}_{s_}")
            P.emit("vector", lambda v: v.reciprocal(out=rstd[:], in_=var[:]),
                   wait=(f"ed_{ti}_{s_}", ev.get("rstd_user"),), inc="ve",
                   event=f"er_{ti}_{s_}")
            P.emit("vector", lambda v: v.tensor_tensor(out=mb[:], in0=mb[:],
                                                       in1=rstd[:], op=ALU.mult),
                   wait=(f"er_{ti}_{s_}",), inc="ve", event=f"emb_{ti}_{s_}")
            ev["mb_user"] = f"emb_{ti}_{s_}"
            P.emit("scalar", lambda s, ob=ob: s.activation(
                out=x1o[ob][:], in_=x1o[ob][:], func=AF.Identity, bias=mb[:],
                scale=rstd[:]),
                wait=(f"emb_{ti}_{s_}",), inc="se", event=f"eln_{ti}_{s_}")
            ev["mb_user"] = f"eln_{ti}_{s_}"
            ev["rstd_user"] = f"eln_{ti}_{s_}"
            P.emit("sync", lambda s, r0=r0, ob=ob: s.dma_start(
                out=X1[r0 - OUT_LO:r0 - OUT_LO + 128, :], in_=x1o[ob][:]),
                wait=(f"eln_{ti}_{s_}",), inc=f"so{ob}", inc_by=16,
                event=f"st_{ti}_{s_}")
            ev[f"x1o_user{ob}"] = f"st_{ti}_{s_}"
        ev["runmx_user"] = f"mt_{ti}_{nsub-1}"
        ev["scbc_user"] = f"qa_{ti}_{NFC-1}"
        ev["dqbc_user"] = f"qb_{ti}_{NFC-1}"
        ev["pw2s_user"] = ev.get("pw2_user0")
        ev["pw2f_user"] = ev.get("pw2_user1")

    for ti in range(len(tts)):
        t0, W = tts[ti]
        b_stage1(ti, t0, W)
        b_loop1_pre(ti, t0, W)
        for fc in range(NFC):
            b_loop1_fc(ti, t0, W, fc)
        b_loop1_post(ti)
        b_chain(ti, t0, W)
        for fc in range(NFC):
            b_loop2_fc(ti, t0, W, fc)
        b_loop2_post(ti)
        b_epilogue(ti, t0, W)


    ctx.close()


def build_B():
    nc = bass.Bass("TRN2", target_bir_lowering=False, debug=False)
    dram = _mk_dram(nc)
    ctx = ExitStack()
    sems = _mk_sems(nc, ctx)(ALL_SEMS)
    P = Plan()
    emit_B(nc, PhasePlan(P, "B."), dram, "ExternalInput", "ExternalOutput")
    P.finish_dma()
    with nc.Block() as block:
        P.emit_into(block, sems)
    ctx.close()
    return nc


def host_prep_B(inputs):
    w1 = np.asarray(inputs["L0_ffn_w1"], np.float32)
    w2 = np.asarray(inputs["L0_ffn_w2"], np.float32)
    conv_w = np.asarray(inputs["L0_ffn_conv_w"], np.float32)
    conv_b = np.asarray(inputs["L0_ffn_conv_b"], np.float32)
    w1T = np.ascontiguousarray(w1.T)
    w2T = np.ascontiguousarray(w2.T)
    taps = np.ascontiguousarray(conv_w[:, 0, :].reshape(NFC, 128, 3)
                                .transpose(1, 0, 2).reshape(128, NFC * 3))
    convb = np.ascontiguousarray(conv_b.reshape(NFC, 128).T)
    ones = np.ones((1, 128), np.float32)
    onesc = np.ones((128, 1), np.float32)
    eye = np.eye(128, dtype=np.float32)
    in_maps = []
    for c in range(NCORES):
        half = c % 2
        fL = 0.0 if half == 0 else 1.0
        fR = 1.0 if half == 0 else 0.0
        cp = np.ones((128, 2), np.float32)
        cp[127, 0] = fL
        cp[0, 1] = fR
        m = {"xln": None,
             "w1T": w1T, "w2T": w2T, "taps": taps, "convb": convb,
             "cpad": cp, "ones_row": ones, "ones_col": onesc, "eye128": eye}
        in_maps.append(m)
    return in_maps

NKT = 20
NQC = 8


def _c_mask_plan(qc):
    base = {(0, 0): "tria", (0, 1): None, (0, 2): None, (0, 3): None,
            (0, 4): "trib",
            (1, 1): "tria", (1, 2): None, (1, 3): None, (1, 4): None,
            (1, 5): "trib"}
    out = dict(base)
    if qc == 0:
        out[(0, 0)] = ("d", 0)
        out[(0, 1)] = ("d", 1)
        out[(1, 1)] = ("d", 2)
    if qc == NQC - 1:
        out[(0, 4)] = ("d", 3)
        out[(1, 4)] = ("d", 4)
        out[(1, 5)] = ("d", 5)
    return [(qs, ks, out[(qs, ks)]) for (qs, ks) in sorted(out)]


def emit_C(nc, P, dram, in_kind, out_kind):
    X1 = dram("x1", [NKT * 128, D], in_kind)
    WT = {k: dram(f"l1w{k}", [D, D], "ExternalInput") for k in "qkvo"}
    BR = {k: dram(f"l1b{k}", [1, D], "ExternalInput") for k in "qkvo"}
    ONES = dram("ones_row", [1, 128], "ExternalInput")
    EYE = dram("eye128", [128, 128], "ExternalInput")
    TRIA = dram("tria", [128, 128], "ExternalInput")
    TRIB = dram("trib", [128, 128], "ExternalInput")
    DMASK = dram("dmask", [6 * 128, 128], "ExternalInput")
    X2 = dram("x2", [2048, D], out_kind)

    ctx = ExitStack()
    sb = lambda name, shape, dt: ctx.enter_context(nc.sbuf_tensor(name + '_pC', shape, dt))
    ps = lambda name, shape: ctx.enter_context(nc.psum_tensor(name + '_pC', shape, F32))

    w16 = {k: sb(f"w16_{k}", [128, 8 * D], BF16) for k in "qkvo"}
    brow = {k: sb(f"brow_{k}", [1, D], F32) for k in "qkvo"}
    ones = sb("ones", [1, 128], F32)
    eye = sb("eye", [128, 128], BF16)
    tria = sb("triaT", [128, 128], BF16)
    trib = sb("tribT", [128, 128], BF16)
    dmask = sb("dmaskT", [128, 6 * 128], BF16)
    x16 = sb("x16C", [128, D], BF16)
    xfm = sb("xfmC", [128, D], BF16)
    q_tm = sb("q_tmC", [128, D], BF16)
    k_tm = sb("k_tmC", [128, D], BF16)
    qfm = [sb(f"qfmC{i}", [128, D], BF16) for i in range(8)]
    kfm = [sb(f"kfmC{i}", [128, D], BF16) for i in range(8)]
    v65 = [sb(f"v65C{i}", [128, 1040], BF16) for i in range(8)]
    psc = [sb(f"pscC{i}", [128, 768], BF16) for i in range(2)]
    att16 = sb("att16C", [128, D], BF16)
    attfm = sb("attfmC", [128, D], BF16)
    recips = sb("recipsC", [128, 1], F32)
    xres = [sb(f"xresC{i}", [128, D], F32) for i in range(2)]
    x2o = [sb(f"x2oC{i}", [128, D], F32) for i in range(2)]
    junk = sb("junkC", [128, D], F32)
    sums = sb("sumsC", [128, 1], F32)
    sums2 = sb("sums2C", [128, 1], F32)
    mean = sb("meanC", [128, 1], F32)
    sumsq = sb("sumsqC", [128, 1], F32)
    var = sb("varC", [128, 1], F32)
    rstd = sb("rstdC", [128, 1], F32)
    mb = sb("mbC", [128, 1], F32)

    p_s = [ps("p_sA", [128, 768]), ps("p_sB", [128, 768])]
    p_tr = ps("p_trC", [128, 512])
    p_at = ps("p_atC", [128, 260])
    p_pr = [ps(f"p_prC{i}", [128, 512]) for i in range(2)]

    ev = {}

    for k in "qkvo":
        P.emit("gpsimd",
               lambda g, k=k: g.dma_start(
                   out=w16[k][:].rearrange("p (j e) -> p j e", j=8),
                   in_=WT[k].rearrange("(j p) e -> p j e", p=128)),
               inc="wl", inc_by=16, event=f"w_{k}")
    P.emit("gpsimd", lambda g: g.dma_start(out=eye[:], in_=EYE[:, :]),
           inc="wl", inc_by=16, event="eye")
    P.emit("gpsimd", lambda g: g.dma_start(out=tria[:], in_=TRIA[:, :]),
           inc="wl", inc_by=16, event="tria")
    P.emit("gpsimd", lambda g: g.dma_start(out=trib[:], in_=TRIB[:, :]),
           inc="wl", inc_by=16, event="trib")
    P.emit("gpsimd", lambda g: g.dma_start(
        out=dmask[:].rearrange("p (i c) -> p i c", i=6),
        in_=DMASK.rearrange("(i p) c -> p i c", p=128)),
        inc="wl", inc_by=16, event="dmaskl")
    for k in "qkvo":
        P.emit("sync", lambda s, k=k: s.dma_start(out=brow[k][:], in_=BR[k][:, :]),
               inc="dx", inc_by=16, event=f"b_{k}")
    P.emit("sync", lambda s: s.dma_start(out=ones[:], in_=ONES[:, :]),
           inc="dx", inc_by=16, event="ones")
    P.events["wsetup"] = ("wl", P.count["wl"])
    P.events["dsetup"] = ("dx", P.count["dx"])

    def proj_C_half(tag, w_key, src_ev, half):
        P.emit("tensor",
               lambda t, half=half, w_key=w_key:
               t.matmul(out=p_pr[half][:], lhsT=ones[:],
                        rhs=brow[w_key][:, 512 * half:512 * half + 512],
                        start=True, stop=False, skip_group_check=True),
               wait=(src_ev, "wsetup", "dsetup", ev.get(f"ppr_user{half}"),))
        for j in range(8):
            P.emit("tensor",
                   lambda t, j=j, half=half, w_key=w_key:
                   t.matmul(out=p_pr[half][:],
                            lhsT=xfm[:, 128 * j:128 * j + 128],
                            rhs=w16[w_key][:, 1024 * j + 512 * half:
                                           1024 * j + 512 * half + 512],
                            start=False, stop=(j == 7), skip_group_check=True),
                   inc="pe" if j == 7 else None,
                   event=f"{tag}_mm{half}" if j == 7 else None)

    def transpose8_C(tag, src16, src_ev, dst, dst_free_ev=None):
        for b in range(2):
            for jj in range(4):
                j = 4 * b + jj
                P.emit("tensor",
                       lambda t, j=j, jj=jj, src16=src16:
                       t.matmul(out=p_tr[:, 128 * jj:128 * jj + 128],
                                lhsT=src16[:, 128 * j:128 * j + 128],
                                rhs=eye[:], start=True, stop=True),
                       wait=(src_ev, "wsetup", ev.get("ptrC_user"),),
                       inc="pe" if jj == 3 else None,
                       event=f"{tag}_tr{b}" if jj == 3 else None)
            P.emit("vector",
                   lambda v, b=b, dst=dst:
                   v.tensor_copy(out=dst[:, 512 * b:512 * b + 512],
                                 in_=p_tr[:, 0:512]),
                   wait=(f"{tag}_tr{b}", dst_free_ev),
                   inc="ve", event=f"{tag}_ev{b}")
            ev["ptrC_user"] = f"{tag}_ev{b}"
        P.events[f"{tag}_done"] = P.events[f"{tag}_ev1"]

    def stage1(rt):
        slot = rt % 8
        P.emit("gpsimd", lambda g, rt=rt: g.dma_start(
            out=x16[:], in_=X1[128 * rt:128 * rt + 128, :]),
            wait=(ev.get("x16C_user"),), inc="xg", inc_by=16, event=f"x16_{rt}")
        transpose8_C(f"xf{rt}", x16, f"x16_{rt}", xfm,
                     dst_free_ev=ev.get("xfmC_user"))
        ev["x16C_user"] = f"xf{rt}_tr1"
        if 2 <= rt < 18:
            for half in range(2):
                proj_C_half(f"q{rt}", "q", f"xf{rt}_done", half)
                P.emit("vector", lambda v, half=half: v.tensor_copy(
                    out=q_tm[:, 512 * half:512 * half + 512], in_=p_pr[half][:]),
                    wait=(f"q{rt}_mm{half}", ev.get("qtmC_user"),),
                    inc="ve", event=f"qtm_{rt}_{half}")
                ev[f"ppr_user{half}"] = f"qtm_{rt}_{half}"
            transpose8_C(f"qf{rt}", q_tm, f"qtm_{rt}_1", qfm[slot],
                         dst_free_ev=ev.get(f"qfm_user{slot}"))
            ev["qtmC_user"] = f"qf{rt}_tr1"
        for half in range(2):
            proj_C_half(f"k{rt}", "k", f"xf{rt}_done", half)
            P.emit("vector", lambda v, half=half: v.tensor_copy(
                out=k_tm[:, 512 * half:512 * half + 512], in_=p_pr[half][:]),
                wait=(f"k{rt}_mm{half}", ev.get("ktmC_user"),),
                inc="ve", event=f"ktm_{rt}_{half}")
            ev[f"ppr_user{half}"] = f"ktm_{rt}_{half}"
        transpose8_C(f"kf{rt}", k_tm, f"ktm_{rt}_1", kfm[slot],
                     dst_free_ev=ev.get(f"kfm_user{slot}"))
        ev["ktmC_user"] = f"kf{rt}_tr1"
        P.emit("vector", lambda v, slot=slot: v.memset(
            v65[slot][:, 64:1040:65], 1.0),
            wait=(ev.get(f"v65_user{slot}"),), inc="ve", event=f"vm_{rt}")
        for half in range(2):
            proj_C_half(f"v{rt}", "v", f"xf{rt}_done", half)
            P.emit("vector", lambda v, half=half, slot=slot: v.tensor_copy(
                out=v65[slot][:, 520 * half:520 * half + 520]
                    .rearrange("p (h d) -> p h d", h=8)[:, :, 0:64],
                in_=p_pr[half][:].rearrange("p (h d) -> p h d", h=8)),
                wait=(f"v{rt}_mm{half}", f"vm_{rt}"),
                inc="ve", event=f"v65_{rt}_{half}")
            ev[f"ppr_user{half}"] = f"v65_{rt}_{half}"
        P.events[f"s1_{rt}"] = P.events[f"v65_{rt}_1"]
        ev["xfmC_user"] = f"v{rt}_mm1"

    def chunk(qc):
        plan_list = _c_mask_plan(qc)
        for qs in range(2):
            qrt = 2 + 2 * qc + qs
            qslot = qrt % 8
            sub = [(ks, mk) for (q2, ks, mk) in plan_list if q2 == qs]
            pcol0 = 0 if qs == 0 else 128
            nsk = len(sub)

            def emit_scores(h):
                si = h % 2
                base = (h % 2) * 64
                jb = (h // 2) * 128
                for idx, (ks, mk) in enumerate(sub):
                    kslot = (2 * qc + ks) % 8
                    P.emit("tensor",
                           lambda t, base=base, jb=jb, qslot=qslot,
                                  kslot=kslot, si=si, ks=ks:
                           t.matmul(out=p_s[si][:, 128 * ks:128 * ks + 128],
                                    lhsT=kfm[kslot][base:base + 64, jb:jb + 128],
                                    rhs=qfm[qslot][base:base + 64, jb:jb + 128],
                                    start=True, stop=True),
                           wait=(f"win_{qc}", ev.get(f"ps_user{si}"),),
                           inc="pe" if idx == nsk - 1 else None,
                           event=f"sc_{qc}_{qs}_{h}" if idx == nsk - 1 else None)
                P.emit("scalar",
                       lambda s, si=si, pcol0=pcol0:
                       s.activation(out=psc[si][:, pcol0:pcol0 + 640],
                                    in_=p_s[si][:, pcol0:pcol0 + 640],
                                    func=AF.Exp, scale=0.125),
                       wait=(f"sc_{qc}_{qs}_{h}", ev.get(f"psc_user{si}"),),
                       inc="se", event=f"ex_{qc}_{qs}_{h}")
                ev[f"ps_user{si}"] = f"ex_{qc}_{qs}_{h}"
                mask_ev = f"ex_{qc}_{qs}_{h}"
                nm_i = 0
                for (ks, mk) in sub:
                    if mk is None:
                        continue
                    mt = {"tria": tria, "trib": trib}.get(mk)
                    moff = None if mt is not None else 128 * mk[1]
                    pcol = 128 * ks
                    P.emit("vector",
                           lambda v, pcol=pcol, mt=mt, moff=moff, si=si:
                           v.tensor_tensor(out=psc[si][:, pcol:pcol + 128],
                                           in0=psc[si][:, pcol:pcol + 128],
                                           in1=mt[:] if mt is not None
                                           else dmask[:, moff:moff + 128],
                                           op=ALU.mult),
                           wait=(mask_ev, "wsetup"),
                           inc="ve", event=f"mk_{qc}_{qs}_{h}_{nm_i}")
                    mask_ev = f"mk_{qc}_{qs}_{h}_{nm_i}"
                    nm_i += 1
                P.events[f"msk_{qc}_{qs}_{h}"] = P.events[mask_ev]

            def emit_avpart(h, rnd, i4):
                hsi = h % 2
                for si_, (ks, mk) in enumerate(sub):
                    kslot = (2 * qc + ks) % 8
                    pcol = 128 * ks
                    P.emit("tensor",
                           lambda t, si_=si_, i4=i4, h=h, kslot=kslot,
                                  pcol=pcol, hsi=hsi:
                           t.matmul(out=p_at[:, 65 * i4:65 * i4 + 65],
                                    lhsT=psc[hsi][:, pcol:pcol + 128],
                                    rhs=v65[kslot][:, 65 * h:65 * h + 65],
                                    start=(si_ == 0), stop=(si_ == nsk - 1)),
                           wait=(f"msk_{qc}_{qs}_{h}",
                                 ev.get("pat_user") if si_ == 0 else None,),
                           inc="pe" if si_ == nsk - 1 else None,
                           event=f"av_{qc}_{qs}_{h}" if si_ == nsk - 1 else None)
                ev[f"psc_user{hsi}"] = f"av_{qc}_{qs}_{h}"

            emit_scores(0)
            for rnd in range(4):
                for i4 in range(4):
                    h = 4 * rnd + i4
                    if h + 1 < NH:
                        emit_scores(h + 1)
                    emit_avpart(h, rnd, i4)
                for i4 in range(4):
                    h = 4 * rnd + i4
                    P.emit("vector", lambda v, i4=i4: v.reciprocal(
                        out=recips[:], in_=p_at[:, 65 * i4 + 64:65 * i4 + 65]),
                        wait=(f"av_{qc}_{qs}_{4*rnd+3}", ev.get("recips_user"),),
                        inc="ve", event=f"rc_{qc}_{qs}_{h}")
                    P.emit("scalar", lambda s, i4=i4, h=h: s.activation(
                        out=att16[:, 64 * h:64 * h + 64],
                        in_=p_at[:, 65 * i4:65 * i4 + 64],
                        func=AF.Identity, scale=recips[:]),
                        wait=(f"rc_{qc}_{qs}_{h}", ev.get("att16C_user"),),
                        inc="se", event=f"ae_{qc}_{qs}_{h}")
                    ev["recips_user"] = f"ae_{qc}_{qs}_{h}"
                ev["pat_user"] = f"ae_{qc}_{qs}_{4*rnd+3}"

            transpose8_C(f"af{qc}_{qs}", att16, f"ae_{qc}_{qs}_{NH-1}", attfm,
                         dst_free_ev=ev.get("attfmC_user"))
            ev["att16C_user"] = f"af{qc}_{qs}_tr1"
            proj_o_ev = f"af{qc}_{qs}_done"
            r0 = 128 * qrt
            ob = qrt % 2
            P.emit("sync", lambda s, r0=r0, ob=ob: s.dma_start(
                out=xres[ob][:], in_=X1[r0:r0 + 128, :]),
                wait=(ev.get(f"xresC_user{ob}"),), inc=f"xr{ob}", inc_by=16,
                event=f"rl_{qc}_{qs}")
            for half in range(2):
                P.emit("tensor",
                       lambda t, half=half:
                       t.matmul(out=p_pr[half][:], lhsT=ones[:],
                                rhs=brow["o"][:, 512 * half:512 * half + 512],
                                start=True, stop=False, skip_group_check=True),
                       wait=(proj_o_ev, ev.get(f"ppr_user{half}"),))
                for j in range(8):
                    P.emit("tensor",
                           lambda t, j=j, half=half:
                           t.matmul(out=p_pr[half][:],
                                    lhsT=attfm[:, 128 * j:128 * j + 128],
                                    rhs=w16["o"][:, 1024 * j + 512 * half:
                                                 1024 * j + 512 * half + 512],
                                    start=False, stop=(j == 7), skip_group_check=True),
                           inc="pe" if j == 7 else None,
                           event=f"o{qc}_{qs}_mm{half}" if j == 7 else None)
                P.emit("vector", lambda v, ob=ob, half=half: v.scalar_tensor_tensor(
                    out=x2o[ob][:, 512 * half:512 * half + 512], in0=p_pr[half][:],
                    scalar=1.0, in1=xres[ob][:, 512 * half:512 * half + 512],
                    op0=ALU.mult, op1=ALU.add,
                    accum_out=sums[:] if half == 0 else sums2[:]),
                    wait=(f"o{qc}_{qs}_mm{half}", f"rl_{qc}_{qs}",
                          ev.get(f"x2o_user{ob}") if half == 0 else None,),
                    inc="ve", event=f"e{half}_{qc}_{qs}")
                ev[f"ppr_user{half}"] = f"e{half}_{qc}_{qs}"
            ev["attfmC_user"] = f"o{qc}_{qs}_mm1"
            ev[f"xresC_user{ob}"] = f"e1_{qc}_{qs}"
            P.emit("vector", lambda v: v.tensor_add(out=sums[:], in0=sums[:],
                                                    in1=sums2[:]),
                   wait=(f"e1_{qc}_{qs}",), inc="ve", event=f"es_{qc}_{qs}")
            P.emit("vector", lambda v: v.tensor_scalar_mul(out=mean[:], in0=sums[:],
                                                           scalar1=1.0 / D),
                   wait=(f"es_{qc}_{qs}",), inc="ve", event=f"em_{qc}_{qs}")
            P.emit("vector", lambda v: v.tensor_scalar_mul(out=mb[:], in0=mean[:],
                                                           scalar1=-1.0),
                   wait=(f"em_{qc}_{qs}", ev.get("mbC_user"),), inc="ve",
                   event=f"eb_{qc}_{qs}")
            P.emit("scalar", lambda s, ob=ob: s.activation(
                out=junk[:], in_=x2o[ob][:], func=AF.Square, bias=mb[:], scale=1.0,
                accum_out=sumsq[:]),
                wait=(f"eb_{qc}_{qs}", ev.get("sumsqC_user"),), inc="se",
                event=f"eq_{qc}_{qs}")
            P.emit("vector", lambda v: v.tensor_scalar(
                out=var[:], in0=sumsq[:], scalar1=1.0 / D, scalar2=1e-5,
                op0=ALU.mult, op1=ALU.add),
                wait=(f"eq_{qc}_{qs}",), inc="ve", event=f"ev_{qc}_{qs}")
            ev["sumsqC_user"] = f"ev_{qc}_{qs}"
            P.emit("scalar", lambda s: s.activation(out=var[:], in_=var[:],
                                                    func=AF.Sqrt),
                   wait=(f"ev_{qc}_{qs}",), inc="se", event=f"ed_{qc}_{qs}")
            P.emit("vector", lambda v: v.reciprocal(out=rstd[:], in_=var[:]),
                   wait=(f"ed_{qc}_{qs}", ev.get("rstdC_user"),), inc="ve",
                   event=f"er_{qc}_{qs}")
            P.emit("vector", lambda v: v.tensor_tensor(out=mb[:], in0=mb[:],
                                                       in1=rstd[:], op=ALU.mult),
                   wait=(f"er_{qc}_{qs}",), inc="ve", event=f"emb_{qc}_{qs}")
            ev["mbC_user"] = f"emb_{qc}_{qs}"
            P.emit("scalar", lambda s, ob=ob: s.activation(
                out=x2o[ob][:], in_=x2o[ob][:], func=AF.Identity, bias=mb[:],
                scale=rstd[:]),
                wait=(f"emb_{qc}_{qs}",), inc="se", event=f"eln_{qc}_{qs}")
            ev["mbC_user"] = f"eln_{qc}_{qs}"
            ev["rstdC_user"] = f"eln_{qc}_{qs}"
            P.emit("sync", lambda s, r0=r0, ob=ob: s.dma_start(
                out=X2[r0 - 256:r0 - 256 + 128, :], in_=x2o[ob][:]),
                wait=(f"eln_{qc}_{qs}",), inc=f"so{ob}", inc_by=16,
                event=f"sx_{qc}_{qs}")
            ev[f"x2o_user{ob}"] = f"sx_{qc}_{qs}"
        for ks in range(6):
            kslot = (2 * qc + ks) % 8
            ev[f"kfm_user{kslot}"] = f"av_{qc}_{1}_{NH-1}"
            ev[f"v65_user{kslot}"] = f"av_{qc}_{1}_{NH-1}"
        for qs in range(2):
            ev[f"qfm_user{(2 + 2*qc + qs) % 8}"] = f"av_{qc}_{qs}_{NH-1}"

    def _c_exp_ranges(sub):
        out = []
        ks_list = [ks for ks, _ in sub]
        lo0 = min(k for k in ks_list if k < 4) * 128
        hi0 = (max(k for k in ks_list if k < 4) + 1) * 128
        out.append((0, lo0, hi0))
        if any(k >= 4 for k in ks_list):
            lo1 = (min(k for k in ks_list if k >= 4) - 4) * 128
            hi1 = (max(k for k in ks_list if k >= 4) - 4 + 1) * 128
            out.append((1, lo1, hi1))
        return out

    fired = set()
    for rt in range(NKT):
        stage1(rt)
        for qc in range(NQC):
            if qc not in fired and 2 * qc + 5 <= rt:
                if 2 * qc + 3 <= rt:
                    fired.add(qc)
                    P.events[f"win_{qc}"] = P.events[f"s1_{2*qc+5}"]
                    chunk(qc)
    for qc in range(NQC):
        assert qc in fired

    ctx.close()


def build_C():
    nc = bass.Bass("TRN2", target_bir_lowering=False, debug=False)
    dram = _mk_dram(nc)
    ctx = ExitStack()
    sems = _mk_sems(nc, ctx)(ALL_SEMS)
    P = Plan()
    emit_C(nc, PhasePlan(P, "C."), dram, "ExternalInput", "ExternalOutput")
    P.finish_dma()
    with nc.Block() as block:
        P.emit_into(block, sems)
    ctx.close()
    return nc


def host_prep_C(inputs):
    wq = np.ascontiguousarray(np.asarray(inputs["L1_attn_wq"], np.float32))
    wk = np.ascontiguousarray(np.asarray(inputs["L1_attn_wk"], np.float32))
    wv = np.ascontiguousarray(np.asarray(inputs["L1_attn_wv"], np.float32))
    wo = np.ascontiguousarray(np.asarray(inputs["L1_attn_wo"], np.float32))
    b = {k: np.ascontiguousarray(np.asarray(inputs[f"L1_attn_b{k}"], np.float32)[None, :])
         for k in "qkvo"}
    ones = np.ones((1, 128), np.float32)
    eye = np.eye(128, dtype=np.float32)
    ii, jj = np.meshgrid(np.arange(128), np.arange(128), indexing="ij")
    tria = (ii >= jj).astype(np.float32)
    trib = (ii <= jj).astype(np.float32)
    full = np.ones((128, 128), np.float32)
    zero = np.zeros((128, 128), np.float32)
    in_maps = []
    for c in range(NCORES):
        half = c % 2
        if half == 0:
            dm = [zero, zero, zero, trib, full, trib]
        else:
            dm = [tria, full, tria, zero, zero, zero]
        dmask = np.ascontiguousarray(np.concatenate(dm, axis=0))
        m = {"x1": None,
             "l1wq": wq, "l1wk": wk, "l1wv": wv, "l1wo": wo,
             "l1bq": b["q"], "l1bk": b["k"], "l1bv": b["v"], "l1bo": b["o"],
             "ones_row": ones, "eye128": eye, "tria": tria, "trib": trib,
             "dmask": dmask}
        in_maps.append(m)
    return in_maps

NE = 18
IM = 512
NTD = 16



def precastD_jobs(nc, dram):
    MGW = dram("mgw", [NE * D, IM], "ExternalInput")
    MUW = dram("muw", [NE * D, IM], "ExternalInput")
    MDW = dram("mdw", [NE * IM, D], "ExternalInput")
    MGWQ = dram("mgwq_scratch", [NE * D, IM], "InternalB16")
    MUWQ = dram("muwq_scratch", [NE * D, IM], "InternalB16")
    MDWQ = dram("mdwq_scratch", [NE * IM, D], "InternalB16")
    jobs = []
    for e in range(NE):
        for srcT, dstT, rows in ((MGW, MGWQ, D), (MUW, MUWQ, D), (MDW, MDWQ, IM)):
            jobs.append(lambda g, srcT=srcT, dstT=dstT, e=e, rows=rows:
                        g.dma_start(out=dstT[rows * e:rows * (e + 1), :],
                                    in_=srcT[rows * e:rows * (e + 1), :]))
    return jobs


def emit_precastD(nc, P, dram):
    for jb in precastD_jobs(nc, dram):
        P.emit("gpsimd", jb, inc="mq", inc_by=16)
    P.events["mq_all"] = ("mq", P.count["mq"])


def emit_D(nc, P, dram, in_kind):
    X2 = dram("x2", [2048, D], in_kind)
    RT = dram("routerT", [D, 16], "ExternalInput")
    EB = dram("ebias", [1, 16], "ExternalInput")
    TB = dram("tbreak", [1, 16], "ExternalInput")
    MGW = dram("mgw", [NE * D, IM], "ExternalInput")
    MUW = dram("muw", [NE * D, IM], "ExternalInput")
    MDW = dram("mdw", [NE * IM, D], "ExternalInput")
    ONES = dram("ones_row", [1, 128], "ExternalInput")
    EYE = dram("eye128", [128, 128], "ExternalInput")
    OUT = dram("out", [2048, D], "ExternalOutput")
    MGWQ = dram("mgwq_scratch", [NE * D, IM], "InternalB16")
    MUWQ = dram("muwq_scratch", [NE * D, IM], "InternalB16")
    MDWQ = dram("mdwq_scratch", [NE * IM, D], "InternalB16")

    ctx = ExitStack()
    sb = lambda name, shape, dt: ctx.enter_context(nc.sbuf_tensor(name + '_pD', shape, dt))
    ps = lambda name, shape: ctx.enter_context(nc.psum_tensor(name + '_pD', shape, F32))

    x2fm = [sb(f"x2fmD{i}", [128, D], BF16) for i in range(NTD)]
    racc = [sb(f"raccD{i}", [128, D], F32) for i in range(8)]
    gw16 = [sb(f"gw16D{i}", [128, 8 * IM], BF16) for i in range(2)]
    uw16 = [sb(f"uw16D{i}", [128, 8 * IM], BF16) for i in range(2)]
    dw16 = [sb(f"dw16D{i}", [128, 4 * D], BF16) for i in range(2)]
    rT = sb("rTD", [128, 8 * 16], F32)
    comb = sb("combD", [128, NTD * NE], F32)
    ones = sb("onesD", [1, 128], F32)
    eye = sb("eyeD", [128, 128], BF16)
    eyef = sb("eyefD", [128, 128], F32)
    ebias_bc = sb("ebiasD", [128, 16], F32)
    tb_bc = sb("tbD", [128, 16], F32)
    x16 = sb("x16D", [128, D], BF16)
    xf32 = sb("xf32D", [128, D], F32)
    i8mark = sb("i8markD", [128, 64], I8)
    fmark = sb("fmarkD", [128, 16], F32)
    xffm = sb("xffmD", [128, D], F32)
    scr = sb("scrD", [128, 16], F32)
    sc = sb("scD", [128, 16], F32)
    msk = sb("mskD", [128, 16], F32)
    msk2 = sb("msk2D", [128, 16], F32)
    gsel = sb("gselD", [128, 16], F32)
    u1 = sb("u1D", [128, 4], F32)
    w1_ = sb("w1D", [128, 4], F32)
    u2 = sb("u2D", [128, 4], F32)
    w2_ = sb("w2D", [128, 4], F32)
    min1 = sb("min1D", [128, 1], F32)
    min2 = sb("min2D", [128, 1], F32)
    ta = sb("taD", [128, 4], F32)
    tbv = sb("tbvD", [128, 4], F32)
    gs = sb("gsD", [128, 4], F32)
    gp = sb("gpD", [128, 2], F32)
    gq = sb("gqD", [128, 2], F32)
    sa = sb("saD", [128, 1], F32)
    sbv = sb("sbD", [128, 1], F32)
    mx2 = sb("mx2D", [128, 1], F32)
    gmask = sb("gmaskD", [128, 4], F32)
    mr = sb("mrD", [128, 1], F32)
    gr = sb("grD", [128, 16], F32)
    tw = sb("twD", [128, 16], F32)
    dn = sb("dnD", [128, 1], F32)
    cs = sb("csD", [128, 1], F32)
    gsl = [sb(f"gslD{i}", [128, 512], F32) for i in range(2)]
    he = [sb(f"heD{i}", [128, 512], F32) for i in range(2)]
    hes = [sb(f"hesD{i}", [128, 512], BF16) for i in range(2)]
    hesf = [sb(f"hesfD{i}", [128, 512], BF16) for i in range(2)]
    xres = [sb(f"xresD{i}", [128, D], F32) for i in range(2)]
    oo = [sb(f"ooD{i}", [128, D], F32) for i in range(2)]
    junk = sb("junkD", [128, D], F32)
    sums = sb("sumsD", [128, 1], F32)
    sums2 = sb("sums2D", [128, 1], F32)
    mean = sb("meanD", [128, 1], F32)
    sumsq = sb("sumsqD", [128, 1], F32)
    var = sb("varD", [128, 1], F32)
    rstd = sb("rstdD", [128, 1], F32)
    mb = sb("mbD", [128, 1], F32)

    p_g = [ps(f"p_gD{i}", [128, 512]) for i in range(2)]
    p_u = [ps(f"p_uD{i}", [128, 512]) for i in range(2)]
    p_d = ps("p_dD", [128, D])
    p_tr = [ps(f"p_trD{i}", [128, 512]) for i in range(2)]
    p_r = p_tr[0]

    ev = {}

    P.emit("gpsimd", lambda g: g.dma_start(out=eye[:], in_=EYE[:, :]),
           inc="wl", inc_by=16, event="eye")
    P.emit("sync", lambda s: s.dma_start(out=eyef[:], in_=EYE[:, :]),
           inc="dx", inc_by=16, event="eyef")
    P.emit("sync", lambda s: s.dma_start(out=ones[:], in_=ONES[:, :]),
           inc="dx", inc_by=16, event="ones")
    P.emit("sync", lambda s: s.dma_start(
        out=rT[:].rearrange("p (j e) -> p j e", j=8),
        in_=RT.rearrange("(j p) e -> p j e", p=128)),
        inc="dx", inc_by=16, event="rT")
    P.emit("sync", lambda s: s.dma_start(out=scr[0:1, :], in_=EB[:, :]),
           inc="dx", inc_by=16, event="ebrow")
    P.emit("sync", lambda s: s.dma_start(out=sc[0:1, :], in_=TB[:, :]),
           inc="dx", inc_by=16, event="tbrow")
    P.events["dsetup"] = ("dx", P.count["dx"])
    P.emit("vector", lambda v: v.memset(i8mark[:], 1),
           inc="ve", event="i8m")
    P.emit("vector", lambda v: v.tensor_copy(out=fmark[:].bitcast(I8), in_=i8mark[:]),
           wait=("i8m",), inc="ve", event="fmark")
    P.emit("tensor", lambda t: t.matmul(out=p_r[:, 0:16], lhsT=ones[:],
                                        rhs=scr[0:1, :], start=True, stop=True),
           wait=("dsetup",), inc="pe", event="ebmm")
    P.emit("vector", lambda v: v.tensor_copy(out=ebias_bc[:], in_=p_r[:, 0:16]),
           wait=("ebmm",), inc="ve", event="ebbc")
    P.emit("tensor", lambda t: t.matmul(out=p_r[:, 0:16], lhsT=ones[:],
                                        rhs=sc[0:1, :], start=True, stop=True),
           wait=("ebbc",), inc="pe", event="tbmm")
    P.emit("vector", lambda v: v.tensor_copy(out=tb_bc[:], in_=p_r[:, 0:16]),
           wait=("tbmm",), inc="ve", event="tbbc")
    ev0_init = True

    ev["ptrD_user0"] = "tbbc"
    for t in range(NTD):
        P.emit("gpsimd", lambda g, t=t: g.dma_start(
            out=x16[:], in_=X2[128 * t:128 * t + 128, :]),
            wait=(ev.get("x16D_user"),), inc="xg", inc_by=16, event=f"x16_{t}")
        P.emit("sync", lambda s, t=t: s.dma_start(
            out=xf32[:], in_=X2[128 * t:128 * t + 128, :]),
            wait=(ev.get("xf32D_user"),), inc="xr0", inc_by=16, event=f"xf_{t}")
        for b in range(2):
            for jj in range(4):
                j = 4 * b + jj
                P.emit("tensor", lambda tt, j=j, jj=jj, b=b:
                       tt.matmul(out=p_tr[b][:, 128 * jj:128 * jj + 128],
                                 lhsT=x16[:, 128 * j:128 * j + 128],
                                 rhs=eye[:], start=True, stop=True),
                       wait=(f"x16_{t}", "eye", ev.get(f"ptrD_user{b}"),),
                       inc="pe" if jj == 3 else None,
                       event=f"tr_{t}_{b}" if jj == 3 else None)
            P.emit("vector", lambda v, b=b, t=t: v.tensor_copy(
                out=x2fm[t][:, 512 * b:512 * b + 512], in_=p_tr[b][:]),
                wait=(f"tr_{t}_{b}",), inc="ve", event=f"tre_{t}_{b}")
            ev[f"ptrD_user{b}"] = f"tre_{t}_{b}"
        ev["x16D_user"] = f"tr_{t}_1"
        P.events[f"x2fm_{t}"] = P.events[f"tre_{t}_1"]
        for j in range(8):
            P.emit("tensor", lambda tt, j=j:
                   tt.transpose(out=p_tr[0][:, 0:128], in_=xf32[:, 128 * j:128 * j + 128],
                                identity=eyef[:]),
                   wait=(f"xf_{t}", "eyef", ev.get("ptrD_user0"),),
                   inc="pe", event=f"ftr_{t}_{j}")
            P.emit("vector", lambda v, j=j: v.tensor_copy(
                out=xffm[:, 128 * j:128 * j + 128], in_=p_tr[0][:, 0:128]),
                wait=(f"ftr_{t}_{j}", ev.get("xffm_user"),),
                inc="ve", event=f"fte_{t}_{j}")
            ev["ptrD_user0"] = f"fte_{t}_{j}"
        ev["xf32D_user"] = f"ftr_{t}_7"
        for j in range(8):
            P.emit("tensor", lambda tt, j=j:
                   tt.matmul(out=p_r[:, 0:16], lhsT=xffm[:, 128 * j:128 * j + 128],
                             rhs=rT[:, 16 * j:16 * j + 16],
                             start=(j == 0), stop=(j == 7)),
                   wait=(f"fte_{t}_7", "rT",
                         ev.get("ptrD_user0") if j == 0 else None,),
                   inc="pe" if j == 7 else None,
                   event=f"rlg_{t}" if j == 7 else None)
        ev["xffm_user"] = f"rlg_{t}"
        def VE(fn, wait, evn):
            P.emit("vector", fn, wait=wait, inc="ve", event=evn)
        P.emit("scalar", lambda s: s.activation(out=scr[:], in_=p_r[:, 0:16],
                                                func=AF.Sigmoid),
               wait=(f"rlg_{t}", ev.get("scr_user"),), inc="se", event=f"sg_{t}")
        ev["ptrD_user0"] = f"sg_{t}"
        VE(lambda v: v.tensor_add(out=sc[:], in0=scr[:], in1=ebias_bc[:]),
           (f"sg_{t}", "ebbc", "tbbc", ev.get("sc_user"),), f"sc_{t}")
        VE(lambda v: v.tensor_tensor(out=u1[:], in0=sc[:, 0:16:4], in1=sc[:, 1:16:4], op=ALU.min),
           (f"sc_{t}",), f"u1_{t}")
        VE(lambda v: v.tensor_tensor(out=w1_[:], in0=sc[:, 0:16:4], in1=sc[:, 1:16:4], op=ALU.max),
           (f"u1_{t}",), f"w1_{t}")
        VE(lambda v: v.tensor_tensor(out=u2[:], in0=sc[:, 2:16:4], in1=sc[:, 3:16:4], op=ALU.min),
           (f"w1_{t}",), f"u2_{t}")
        VE(lambda v: v.tensor_tensor(out=w2_[:], in0=sc[:, 2:16:4], in1=sc[:, 3:16:4], op=ALU.max),
           (f"u2_{t}",), f"w2_{t}")
        VE(lambda v: v.tensor_tensor(out=ta[:], in0=u1[:], in1=u2[:], op=ALU.max),
           (f"w2_{t}",), f"ta_{t}")
        VE(lambda v: v.tensor_tensor(out=tbv[:], in0=w1_[:], in1=w2_[:], op=ALU.min),
           (f"ta_{t}",), f"tb_{t}")
        VE(lambda v: v.tensor_tensor(out=gs[:], in0=sc[:, 0:16:4], in1=sc[:, 1:16:4], op=ALU.add),
           (f"tb_{t}",), f"g1_{t}")
        VE(lambda v: v.tensor_tensor(out=gs[:], in0=gs[:], in1=sc[:, 2:16:4], op=ALU.add),
           (f"g1_{t}",), f"g2_{t}")
        VE(lambda v: v.tensor_tensor(out=gs[:], in0=gs[:], in1=sc[:, 3:16:4], op=ALU.add),
           (f"g2_{t}",), f"g3_{t}")
        VE(lambda v: v.tensor_tensor(out=u1[:], in0=u1[:], in1=u2[:], op=ALU.min),
           (f"g3_{t}",), f"mn1_{t}")
        VE(lambda v: v.tensor_tensor(out=ta[:], in0=ta[:], in1=tbv[:], op=ALU.min),
           (f"mn1_{t}",), f"mn2_{t}")
        VE(lambda v: v.tensor_sub(out=gs[:], in0=gs[:], in1=u1[:]),
           (f"mn2_{t}",), f"g4_{t}")
        VE(lambda v: v.tensor_sub(out=gs[:], in0=gs[:], in1=ta[:]),
           (f"g4_{t}",), f"g5_{t}")
        VE(lambda v: v.tensor_tensor(out=gp[:], in0=gs[:, 0:4:2], in1=gs[:, 1:4:2], op=ALU.max),
           (f"g5_{t}",), f"gp_{t}")
        VE(lambda v: v.tensor_tensor(out=gq[:], in0=gs[:, 0:4:2], in1=gs[:, 1:4:2], op=ALU.min),
           (f"gp_{t}",), f"gq_{t}")
        VE(lambda v: v.tensor_tensor(out=sa[:], in0=gp[:, 0:1], in1=gp[:, 1:2], op=ALU.min),
           (f"gq_{t}",), f"sa_{t}")
        VE(lambda v: v.tensor_tensor(out=sbv[:], in0=gq[:, 0:1], in1=gq[:, 1:2], op=ALU.max),
           (f"sa_{t}",), f"sb_{t}")
        VE(lambda v: v.tensor_tensor(out=mx2[:], in0=sa[:], in1=sbv[:], op=ALU.max),
           (f"sb_{t}",), f"mx2_{t}")
        VE(lambda v: v.tensor_single_scalar(out=gmask[:], in_=gs[:], scalar=mx2[:], op=ALU.is_ge),
           (f"mx2_{t}",), f"gm_{t}")
        for j4 in range(4):
            VE(lambda v, j4=j4: v.tensor_tensor(out=msk[:, j4:16:4], in0=sc[:, j4:16:4],
                                                in1=gmask[:], op=ALU.mult),
               (f"gm_{t}" if j4 == 0 else f"mk{j4-1}_{t}",), f"mk{j4}_{t}")
        VE(lambda v: v.tensor_add(out=msk[:], in0=msk[:], in1=tb_bc[:]),
           (f"mk3_{t}",), f"mtb_{t}")
        VE(lambda v: v.tensor_copy(out=msk2[:], in_=msk[:]),
           (f"mtb_{t}",), f"ms2_{t}")
        prev = f"ms2_{t}"
        for r_ in range(4):
            VE(lambda v: v.reduce_max(out=mr[:], in_=msk2[:], axis=AX.X),
               (prev,), f"xm{r_}_{t}")
            if r_ < 3:
                VE(lambda v: v.tensor_single_scalar(out=gr[:], in_=msk2[:], scalar=mr[:],
                                                    op=ALU.is_ge),
                   (f"xm{r_}_{t}",), f"xg{r_}_{t}")
                VE(lambda v: v.scalar_tensor_tensor(out=msk2[:], in0=gr[:], scalar=-1e9,
                                                    in1=msk2[:], op0=ALU.mult, op1=ALU.add),
                   (f"xg{r_}_{t}",), f"xs{r_}_{t}")
                prev = f"xs{r_}_{t}"
        VE(lambda v: v.tensor_single_scalar(out=gr[:], in_=msk[:], scalar=mr[:], op=ALU.is_ge),
           (f"xm3_{t}",), f"sel_{t}")
        VE(lambda v: v.tensor_tensor(out=tw[:], in0=gr[:], in1=scr[:], op=ALU.mult),
           (f"sel_{t}",), f"tw_{t}")
        VE(lambda v: v.reduce_sum(out=dn[:], in_=tw[:], axis=AX.X),
           (f"tw_{t}",), f"dn_{t}")
        VE(lambda v: v.tensor_single_scalar(out=dn[:], in_=dn[:], scalar=1e-20, op=ALU.add),
           (f"dn_{t}",), f"dn2_{t}")
        VE(lambda v: v.reciprocal(out=cs[:], in_=dn[:]),
           (f"dn2_{t}",), f"rc_{t}")
        VE(lambda v: v.tensor_scalar_mul(out=cs[:], in0=cs[:], scalar1=2.5),
           (f"rc_{t}",), f"cs_{t}")
        VE(lambda v, t=t: v.tensor_scalar_mul(out=comb[:, NE * t:NE * t + 16],
                                              in0=tw[:], scalar1=cs[:]),
           (f"cs_{t}",), f"cb_{t}")
        VE(lambda v, t=t: v.memset(comb[:, NE * t + 16:NE * t + 18], 1.0),
           (f"cb_{t}",), f"cb2_{t}")
        ev["scr_user"] = f"tw_{t}"
        ev["sc_user"] = f"mk3_{t}"

    for hp in range(2):
        tiles = list(range(8 * hp, 8 * hp + 8))
        for i8_ in range(8):
            P.emit("vector", lambda v, i8_=i8_: v.memset(racc[i8_][:], 0.0),
                   wait=(ev.get(f"racc_user{i8_}"),), inc="ve", event=f"rz_{hp}_{i8_}")
        for e in range(NE):
            b = e % 2
            for nm, buf, src, ntile in (("g", gw16, MGWQ, 8), ("u", uw16, MUWQ, 8),
                                        ("d", dw16, MDWQ, 4)):
                P.emit("sync", lambda g, nm=nm, buf=buf, src=src, b=b, e=e, ntile=ntile:
                       g.dma_start(
                           out=buf[b][:].rearrange("p (j i) -> p j i", j=ntile),
                           in_=src[(D if ntile == 8 else IM) * e:
                                   (D if ntile == 8 else IM) * (e + 1), :]
                               .rearrange("(j p) i -> p j i", p=128)),
                       wait=("mq_all", ev.get(f"ew_user_{nm}{b}"),),
                       inc=f"ew{b}", inc_by=16, event=f"ld{nm}_{hp}_{e}")
            P.events[f"ld_{hp}_{e}"] = P.events[f"ldd_{hp}_{e}"]
            def emit_gu(t):
                pp_ = t % 2
                for nm, buf, pp in (("g", gw16, p_g[pp_]), ("u", uw16, p_u[pp_])):
                    for j in range(8):
                        P.emit("tensor", lambda tt, nm=nm, buf=buf, pp=pp, j=j, b=b, t=t:
                               tt.matmul(out=pp[:],
                                         lhsT=x2fm[t][:, 128 * j:128 * j + 128],
                                         rhs=buf[b][:, 512 * j:512 * j + 512],
                                         start=(j == 0), stop=(j == 7)),
                               wait=(f"ld_{hp}_{e}", f"x2fm_{t}",
                                     ev.get(f"p{nm}_user{pp_}"),),
                               inc="pe" if j == 7 else None,
                               event=f"{nm}_{hp}_{e}_{t}" if j == 7 else None)
                P.emit("scalar", lambda s, pp_=pp_: s.activation(
                    out=gsl[pp_][:], in_=p_g[pp_][:], func=AF.Silu),
                    wait=(f"g_{hp}_{e}_{t}", ev.get(f"gsl_user{pp_}"),),
                    inc="se", event=f"sl_{hp}_{e}_{t}")
                ev[f"pg_user{pp_}"] = f"sl_{hp}_{e}_{t}"
                P.emit("vector", lambda v, pp_=pp_: v.tensor_tensor(
                    out=he[pp_][:], in0=gsl[pp_][:], in1=p_u[pp_][:], op=ALU.mult),
                    wait=(f"sl_{hp}_{e}_{t}", f"u_{hp}_{e}_{t}",
                          ev.get(f"he_user{pp_}"),),
                    inc="ve", event=f"he_{hp}_{e}_{t}")
                ev[f"pu_user{pp_}"] = f"he_{hp}_{e}_{t}"
                ev[f"gsl_user{pp_}"] = f"he_{hp}_{e}_{t}"
                P.emit("vector", lambda v, e=e, t=t, pp_=pp_: v.tensor_scalar_mul(
                    out=hes[pp_][:], in0=he[pp_][:],
                    scalar1=comb[:, NE * t + e:NE * t + e + 1]),
                    wait=(f"he_{hp}_{e}_{t}", f"cb2_{t}", ev.get(f"hes_user{pp_}"),),
                    inc="ve", event=f"hs_{hp}_{e}_{t}")
                ev[f"he_user{pp_}"] = f"hs_{hp}_{e}_{t}"

            def emit_tail(t):
                pp_ = t % 2
                i8_ = t % 8
                for jj in range(4):
                    P.emit("tensor", lambda tt, jj=jj, pp_=pp_:
                           tt.matmul(out=p_tr[1][:, 128 * jj:128 * jj + 128],
                                     lhsT=hes[pp_][:, 128 * jj:128 * jj + 128],
                                     rhs=eye[:], start=True, stop=True),
                           wait=(f"hs_{hp}_{e}_{t}", ev.get("ptrD_user1"),),
                           inc="pe" if jj == 3 else None,
                           event=f"ht_{hp}_{e}_{t}" if jj == 3 else None)
                P.emit("vector", lambda v, pp_=pp_: v.tensor_copy(
                    out=hesf[pp_][:], in_=p_tr[1][:]),
                    wait=(f"ht_{hp}_{e}_{t}", ev.get(f"hesf_user{pp_}"),),
                    inc="ve", event=f"hf_{hp}_{e}_{t}")
                ev["ptrD_user1"] = f"hf_{hp}_{e}_{t}"
                ev[f"hes_user{pp_}"] = f"ht_{hp}_{e}_{t}"
                for cc in range(4):
                    for half in range(2):
                        last = (cc == 3 and half == 1)
                        P.emit("tensor", lambda tt, cc=cc, half=half, b=b, pp_=pp_:
                               tt.matmul(out=p_d[:, 512 * half:512 * half + 512],
                                         lhsT=hesf[pp_][:, 128 * cc:128 * cc + 128],
                                         rhs=dw16[b][:, 1024 * cc + 512 * half:
                                                     1024 * cc + 512 * half + 512],
                                         start=(cc == 0), stop=(cc == 3)),
                               wait=(f"hf_{hp}_{e}_{t}",
                                     ev.get("pd_user") if cc == 0 else None,),
                               inc="pe" if last else None,
                               event=f"dn_{hp}_{e}_{t}" if last else None)
                P.emit("vector", lambda v, i8_=i8_: v.tensor_add(
                    out=racc[i8_][:], in0=racc[i8_][:], in1=p_d[:]),
                    wait=(f"dn_{hp}_{e}_{t}", f"rz_{hp}_{i8_}",
                          ev.get("racc_add"),),
                    inc="ve", event=f"ra_{hp}_{e}_{t}")
                ev["pd_user"] = f"ra_{hp}_{e}_{t}"
                ev["racc_add"] = f"ra_{hp}_{e}_{t}"
                ev[f"hesf_user{pp_}"] = f"dn_{hp}_{e}_{t}"

            for idx, t in enumerate(tiles):
                emit_gu(t)
                if idx > 0:
                    emit_tail(tiles[idx - 1])
            emit_tail(tiles[-1])
            for nm in "gud":
                ev[f"ew_user_{nm}{b}"] = f"dn_{hp}_{e}_{NE}" if False else f"dn_{hp}_{e}_{tiles[-1]}"
        for t in tiles:
            i8_ = t % 8
            ob = t % 2
            P.emit("sync", lambda s, t=t, ob=ob: s.dma_start(
                out=xres[ob][:], in_=X2[128 * t:128 * t + 128, :]),
                wait=(ev.get(f"xresD_user{ob}"),), inc="xr1", inc_by=16,
                event=f"rl_{t}")
            P.emit("vector", lambda v, i8_=i8_, ob=ob: v.scalar_tensor_tensor(
                out=oo[ob][:, 0:512], in0=racc[i8_][:, 0:512], scalar=1.0,
                in1=xres[ob][:, 0:512], op0=ALU.mult, op1=ALU.add,
                accum_out=sums[:]),
                wait=(f"ra_{hp}_{NE-1}_{t}", f"rl_{t}", ev.get(f"oo_user{ob}"),),
                inc="ve", event=f"e0_{t}")
            P.emit("vector", lambda v, i8_=i8_, ob=ob: v.scalar_tensor_tensor(
                out=oo[ob][:, 512:1024], in0=racc[i8_][:, 512:1024], scalar=1.0,
                in1=xres[ob][:, 512:1024], op0=ALU.mult, op1=ALU.add,
                accum_out=sums2[:]),
                wait=(f"e0_{t}",), inc="ve", event=f"e1_{t}")
            ev[f"xresD_user{ob}"] = f"e1_{t}"
            ev[f"racc_user{i8_}"] = f"e1_{t}"
            P.emit("vector", lambda v: v.tensor_add(out=sums[:], in0=sums[:], in1=sums2[:]),
                   wait=(f"e1_{t}",), inc="ve", event=f"es_{t}")
            P.emit("vector", lambda v: v.tensor_scalar_mul(out=mean[:], in0=sums[:],
                                                           scalar1=1.0 / D),
                   wait=(f"es_{t}",), inc="ve", event=f"em_{t}")
            P.emit("vector", lambda v: v.tensor_scalar_mul(out=mb[:], in0=mean[:],
                                                           scalar1=-1.0),
                   wait=(f"em_{t}", ev.get("mbD_user"),), inc="ve", event=f"eb_{t}")
            P.emit("scalar", lambda s, ob=ob: s.activation(
                out=junk[:], in_=oo[ob][:], func=AF.Square, bias=mb[:], scale=1.0,
                accum_out=sumsq[:]),
                wait=(f"eb_{t}", ev.get("sumsqD_user"),), inc="se", event=f"eq_{t}")
            P.emit("vector", lambda v: v.tensor_scalar(
                out=var[:], in0=sumsq[:], scalar1=1.0 / D, scalar2=1e-5,
                op0=ALU.mult, op1=ALU.add),
                wait=(f"eq_{t}",), inc="ve", event=f"evr_{t}")
            ev["sumsqD_user"] = f"evr_{t}"
            P.emit("scalar", lambda s: s.activation(out=var[:], in_=var[:], func=AF.Sqrt),
                   wait=(f"evr_{t}",), inc="se", event=f"ed_{t}")
            P.emit("vector", lambda v: v.reciprocal(out=rstd[:], in_=var[:]),
                   wait=(f"ed_{t}", ev.get("rstdD_user"),), inc="ve", event=f"er_{t}")
            P.emit("vector", lambda v: v.tensor_tensor(out=mb[:], in0=mb[:],
                                                       in1=rstd[:], op=ALU.mult),
                   wait=(f"er_{t}",), inc="ve", event=f"embD_{t}")
            ev["mbD_user"] = f"embD_{t}"
            P.emit("scalar", lambda s, ob=ob: s.activation(
                out=oo[ob][:], in_=oo[ob][:], func=AF.Identity, bias=mb[:],
                scale=rstd[:]),
                wait=(f"embD_{t}",), inc="se", event=f"elnD_{t}")
            ev["mbD_user"] = f"elnD_{t}"
            ev["rstdD_user"] = f"elnD_{t}"
            P.emit("sync", lambda s, t=t, ob=ob: s.dma_start(
                out=OUT[128 * t:128 * t + 128, :], in_=oo[ob][:]),
                wait=(f"elnD_{t}",), inc=f"so{ob}", inc_by=16, event=f"sx_{t}")
            ev[f"oo_user{ob}"] = f"sx_{t}"

    ctx.close()


def build_D():
    nc = bass.Bass("TRN2", target_bir_lowering=False, debug=False)
    dram = _mk_dram(nc)
    ctx = ExitStack()
    sems = _mk_sems(nc, ctx)(ALL_SEMS)
    P = Plan()
    emit_precastD(nc, PhasePlan(P, "D."), dram)
    emit_D(nc, PhasePlan(P, "D."), dram, "ExternalInput")
    P.finish_dma()
    with nc.Block() as block:
        P.emit_into(block, sems)
    ctx.close()
    return nc


def build_fused():
    nc = bass.Bass("TRN2", target_bir_lowering=False, debug=False)
    dram = _mk_dram(nc)
    ctx = ExitStack()
    sems = _mk_sems(nc, ctx)(ALL_SEMS)
    P = Plan()
    bg = precastD_jobs(nc, dram)
    emit_A(nc, PhasePlan(P, "A."), dram, "Internal", bg=bg)
    for jb in bg:
        P.emit("gpsimd", jb, inc="mq", inc_by=16)
    P.events["D.mq_all"] = ("mq", P.count["mq"])
    P.barrier("AB")
    emit_B(nc, PhasePlan(P, "B."), dram, "Internal", "Internal")
    P.barrier("BC")
    emit_C(nc, PhasePlan(P, "C."), dram, "Internal", "Internal")
    P.barrier("CD")
    emit_D(nc, PhasePlan(P, "D."), dram, "Internal")
    P.finish_dma()
    with nc.Block() as block:
        P.emit_into(block, sems)
    ctx.close()
    return nc


def host_prep_D(inputs):
    gw = np.asarray(inputs["L1_ffn_gate_w"], np.float32)
    uw = np.asarray(inputs["L1_ffn_up_w"], np.float32)
    dw = np.asarray(inputs["L1_ffn_down_w"], np.float32)
    shg = np.asarray(inputs["L1_ffn_sh_gate_w"], np.float32)
    shu = np.asarray(inputs["L1_ffn_sh_up_w"], np.float32)
    shd = np.asarray(inputs["L1_ffn_sh_down_w"], np.float32)
    mgw = np.concatenate([gw[e].T for e in range(16)]
                         + [shg[0:512].T, shg[512:1024].T], axis=0)
    muw = np.concatenate([uw[e].T for e in range(16)]
                         + [shu[0:512].T, shu[512:1024].T], axis=0)
    mdw = np.concatenate([dw[e].T for e in range(16)]
                         + [shd[:, 0:512].T, shd[:, 512:1024].T], axis=0)
    rT = np.ascontiguousarray(np.asarray(inputs["L1_ffn_router_weight"], np.float32).T)
    eb = np.ascontiguousarray(np.asarray(inputs["L1_ffn_e_bias"], np.float32)[None, :])
    tb = ((15 - np.arange(16)) * 1e-9).astype(np.float32)[None, :]
    ones = np.ones((1, 128), np.float32)
    eye = np.eye(128, dtype=np.float32)
    m = {"x2": None, "routerT": rT, "ebias": eb, "tbreak": np.ascontiguousarray(tb),
         "mgw": np.ascontiguousarray(mgw), "muw": np.ascontiguousarray(muw),
         "mdw": np.ascontiguousarray(mdw), "ones_row": ones, "eye128": eye}
    return [dict(m) for _ in range(NCORES)]

_NC_CACHE = {}


def _built(name, fn):
    if name not in _NC_CACHE:
        _NC_CACHE[name] = fn()
    return _NC_CACHE[name]


def kernel(**inputs):
    nc = _built("F", build_fused)
    core_ids = list(range(NCORES))
    maps = []
    for c in range(NCORES):
        m = {}
        for prep in (host_prep_A, host_prep_B, host_prep_C, host_prep_D):
            mm = prep(inputs)[c]
            m.update({k: v for k, v in mm.items() if v is not None})
        maps.append(m)
    res = run_bass_kernel_spmd(nc, maps, core_ids=core_ids)
    out = np.empty((4, 4096, D), np.float32)
    for c in range(NCORES):
        seq, half = c // 2, c % 2
        out[seq, 2048 * half:2048 * (half + 1)] = res.results[c]["out"]
    return out
